# version 05
# baseline (speedup 1.0000x reference)
import math
from contextlib import ExitStack

import numpy as np

import concourse.bass as bass
import concourse.mybir as mybir
from concourse.bass_utils import run_bass_kernel_spmd

F32 = mybir.dt.float32
BF16 = mybir.dt.bfloat16
AF = mybir.ActivationFunctionType
ALU = mybir.AluOpType
AX = mybir.AxisListType

NCORES = 8
DBGREG = False
DBG_OUTS = ()
ENGINES = ("sync", "gpsimd", "tensor", "vector", "scalar")
ENG_SEM = {"sync": "ld", "gpsimd": "st", "tensor": "pe",
           "vector": "ve", "scalar": "ac"}


class Cfg:
    def __init__(self, d=1024, nh=16, S=2048, nseq=2, F=4096, E=8,
                 cap=5120, Ttot=32768):
        self.d, self.nh, self.S, self.nseq = d, nh, S, nseq
        self.dh = d // nh
        self.T = S * nseq
        self.F, self.E, self.cap, self.Ttot = F, E, cap, Ttot
        self.nc_d = d // 128
        self.nt = self.T // 128
        self.nt_s = S // 128
        self.nF = F // 128


class Step:
    __slots__ = ("eng", "sem", "val", "idx", "fn", "deps", "n_dma", "note")


DMA_POOL = 8


class Sched:
    def __init__(self, nc):
        self.nc = nc
        self.es = ExitStack()
        self.sems = {}
        for n in ("pe", "ve", "ac"):
            self.sems[n] = [self.es.enter_context(nc.semaphore(n)), 0]
        for q in ("ld", "st"):
            for i in range(DMA_POOL):
                n = f"{q}{i}"
                self.sems[n] = [self.es.enter_context(nc.semaphore(n)), 0]
        self.steps = []
        self.pending = {e: [] for e in ENGINES}
        self.dma_rr = {"sync": 0, "gpsimd": 0}
        self.last_by_sem = {}

    def step(self, eng, fn, deps=(), note=""):
        s = Step()
        s.eng, s.sem, s.fn, s.n_dma, s.note = eng, ENG_SEM[eng], fn, None, note
        s.idx = len(self.steps)
        s.deps = self._mkdeps(s, deps)
        self.sems[s.sem][1] += 1
        s.val = self.sems[s.sem][1]
        self.steps.append(s)
        self.pending[eng].append(s)
        self.last_by_sem[s.sem] = s
        return s

    def dma(self, eng, fn, n_dma, deps=(), note=""):
        assert eng in ("sync", "gpsimd")
        q = "ld" if eng == "sync" else "st"
        i = self.dma_rr[eng]
        self.dma_rr[eng] = (i + 1) % DMA_POOL
        s = Step()
        s.eng, s.sem, s.fn, s.n_dma, s.note = eng, f"{q}{i}", fn, n_dma, note
        s.idx = len(self.steps)
        s.deps = self._mkdeps(s, deps)
        prev = self.sems[s.sem][1]
        if prev > 0 and (s.sem not in s.deps or s.deps[s.sem] < prev):
            s.deps[s.sem] = prev
        self.sems[s.sem][1] += 16 * n_dma
        s.val = self.sems[s.sem][1]
        self.steps.append(s)
        self.pending[eng].append(s)
        self.last_by_sem[s.sem] = s
        return s

    def _mkdeps(self, s, deps):
        out = {}
        for dp in deps:
            if dp is None:
                continue
            assert dp.idx < s.idx, f"forward dep {dp.note} -> {s.note}"
            if dp.sem not in out or dp.val > out[dp.sem]:
                out[dp.sem] = dp.val
        return out

    def lasts(self):
        return list(self.last_by_sem.values())

    def emit_final(self, block):
        def body(e):
            for n in sorted(self.sems):
                h, v = self.sems[n]
                if v > 0:
                    e.wait_ge(h, v)
        block.gpsimd(body)

    def emit_pending(self, block):
        for eng in ENGINES:
            steps = self.pending[eng]
            if not steps:
                continue
            self.pending[eng] = []

            def body(e, steps=steps, eng=eng):
                for s in steps:
                    for semname, val in sorted(s.deps.items()):
                        e.wait_ge(self.sems[semname][0], val)
                    if s.n_dma is None:
                        last = s.fn(e)
                        assert last is not None, f"step {s.note} returned None"
                        last.then_inc(self.sems[s.sem][0], 1)
                    else:
                        cnt = [0]

                        def dma(out, in_, **kw):
                            ins = e.dma_start(out=out, in_=in_, **kw)
                            ins.then_inc(self.sems[s.sem][0], 16)
                            cnt[0] += 1
                            return ins
                        s.fn(e, dma)
                        assert cnt[0] == s.n_dma, \
                            f"step {s.note}: {cnt[0]} != {s.n_dma} DMAs"
            getattr(block, eng)(body)


def bcast_row(dram_vec, n, lo=None, hi=None):
    ap = dram_vec if isinstance(dram_vec, bass.AP) else dram_vec.ap()
    if ap.ndim == 2:
        ap = ap.squeeze(0) if ap.shape[0] == 1 else ap
    if lo is not None:
        ap = ap[lo:hi]
    return ap.unsqueeze(0).broadcast_to([n, ap.shape[-1]])



class PB:
    def __init__(self, cfg, race=True):
        self.cfg = cfg
        self.nc = bass.Bass("TRN2", target_bir_lowering=False,
                            detect_race_conditions=race)
        self.es = ExitStack()
        self.sched = None
        self.extra_inputs = {}

    def sbuf(self, name, shape, dtype):
        return self.es.enter_context(self.nc.sbuf_tensor(name, shape, dtype))

    def psum(self, name, shape, dtype=F32):
        return self.es.enter_context(self.nc.psum_tensor(name, shape, dtype))

    def dram_const(self, name, shape, dtype):
        if not hasattr(self, "_dconsts"):
            self._dconsts = {}
        if name not in self._dconsts:
            self._dconsts[name] = self.nc.dram_tensor(name, shape, dtype,
                                                      kind="ExternalInput")
        return self._dconsts[name]

    def start(self):
        self.sched = Sched(self.nc)
        self.extra_inputs["c_identb"] = np.eye(128, dtype=np.float32)
        identb_d = self.nc.dram_tensor("c_identb", [128, 128], F32,
                                       kind="ExternalInput")
        self.identb = self.sbuf("identb", [128, 128], BF16)
        self.identf = self.sbuf("identf", [128, 128], F32)

        def f(e, dma):
            dma(out=self.identf[:, :], in_=identb_d[:, :])
        s1 = self.sched.dma("sync", f, 1, note="ident_ld")
        self.sched.step("vector",
                        lambda e: e.tensor_copy(self.identb[:, :],
                                                self.identf[:, :]),
                        deps=[s1], note="ident_cast")



def ph_ln(p, x_d, fm_d=None, tm_d=None, eps=1e-5, g_d=None, b_d=None,
          tag="ln", tiles=None, sb=None):
    cfg = p.cfg
    d, ncd = cfg.d, cfg.nc_d
    sc = p.sched
    if tiles is None:
        tiles = list(range(cfg.nt))
    grp = 4
    groups = [tiles[i:i + grp] for i in range(0, len(tiles), grp)]
    xg, yg, yb, stt = sb["xg"], sb["yg"], sb["yb"], sb["stt"]
    ptp = sb["ptp"]
    affine = g_d is not None
    fence = sc.lasts()

    if affine:
        gb = sb["gb"]

        def f(e, dma):
            dma(out=gb[:, 0:d], in_=bcast_row(g_d, 128))
            dma(out=gb[:, d:2 * d], in_=bcast_row(b_d, 128))
        s_gb = sc.dma("sync", f, 2, deps=fence, note=tag + "_gb")
    else:
        s_gb = None

    ld_steps = [None] * len(groups)
    y_steps = [None] * len(groups)
    tp_steps = {}
    cp_steps = {}
    st_steps = {}
    tmst_steps = [None] * len(groups)
    out_steps = []

    for gi, gts in enumerate(groups):
        sl = gi % 2
        o, so = sl * grp, sl * 8 * grp
        n = len(gts)

        def f_ld(e, dma, gts=gts, o=o):
            for j, t in enumerate(gts):
                dma(out=xg[:, (o + j) * d:(o + j + 1) * d],
                    in_=x_d[t * 128:(t + 1) * 128, 0:d])
        deps = list(fence)
        if gi >= 2:
            deps.append(y_steps[gi - 2])
            if tm_d is not None:
                deps.append(tmst_steps[gi - 2])
        ld_steps[gi] = sc.dma("sync", f_ld, n, deps=deps, note=f"{tag}_ld{gi}")

        deps = [ld_steps[gi]]
        if gi >= 2:
            deps.append(y_steps[gi - 2])
            if tm_d is not None:
                deps.append(tmst_steps[gi - 2])
            lastt = (gi - 2) * grp + (len(groups[gi - 2]) - 1)
            deps.append(tp_steps[lastt])

        def f_sq(e, gts=gts, o=o, so=so, n=n):
            for j in range(n):
                ins = e.activation(yg[:, (o + j) * d:(o + j + 1) * d],
                                   xg[:, (o + j) * d:(o + j + 1) * d],
                                   AF.Square,
                                   accum_out=stt[:, so + grp + j:so + grp + j + 1])
            return ins
        s_sq = sc.step("scalar", f_sq, deps=deps, note=f"{tag}_sq{gi}")

        def f_sum(e, gts=gts, o=o, so=so, n=n):
            for j in range(n):
                ins = e.tensor_reduce(stt[:, so + j:so + j + 1],
                                      xg[:, (o + j) * d:(o + j + 1) * d],
                                      axis=AX.X, op=ALU.add)
            return ins
        s_sum = sc.step("vector", f_sum, deps=[ld_steps[gi]],
                        note=f"{tag}_sum{gi}")

        def f_mean(e, so=so, n=n):
            return e.tensor_scalar(stt[:, so:so + n], stt[:, so:so + n],
                                   1.0 / d, None, ALU.mult)
        s_mean = sc.step("vector", f_mean, deps=[s_sum],
                         note=f"{tag}_mean{gi}")

        def f_var1(e, so=so, n=n):
            e.tensor_scalar(stt[:, so + 3 * grp:so + 3 * grp + n],
                            stt[:, so + grp:so + grp + n],
                            1.0 / d, eps, ALU.mult, ALU.add)
            return e.tensor_tensor(stt[:, so + 2 * grp:so + 2 * grp + n],
                                   stt[:, so:so + n], stt[:, so:so + n],
                                   ALU.mult)
        s_var1 = sc.step("vector", f_var1, deps=[s_sq, s_mean],
                         note=f"{tag}_var1_{gi}")

        def f_var2(e, so=so, n=n):
            return e.tensor_sub(stt[:, so + 3 * grp:so + 3 * grp + n],
                                stt[:, so + 3 * grp:so + 3 * grp + n],
                                stt[:, so + 2 * grp:so + 2 * grp + n])
        s_var2 = sc.step("vector", f_var2, deps=[s_var1],
                         note=f"{tag}_var2_{gi}")

        def f_sqrt(e, so=so, n=n):
            return e.sqrt(stt[:, so + 4 * grp:so + 4 * grp + n],
                          stt[:, so + 3 * grp:so + 3 * grp + n])
        s_rs1 = sc.step("scalar", f_sqrt, deps=[s_var2], note=f"{tag}_sqrt{gi}")

        def f_rec(e, so=so, n=n):
            return e.reciprocal(stt[:, so + 4 * grp:so + 4 * grp + n],
                                stt[:, so + 4 * grp:so + 4 * grp + n])
        s_rs2 = sc.step("vector", f_rec, deps=[s_rs1], note=f"{tag}_rec{gi}")

        def f_y(e, o=o, so=so, n=n):
            for j in range(n):
                ins = e.tensor_scalar(yg[:, (o + j) * d:(o + j + 1) * d],
                                      xg[:, (o + j) * d:(o + j + 1) * d],
                                      stt[:, so + j:so + j + 1],
                                      stt[:, so + 4 * grp + j:so + 4 * grp + j + 1],
                                      ALU.subtract, ALU.mult)
            return ins
        s_y = sc.step("vector", f_y, deps=[s_rs2, s_sq], note=f"{tag}_y{gi}")
        if affine:
            def f_g(e, o=o, n=n):
                for j in range(n):
                    ins = e.tensor_tensor(yg[:, (o + j) * d:(o + j + 1) * d],
                                          yg[:, (o + j) * d:(o + j + 1) * d],
                                          gb[:, 0:d], ALU.mult)
                return ins
            s_y = sc.step("vector", f_g, deps=[s_y, s_gb], note=f"{tag}_yg{gi}")

            def f_b(e, o=o, n=n):
                for j in range(n):
                    ins = e.tensor_tensor(yg[:, (o + j) * d:(o + j + 1) * d],
                                          yg[:, (o + j) * d:(o + j + 1) * d],
                                          gb[:, d:2 * d], ALU.add)
                return ins
            s_y = sc.step("vector", f_b, deps=[s_y], note=f"{tag}_yb{gi}")

        if fm_d is not None:
            def f_cast(e, o=o, n=n):
                for j in range(n):
                    ins = e.tensor_copy(yb[:, (o + j) * d:(o + j + 1) * d],
                                        yg[:, (o + j) * d:(o + j + 1) * d])
                return ins
            s_y = sc.step("vector", f_cast, deps=[s_y], note=f"{tag}_cast{gi}")
        y_steps[gi] = s_y

        if tm_d is not None:
            def f_st(e, dma, gts=gts, o=o, n=n):
                for j, t in enumerate(gts):
                    dma(out=tm_d[t * 128:(t + 1) * 128, 0:d],
                        in_=yg[:, (o + j) * d:(o + j + 1) * d])
            tmst_steps[gi] = sc.dma("gpsimd", f_st, n, deps=[s_y],
                                    note=f"{tag}_tmst{gi}")
            out_steps.append(tmst_steps[gi])

        if fm_d is not None:
            for j, t in enumerate(gts):
                ti = gi * grp + j

                def f_tp(e, o=o, j=j):
                    for c in range(ncd):
                        ins = e.transpose(
                            ptp[:, c * 128:(c + 1) * 128],
                            yb[:, (o + j) * d + c * 128:(o + j) * d + (c + 1) * 128],
                            p.identb[:, :])
                    return ins
                deps = [s_y]
                if ti >= 1:
                    deps.append(cp_steps.get(ti - 1))
                tp_steps[ti] = sc.step("tensor", f_tp, deps=deps,
                                       note=f"{tag}_tp{ti}")

                fms = sb["fms"]

                def f_cp(e, ti=ti):
                    fsl = ti % 2
                    return e.tensor_copy(fms[:, fsl * d:(fsl + 1) * d],
                                         ptp[:, 0:d])
                deps = [tp_steps[ti]]
                if ti >= 2:
                    deps.append(st_steps.get(ti - 2))
                cp_steps[ti] = sc.step("vector", f_cp, deps=deps,
                                       note=f"{tag}_cp{ti}")

                def f_fmst(e, dma, t=t, ti=ti):
                    fsl = ti % 2
                    dma(out=fm_d[0:ncd, 0:128, t * 128:(t + 1) * 128]
                        .rearrange("c p x -> p c x"),
                        in_=fms[:, fsl * d:(fsl + 1) * d]
                        .rearrange("p (c x) -> p c x", c=ncd))
                st_steps[ti] = sc.dma("gpsimd", f_fmst, 1, deps=[cp_steps[ti]],
                                      note=f"{tag}_fmst{ti}")
                out_steps.append(st_steps[ti])
    return out_steps


def np_dt(dtype):
    import ml_dtypes
    return {F32: np.float32, BF16: ml_dtypes.bfloat16}[dtype]



def ph_proj(p, xfm_d, w_d, din, dout, b_d=None, out_fm_d=None, out_tm_d=None,
            act=None, w_transposed=False, tag="pj", tokens=None):
    cfg = p.cfg
    sc = p.sched
    nci, nco = din // 128, dout // 128
    T = cfg.T if tokens is None else tokens
    ngr = T // 512
    fence = sc.lasts()

    wst_nb = 1 if max(din, dout) > 2048 else 2
    with ExitStack() as es:
        wbf = es.enter_context(p.nc.sbuf_tensor(tag + "_wbf", [128, nci * dout], BF16))
        wst = es.enter_context(p.nc.sbuf_tensor(tag + "_wst", [128, wst_nb * max(din, dout)], F32))
        bcol = es.enter_context(p.nc.sbuf_tensor(tag + "_bc", [128, max(nco, 8)], F32))
        bro = es.enter_context(p.nc.sbuf_tensor(tag + "_br", [128, dout], F32))
        xg = es.enter_context(p.nc.sbuf_tensor(tag + "_xg", [128, 2 * nci * 512], BF16))
        og = es.enter_context(p.nc.sbuf_tensor(tag + "_og", [128, 4 * 512], BF16))
        s0 = es.enter_context(p.nc.sbuf_tensor(tag + "_s0", [128, 8 * 512], F32)) \
            if act == "elu1" else None
        pj = es.enter_context(p.nc.psum_tensor(tag + "_pj", [128, 4 * 512], F32))
        ptw = es.enter_context(p.nc.psum_tensor(tag + "_ptw", [128, 1024], BF16))

        wld = []
        wcast = []
        wtp = []
        if not w_transposed:
            wsl = max(din, dout)
            for k in range(nci):
                def f(e, dma, k=k):
                    dma(out=wst[:, (k % wst_nb) * wsl:(k % wst_nb) * wsl + dout],
                        in_=w_d[k * 128:(k + 1) * 128, 0:dout])
                deps = list(fence)
                if k >= wst_nb:
                    deps.append(wcast[k - wst_nb])
                wld.append(sc.dma("sync", f, 1, deps=deps, note=f"{tag}_wld{k}"))

                def fc(e, k=k):
                    return e.tensor_copy(wbf[:, k * dout:(k + 1) * dout],
                                         wst[:, (k % wst_nb) * wsl:(k % wst_nb) * wsl + dout])
                wcast.append(sc.step("vector", fc, deps=[wld[k]],
                                     note=f"{tag}_wc{k}"))
            w_ready = wcast[-1:]
        else:
            cast_tmp = es.enter_context(
                p.nc.sbuf_tensor(tag + "_wtt", [128, 2 * 1024], BF16))
            wsl = max(din, dout)
            for r in range(nco):
                def f(e, dma, r=r):
                    dma(out=wst[:, (r % 2) * wsl:(r % 2) * wsl + din],
                        in_=w_d[r * 128:(r + 1) * 128, 0:din])
                deps = list(fence)
                if r >= 2:
                    deps.append(wtp[(r - 2) * nci + nci - 1])
                wld.append(sc.dma("sync", f, 1, deps=deps, note=f"{tag}_wld{r}"))

                def fc(e, r=r):
                    return e.tensor_copy(
                        cast_tmp[:, (r % 2) * 1024:(r % 2) * 1024 + din],
                        wst[:, (r % 2) * wsl:(r % 2) * wsl + din])
                wcast.append(sc.step("vector", fc, deps=[wld[r]],
                                     note=f"{tag}_wc{r}"))
                for k in range(nci):
                    def ft(e, r=r, k=k):
                        return e.transpose(
                            ptw[:, (k % 4) * 128:(k % 4) * 128 + 128],
                            cast_tmp[:, (r % 2) * 1024 + k * 128:
                                     (r % 2) * 1024 + (k + 1) * 128],
                            p.identb[:, :])
                    deps = [wcast[r]]
                    idx = r * nci + k
                    if idx >= 1:
                        deps.append(wtp_cp[idx - 1])
                    wtp.append(sc.step("tensor", ft, deps=deps,
                                       note=f"{tag}_wtp{idx}"))

                    def fcp(e, r=r, k=k):
                        return e.tensor_copy(
                            wbf[:, k * dout + r * 128:k * dout + (r + 1) * 128],
                            ptw[:, (k % 4) * 128:(k % 4) * 128 + 128])
                    if idx == 0:
                        wtp_cp = []
                    wtp_cp.append(sc.step("vector", fcp, deps=[wtp[idx]],
                                          note=f"{tag}_wtpc{idx}"))
            w_ready = wtp_cp[-1:]

        b_ready = []
        if b_d is not None:
            if out_fm_d is not None:
                def f(e, dma):
                    for m in range(nco):
                        dma(out=bcol[:, m:m + 1],
                            in_=b_d[m * 128:(m + 1) * 128].unsqueeze(1))
                b_ready.append(sc.dma("sync", f, nco, deps=fence,
                                      note=tag + "_bcol"))
            if out_tm_d is not None:
                def f2(e, dma):
                    dma(out=bro[:, 0:dout], in_=bcast_row(b_d, 128))
                b_ready.append(sc.dma("sync", f2, 1, deps=fence,
                                      note=tag + "_brow"))

        xld = [None] * ngr
        xrel = {}
        if out_fm_d is not None:
            ost = {}
            for g in range(ngr):
                def f(e, dma, g=g):
                    sl = g % 2
                    for k in range(nci):
                        dma(out=xg[:, (sl * nci + k) * 512:(sl * nci + k + 1) * 512],
                            in_=xfm_d[k, 0:128, g * 512:(g + 1) * 512])
                deps = list(fence)
                if g >= 2:
                    deps.append(xrel[g - 2])
                xld[g] = sc.dma("sync", f, nci, deps=deps, note=f"{tag}_xld{g}")
                for m in range(nco):
                    i = g * nco + m
                    pb = i % 4

                    def fmm(e, g=g, m=m, pb=pb):
                        sl = g % 2
                        for k in range(nci):
                            ins = e.matmul(
                                pj[:, pb * 512:(pb + 1) * 512],
                                wbf[:, k * dout + m * 128:k * dout + (m + 1) * 128],
                                xg[:, (sl * nci + k) * 512:(sl * nci + k + 1) * 512],
                                start=(k == 0), stop=(k == nci - 1))
                        return ins
                    deps = [xld[g]] + w_ready
                    if i >= 4:
                        deps.append(ost[i - 4][1])
                    s_mm = sc.step("tensor", fmm, deps=deps, note=f"{tag}_mm{i}")
                    ost[i] = emit_evac_fm(p, sc, tag, i, pb, m, g, pj, og, s0,
                                          bcol, act, b_d, out_fm_d, s_mm, ost,
                                          b_ready=b_ready)
                xrel[g] = ost[g * nco + nco - 1][0]

        if out_tm_d is not None:
            nt = T // 128
            tst = {}
            tld = [None] * nt
            for t in range(nt):
                def f(e, dma, t=t):
                    sl = t % 2
                    for k in range(nci):
                        dma(out=xg[:, (sl * nci + k) * 512 + 0:(sl * nci + k) * 512 + 128]
                            if False else
                            xg[:, (sl * nci + k) * 512:(sl * nci + k) * 512 + 128],
                            in_=xfm_d[k, 0:128, t * 128:(t + 1) * 128])
                deps = list(fence)
                if t >= 2:
                    deps.append(tst[t - 2][0])
                tld[t] = sc.dma("sync", f, nci, deps=deps, note=f"{tag}_tld{t}")
                ncol = dout // 512
                pb = t % 2

                def fmm(e, t=t, pb=pb, ncol=ncol):
                    sl = t % 2
                    for k in range(nci):
                        for cgi in range(ncol):
                            ins = e.matmul(
                                pj[:, (pb * ncol + cgi) * 512:(pb * ncol + cgi + 1) * 512],
                                xg[:, (sl * nci + k) * 512:(sl * nci + k) * 512 + 128],
                                wbf[:, k * dout + cgi * 512:k * dout + (cgi + 1) * 512],
                                start=(k == 0), stop=(k == nci - 1))
                    return ins
                deps = [tld[t]] + w_ready
                if t >= 2:
                    deps.append(tst[t - 2][0])
                s_mm = sc.step("tensor", fmm, deps=deps, note=f"{tag}_tmm{t}")
                osl = t % 2

                def fev(e, t=t, pb=pb, osl=osl, ncol=ncol):
                    if b_d is not None:
                        return e.tensor_tensor(
                            og[:, osl * 1024:osl * 1024 + dout],
                            pj[:, pb * ncol * 512:pb * ncol * 512 + dout],
                            bro[:, 0:dout], ALU.add)
                    return e.tensor_copy(og[:, osl * 1024:osl * 1024 + dout],
                                         pj[:, pb * ncol * 512:pb * ncol * 512 + dout])
                deps = [s_mm] + b_ready
                if t >= 2:
                    deps.append(tst[t - 2][1])
                s_ev = sc.step("vector", fev, deps=deps, note=f"{tag}_tev{t}")

                def fst(e, dma, t=t, osl=osl):
                    dma(out=out_tm_d[t * 128:(t + 1) * 128, 0:dout],
                        in_=og[:, osl * 1024:osl * 1024 + dout])
                s_st = sc.dma("gpsimd", fst, 1, deps=[s_ev], note=f"{tag}_tst{t}")
                tst[t] = (s_st, s_ev)
    return sc.lasts()


def emit_evac_fm(p, sc, tag, i, pb, m, g, pj, og, s0, bcol, act, b_d,
                 out_fm_d, s_mm, ost, b_ready=()):
    osl = i % 4
    deps = [s_mm] + list(b_ready)
    if i >= 4:
        deps.append(ost[i - 4][1])

    if act == "elu1":
        def fA(e):
            if b_d is not None:
                return e.tensor_scalar(s0[:, osl * 512:(osl + 1) * 512],
                                       pj[:, pb * 512:(pb + 1) * 512],
                                       bcol[:, m:m + 1], None, ALU.add)
            return e.tensor_copy(s0[:, osl * 512:(osl + 1) * 512],
                                 pj[:, pb * 512:(pb + 1) * 512])
        if i >= 4:
            deps.append(ost[i - 4][2])
        sA = sc.step("vector", fA, deps=deps, note=f"{tag}_eA{i}")

        def fB(e):
            return e.tensor_scalar(s0[:, (4 + osl) * 512:(5 + osl) * 512],
                                   s0[:, osl * 512:(osl + 1) * 512],
                                   0.0, None, ALU.min)
        sB = sc.step("vector", fB, deps=[sA], note=f"{tag}_eB{i}")

        def fC(e):
            return e.activation(s0[:, (4 + osl) * 512:(5 + osl) * 512],
                                s0[:, (4 + osl) * 512:(5 + osl) * 512], AF.Exp)
        sC = sc.step("scalar", fC, deps=[sB], note=f"{tag}_eC{i}")

        def fD(e):
            return e.scalar_tensor_tensor(og[:, osl * 512:(osl + 1) * 512],
                                          s0[:, osl * 512:(osl + 1) * 512],
                                          0.0,
                                          s0[:, (4 + osl) * 512:(5 + osl) * 512],
                                          ALU.max, ALU.add)
        dD = [sC] + ([ost[i - 4][0]] if i >= 4 else [])
        s_ev = sc.step("vector", fD, deps=dD, note=f"{tag}_eD{i}")
        last_reader = sA
    else:
        AFm = {None: AF.Identity, "gelu": AF.Gelu, "relu": AF.Relu}[act]

        def f1(e):
            kw = {}
            if b_d is not None:
                kw["bias"] = bcol[:, m:m + 1]
            return e.activation(og[:, osl * 512:(osl + 1) * 512],
                                pj[:, pb * 512:(pb + 1) * 512], AFm, **kw)
        if i >= 4:
            deps.append(ost[i - 4][0])
        s_ev = sc.step("scalar", f1, deps=deps, note=f"{tag}_ev{i}")
        last_reader = s_ev

    def f2(e, dma):
        dma(out=out_fm_d[m, 0:128, g * 512:(g + 1) * 512],
            in_=og[:, osl * 512:(osl + 1) * 512])
    s_st = sc.dma("gpsimd", f2, 1, deps=[s_ev], note=f"{tag}_st{i}")
    return (s_st, last_reader, s_ev)



def ph_attn(p, qfm_d, kfm_d, vtm_d, ctxfm_d, kind, scale, tag):
    cfg = p.cfg
    sc = p.sched
    S, d, nh, dh = cfg.S, cfg.d, cfg.nh, cfg.dh
    ncd = cfg.nc_d
    fence = sc.lasts()
    if kind == "std":
        KEYS = S
        insts = [(s, h) for s in range(cfg.nseq) for h in range(nh)]
    else:
        KEYS = (d // dh) * 128
        insts = [(s, w) for s in range(cfg.nseq) for w in range(S // 128)]
    RB = KEYS // 128
    NQG = KEYS // 512
    C = KEYS // 128

    with ExitStack() as es:
        ten = lambda n, sh, dt: es.enter_context(p.nc.sbuf_tensor(tag + n, sh, dt))
        qsb = ten("_q", [128, 2 * KEYS], BF16)
        ksb = ten("_k", [128, 2 * KEYS], BF16)
        vsb = ten("_v", [128, 2 * C * 64], BF16)
        atn = ten("_at", [128, KEYS], BF16)
        att = ten("_att", [128, C * 512], BF16)
        cst = ten("_cst", [128, 2 * 512], BF16)
        stt = ten("_stt", [128, 4 * RB], F32)
        psc = es.enter_context(p.nc.psum_tensor(tag + "_psc", [128, KEYS], F32))
        ptp = es.enter_context(p.nc.psum_tensor(tag + "_ptp", [128, 512], BF16))
        pcx = es.enter_context(p.nc.psum_tensor(tag + "_pcx", [128, 512], F32))

        prev_inst_rel = {}
        prev_ctx_mm = None
        prev_exp = None
        prev_norm = None
        prev_tp_last = None
        prev_cst = {}

        for ii, (s, hw) in enumerate(insts):
            sl = ii % 2
            qo = sl * KEYS
            vo = sl * C * 64
            if kind == "std":
                def f_q(e, dma, s=s, h=hw, qo=qo):
                    dma(out=qsb[0:64, qo:qo + S],
                        in_=qfm_d[h // 2, (h % 2) * 64:(h % 2) * 64 + 64,
                                  s * S:(s + 1) * S])
                    dma(out=ksb[0:64, qo:qo + S],
                        in_=kfm_d[h // 2, (h % 2) * 64:(h % 2) * 64 + 64,
                                  s * S:(s + 1) * S])

                def f_v(e, dma, s=s, h=hw, vo=vo):
                    q4 = max(C // 4, 1)
                    for pc in range(0, C, q4):
                        dma(out=vsb[:, vo + pc * 64:vo + (pc + q4) * 64],
                            in_=vtm_d[s * S + pc * 128:s * S + (pc + q4) * 128,
                                      h * 64:(h + 1) * 64]
                            .rearrange("(c p) x -> p c x", p=128))
            else:
                def f_q(e, dma, s=s, w=hw, qo=qo):
                    dma(out=qsb[0:64, qo:qo + KEYS]
                        .rearrange("p (c h x) -> p c h x", c=ncd, h=2),
                        in_=qfm_d[0:ncd, 0:128,
                                  s * S + w * 128:s * S + (w + 1) * 128]
                        .rearrange("c (h p) x -> p c h x", h=2))
                    dma(out=ksb[0:64, qo:qo + KEYS]
                        .rearrange("p (c h x) -> p c h x", c=ncd, h=2),
                        in_=kfm_d[0:ncd, 0:128,
                                  s * S + w * 128:s * S + (w + 1) * 128]
                        .rearrange("c (h p) x -> p c h x", h=2))

                def f_v(e, dma, s=s, w=hw, vo=vo):
                    dma(out=vsb[:, vo:vo + C * 64],
                        in_=vtm_d[s * S + w * 128:s * S + (w + 1) * 128, 0:d])
            deps = list(fence)
            if ii >= 2:
                deps.append(prev_inst_rel[ii - 2])
            s_ldq = sc.dma("sync", f_q, 2, deps=deps, note=f"{tag}_ldq{ii}")
            nv = (4 if C >= 4 else C) if kind == "std" else 1
            s_ldv = sc.dma("sync", f_v, nv, deps=deps, note=f"{tag}_ldv{ii}")

            def lhsT_ap(rb, qo=qo):
                return qsb[0:64, qo + rb * 128:qo + (rb + 1) * 128]

            last_ctx_mm_of_inst = None
            for qg in range(NQG):
                tp_steps = []
                for rbi in range(4):
                    rb = qg * 4 + rbi
                    def f_s(e, rb=rb, qo=qo, lhsT_ap=lhsT_ap):
                        for cg in range(KEYS // 512):
                            ins = e.matmul(psc[:, cg * 512:(cg + 1) * 512],
                                           lhsT_ap(rb),
                                           ksb[0:64, qo + cg * 512:qo + (cg + 1) * 512],
                                           start=True, stop=True)
                        return ins
                    deps = [s_ldq]
                    if prev_exp is not None:
                        deps.append(prev_exp)
                    s_s = sc.step("tensor", f_s, deps=deps, note=f"{tag}_s{ii}_{rb}")

                    def f_m(e, rb=rb):
                        return e.reduce_max(stt[:, rb:rb + 1], psc[:, 0:KEYS],
                                            axis=AX.X, negate=True)
                    s_m = sc.step("vector", f_m, deps=[s_s], note=f"{tag}_m{ii}_{rb}")

                    def f_ms(e, rb=rb):
                        return e.tensor_scalar(stt[:, RB + rb:RB + rb + 1],
                                               stt[:, rb:rb + 1], scale, None,
                                               ALU.mult)
                    s_ms = sc.step("vector", f_ms, deps=[s_m], note=f"{tag}_ms{ii}_{rb}")

                    def f_e(e, rb=rb):
                        return e.activation(atn[:, 0:KEYS], psc[:, 0:KEYS],
                                            AF.Exp, bias=stt[:, RB + rb:RB + rb + 1],
                                            scale=scale,
                                            accum_out=stt[:, 2 * RB + rb:2 * RB + rb + 1])
                    deps = [s_ms]
                    if prev_norm is not None:
                        deps.append(prev_norm)
                    if prev_tp_last is not None:
                        deps.append(prev_tp_last)
                    s_e = sc.step("scalar", f_e, deps=deps, note=f"{tag}_e{ii}_{rb}")
                    prev_exp = s_e

                    def f_r(e, rb=rb):
                        return e.reciprocal(stt[:, 3 * RB + rb:3 * RB + rb + 1],
                                            stt[:, 2 * RB + rb:2 * RB + rb + 1])
                    s_r = sc.step("vector", f_r, deps=[s_e], note=f"{tag}_r{ii}_{rb}")

                    def f_n(e, rb=rb):
                        return e.tensor_scalar(atn[:, 0:KEYS], atn[:, 0:KEYS],
                                               stt[:, 3 * RB + rb:3 * RB + rb + 1],
                                               None, ALU.mult)
                    s_n = sc.step("vector", f_n, deps=[s_r], note=f"{tag}_n{ii}_{rb}")
                    prev_norm = s_n

                    prev_cp = None
                    for tg in range(C // 4):
                        def f_t(e, tg=tg):
                            for j in range(4):
                                kc = tg * 4 + j
                                ins = e.transpose(ptp[:, j * 128:(j + 1) * 128],
                                                  atn[:, kc * 128:(kc + 1) * 128],
                                                  p.identb[:, :])
                            return ins
                        deps = [s_n]
                        if prev_cp is not None:
                            deps.append(prev_cp)
                        elif prev_ctx_mm is not None:
                            deps.append(prev_ctx_mm)
                        s_t = sc.step("tensor", f_t, deps=deps,
                                      note=f"{tag}_t{ii}_{rb}_{tg}")

                        def f_c(e, tg=tg, rbi=rbi):
                            return e.tensor_copy(
                                att[:, 0:512]
                                if False else
                                att.ap()[:, :].rearrange(
                                    "p (c x) -> p c x", c=C)[:, tg * 4:tg * 4 + 4,
                                                            rbi * 128:(rbi + 1) * 128],
                                ptp[:, 0:512].rearrange("p (c x) -> p c x", c=4))
                        deps2 = [s_t]
                        if prev_ctx_mm is not None:
                            deps2.append(prev_ctx_mm)
                        s_c = sc.step("vector", f_c, deps=deps2,
                                      note=f"{tag}_c{ii}_{rb}_{tg}")
                        prev_cp = s_c
                        prev_tp_last = s_t
                        tp_steps.append(s_c)

                def f_x(e, vo=vo):
                    for kc in range(C):
                        ins = e.matmul(pcx[0:64, 0:512],
                                       vsb[:, vo + kc * 64:vo + (kc + 1) * 64],
                                       att[:, kc * 512:(kc + 1) * 512],
                                       start=(kc == 0), stop=(kc == C - 1))
                    return ins
                deps = tp_steps + [s_ldv]
                if prev_cst.get(qg % 2) is not None:
                    deps.append(prev_cst[qg % 2])
                s_x = sc.step("tensor", f_x, deps=deps, note=f"{tag}_x{ii}_{qg}")
                prev_ctx_mm = s_x
                last_ctx_mm_of_inst = s_x

                csl = qg % 2

                def f_cc(e, csl=csl):
                    return e.tensor_copy(cst[0:64, csl * 512:(csl + 1) * 512],
                                         pcx[0:64, 0:512])
                s_cc = sc.step("vector", f_cc, deps=[s_x], note=f"{tag}_cc{ii}_{qg}")

                if kind == "std":
                    def f_st(e, dma, s=s, h=hw, qg=qg, csl=csl):
                        dma(out=ctxfm_d[h // 2, (h % 2) * 64:(h % 2) * 64 + 64,
                                        s * S + qg * 512:s * S + (qg + 1) * 512],
                            in_=cst[0:64, csl * 512:(csl + 1) * 512])
                    s_stc = sc.dma("gpsimd", f_st, 1, deps=[s_cc],
                                   note=f"{tag}_st{ii}_{qg}")
                else:
                    def f_st(e, dma, s=s, w=hw, qg=qg, csl=csl):
                        for gi in range(4):
                            g = qg * 4 + gi
                            dma(out=ctxfm_d[g // 2, (g % 2) * 64:(g % 2) * 64 + 64,
                                            s * S + w * 128:s * S + (w + 1) * 128],
                                in_=cst[0:64, csl * 512 + gi * 128:csl * 512 + (gi + 1) * 128])
                    s_stc = sc.dma("gpsimd", f_st, 4, deps=[s_cc],
                                   note=f"{tag}_st{ii}_{qg}")
                prev_cst[csl] = s_stc
            prev_inst_rel[ii] = last_ctx_mm_of_inst
    return sc.lasts()



def ph_fm2tm(p, src_fm_d, dst_tm_d, dt=F32, tag="f2t"):
    cfg = p.cfg
    sc = p.sched
    d, ncd, nt = cfg.d, cfg.nc_d, cfg.nt
    fence = sc.lasts()
    with ExitStack() as es:
        xin = es.enter_context(p.nc.sbuf_tensor(tag + "_x", [128, 2 * d], dt))
        out = es.enter_context(p.nc.sbuf_tensor(tag + "_o", [128, 2 * d], F32))
        ptt = es.enter_context(p.nc.psum_tensor(tag + "_pt", [128, 2 * 512], dt))
        ident = p.identb if dt == BF16 else p.identf
        prev_cp = {}
        prev_st = {}
        prev_tp = {}
        for t in range(nt):
            sl = t % 2

            def f_ld(e, dma, t=t, sl=sl):
                dma(out=xin[:, sl * d:(sl + 1) * d]
                    .rearrange("p (c x) -> p c x", c=ncd),
                    in_=src_fm_d[0:ncd, 0:128, t * 128:(t + 1) * 128]
                    .rearrange("c p x -> p c x"))
            deps = list(fence)
            if t >= 2:
                deps.append(prev_tp[t - 2])
            s_ld = sc.dma("sync", f_ld, 1, deps=deps, note=f"{tag}_ld{t}")

            half_steps = []
            last_cp = prev_cp.get((t - 1, 1)) if t >= 1 else None
            for hh in range(2):
                def f_tp(e, t=t, sl=sl, hh=hh):
                    for c in range(4):
                        ins = e.transpose(
                            ptt[:, hh * 512 + c * 128:hh * 512 + (c + 1) * 128],
                            xin[:, sl * d + (hh * 4 + c) * 128:sl * d + (hh * 4 + c + 1) * 128],
                            ident[:, :])
                    return ins
                deps = [s_ld]
                if hh == 1:
                    deps.append(prev_cp[(t, 0)])
                elif last_cp is not None:
                    deps.append(last_cp)
                s_tp = sc.step("tensor", f_tp, deps=deps, note=f"{tag}_tp{t}_{hh}")
                prev_tp[t] = s_tp

                def f_cp(e, t=t, sl=sl, hh=hh):
                    return e.tensor_copy(out[:, sl * d + hh * 512:sl * d + (hh + 1) * 512],
                                         ptt[:, hh * 512:(hh + 1) * 512])
                deps = [s_tp]
                if t >= 2:
                    deps.append(prev_st[t - 2])
                s_cp = sc.step("vector", f_cp, deps=deps, note=f"{tag}_cp{t}_{hh}")
                prev_cp[(t, hh)] = s_cp
                half_steps.append(s_cp)

            def f_st(e, dma, t=t, sl=sl):
                dma(out=dst_tm_d[t * 128:(t + 1) * 128, 0:d],
                    in_=out[:, sl * d:(sl + 1) * d])
            prev_st[t] = sc.dma("gpsimd", f_st, 1, deps=half_steps,
                                note=f"{tag}_st{t}")
    return sc.lasts()



def ph_resid(p, mode, x_d, a_d, y_d=None, tm_d=None, fm_d=None, coef=1.0,
             eps=1e-5, eps2=1e-5, g_d=None, b_d=None, adt=BF16, tag="rs"):
    cfg = p.cfg
    sc = p.sched
    d, ncd, nt = cfg.d, cfg.nc_d, cfg.nt
    fence = sc.lasts()
    grp = 4
    groups = [list(range(i, min(i + grp, nt))) for i in range(0, nt, grp)]
    with ExitStack() as es:
        ten = lambda n, sh, dt: es.enter_context(p.nc.sbuf_tensor(tag + n, sh, dt))
        xg = ten("_x", [128, 2 * grp * d], F32)
        ag = ten("_a", [128, 2 * grp * d], adt)
        ug = ten("_u", [128, 2 * grp * d], F32)
        yg = ten("_y", [128, 2 * grp * d], F32)
        yb = ten("_yb", [128, 2 * grp * d], BF16) if fm_d is not None else None
        fms = ten("_fms", [128, 2 * d], BF16) if fm_d is not None else None
        stt = ten("_stt", [128, 2 * 12 * grp], F32)
        gb = ten("_gb", [128, 2 * d], F32) if g_d is not None else None
        ptp = es.enter_context(p.nc.psum_tensor(tag + "_ptp", [128, 1024], BF16)) \
            if fm_d is not None else None

        if g_d is not None:
            def f(e, dma):
                dma(out=gb[:, 0:d], in_=bcast_row(g_d, 128))
                dma(out=gb[:, d:2 * d], in_=bcast_row(b_d, 128))
            s_gb = sc.dma("sync", f, 2, deps=fence, note=tag + "_gb")
        else:
            s_gb = None

        y_done = [None] * len(groups)
        st_done = [None] * len(groups)
        tp_steps = {}
        cp_steps = {}
        fmst_steps = {}
        for gi, gts in enumerate(groups):
            sl = gi % 2
            o, so = sl * grp, sl * 12 * grp
            n = len(gts)

            def f_ld(e, dma, gts=gts, o=o):
                for j, t in enumerate(gts):
                    dma(out=xg[:, (o + j) * d:(o + j + 1) * d],
                        in_=x_d[t * 128:(t + 1) * 128, 0:d])
                    dma(out=ag[:, (o + j) * d:(o + j + 1) * d],
                        in_=a_d[t * 128:(t + 1) * 128, 0:d])
            deps = list(fence)
            if gi >= 2:
                deps.append(y_done[gi - 2])
                deps.append(st_done[gi - 2])
                if fm_d is not None:
                    lastt = (gi - 2) * grp + len(groups[gi - 2]) - 1
                    deps.append(tp_steps[lastt])
            s_ld = sc.dma("sync", f_ld, 2 * n, deps=deps, note=f"{tag}_ld{gi}")

            def ln_stats(src_buf, so_m, so_ssq, so_var, so_rstd, dep, eps_v,
                         sq_scratch, so_msq=None, nn=n, oo=o):
                if so_msq is None:
                    so_msq = so_rstd + grp * 4
                def f_sq(e):
                    for j in range(nn):
                        ins = e.activation(sq_scratch[:, (oo + j) * d:(oo + j + 1) * d],
                                           src_buf[:, (oo + j) * d:(oo + j + 1) * d],
                                           AF.Square,
                                           accum_out=stt[:, so_ssq + j:so_ssq + j + 1])
                    return ins
                s_sq = sc.step("scalar", f_sq, deps=dep, note=f"{tag}_sq")

                def f_sum(e):
                    for j in range(nn):
                        ins = e.tensor_reduce(stt[:, so_m + j:so_m + j + 1],
                                              src_buf[:, (oo + j) * d:(oo + j + 1) * d],
                                              axis=AX.X, op=ALU.add)
                    return ins
                s_sum = sc.step("vector", f_sum, deps=dep, note=f"{tag}_sm")

                def f_mean(e):
                    return e.tensor_scalar(stt[:, so_m:so_m + nn],
                                           stt[:, so_m:so_m + nn],
                                           1.0 / d, None, ALU.mult)
                s_mean = sc.step("vector", f_mean, deps=[s_sum], note=f"{tag}_mn")

                def f_v1(e):
                    e.tensor_scalar(stt[:, so_var:so_var + nn],
                                    stt[:, so_ssq:so_ssq + nn],
                                    1.0 / d, eps_v, ALU.mult, ALU.add)
                    return e.tensor_tensor(stt[:, so_msq:so_msq + nn],
                                           stt[:, so_m:so_m + nn],
                                           stt[:, so_m:so_m + nn], ALU.mult)
                s_v1 = sc.step("vector", f_v1, deps=[s_sq, s_mean],
                               note=f"{tag}_v1")

                def f_v2(e):
                    return e.tensor_sub(stt[:, so_var:so_var + nn],
                                        stt[:, so_var:so_var + nn],
                                        stt[:, so_msq:so_msq + nn])
                s_v2 = sc.step("vector", f_v2, deps=[s_v1], note=f"{tag}_v2")

                def f_sqrt(e):
                    return e.sqrt(stt[:, so_rstd:so_rstd + nn],
                                  stt[:, so_var:so_var + nn])
                s_q = sc.step("scalar", f_sqrt, deps=[s_v2], note=f"{tag}_q")

                def f_rec(e):
                    return e.reciprocal(stt[:, so_rstd:so_rstd + nn],
                                        stt[:, so_rstd:so_rstd + nn])
                return sc.step("vector", f_rec, deps=[s_q], note=f"{tag}_rc")

            if mode == "sum_ln":
                s_r1 = ln_stats(xg, so, so + grp, so + 2 * grp, so + 3 * grp,
                                [s_ld], eps, ug)

                def f_u(e, o=o, so=so, n=n):
                    for j in range(n):
                        ins = e.tensor_scalar(ug[:, (o + j) * d:(o + j + 1) * d],
                                              xg[:, (o + j) * d:(o + j + 1) * d],
                                              stt[:, so + j:so + j + 1],
                                              stt[:, so + 3 * grp + j:so + 3 * grp + j + 1],
                                              ALU.subtract, ALU.mult)
                    return ins
                s_u = sc.step("vector", f_u, deps=[s_r1], note=f"{tag}_u{gi}")

                def f_u2(e, o=o, n=n):
                    for j in range(n):
                        ins = e.tensor_tensor(ug[:, (o + j) * d:(o + j + 1) * d],
                                              ug[:, (o + j) * d:(o + j + 1) * d],
                                              ag[:, (o + j) * d:(o + j + 1) * d],
                                              ALU.add)
                    return ins
                s_u2 = sc.step("vector", f_u2, deps=[s_u, s_ld], note=f"{tag}_u2{gi}")
                s_r2 = ln_stats(ug, so + 4 * grp, so + 5 * grp, so + 6 * grp,
                                so + 7 * grp, [s_u2], eps2, yg)

                def f_y(e, o=o, so=so, n=n):
                    for j in range(n):
                        ins = e.tensor_scalar(yg[:, (o + j) * d:(o + j + 1) * d],
                                              ug[:, (o + j) * d:(o + j + 1) * d],
                                              stt[:, so + 4 * grp + j:so + 4 * grp + j + 1],
                                              stt[:, so + 7 * grp + j:so + 7 * grp + j + 1],
                                              ALU.subtract, ALU.mult)
                    return ins
                s_y = sc.step("vector", f_y, deps=[s_r2], note=f"{tag}_y{gi}")

                def f_g(e, o=o, n=n):
                    for j in range(n):
                        ins = e.tensor_tensor(yg[:, (o + j) * d:(o + j + 1) * d],
                                              yg[:, (o + j) * d:(o + j + 1) * d],
                                              gb[:, 0:d], ALU.mult)
                    return ins
                s_y = sc.step("vector", f_g, deps=[s_y, s_gb], note=f"{tag}_g{gi}")

                def f_b(e, o=o, n=n):
                    for j in range(n):
                        ins = e.tensor_tensor(yg[:, (o + j) * d:(o + j + 1) * d],
                                              yg[:, (o + j) * d:(o + j + 1) * d],
                                              gb[:, d:2 * d], ALU.add)
                    return ins
                s_y = sc.step("vector", f_b, deps=[s_y], note=f"{tag}_b{gi}")
                out_src = yg
                store_d = tm_d
            elif mode == "x_cln":
                def cast_a(e, o=o, n=n):
                    for j in range(n):
                        ins = e.tensor_copy(yg[:, (o + j) * d:(o + j + 1) * d],
                                            ag[:, (o + j) * d:(o + j + 1) * d])
                    return ins
                s_ca = sc.step("vector", cast_a, deps=[s_ld], note=f"{tag}_ca{gi}")
                s_r1 = ln_stats(yg, so, so + grp, so + 2 * grp, so + 3 * grp,
                                [s_ca], eps, ug)

                def f_y(e, o=o, so=so, n=n):
                    for j in range(n):
                        ins = e.tensor_scalar(yg[:, (o + j) * d:(o + j + 1) * d],
                                              yg[:, (o + j) * d:(o + j + 1) * d],
                                              stt[:, so + j:so + j + 1],
                                              stt[:, so + 3 * grp + j:so + 3 * grp + j + 1],
                                              ALU.subtract, ALU.mult)
                    return ins
                s_y = sc.step("vector", f_y, deps=[s_r1], note=f"{tag}_y{gi}")

                def f_g(e, o=o, n=n):
                    for j in range(n):
                        ins = e.tensor_tensor(yg[:, (o + j) * d:(o + j + 1) * d],
                                              yg[:, (o + j) * d:(o + j + 1) * d],
                                              gb[:, 0:d], ALU.mult)
                    return ins
                s_y = sc.step("vector", f_g, deps=[s_y, s_gb], note=f"{tag}_g{gi}")

                def f_b(e, o=o, n=n):
                    for j in range(n):
                        ins = e.tensor_tensor(yg[:, (o + j) * d:(o + j + 1) * d],
                                              yg[:, (o + j) * d:(o + j + 1) * d],
                                              gb[:, d:2 * d], ALU.add)
                    return ins
                s_y = sc.step("vector", f_b, deps=[s_y], note=f"{tag}_b{gi}")

                def f_fin(e, o=o, n=n):
                    for j in range(n):
                        ins = e.scalar_tensor_tensor(
                            yg[:, (o + j) * d:(o + j + 1) * d],
                            yg[:, (o + j) * d:(o + j + 1) * d], coef,
                            xg[:, (o + j) * d:(o + j + 1) * d],
                            ALU.mult, ALU.add)
                    return ins
                s_y = sc.step("vector", f_fin, deps=[s_y], note=f"{tag}_f{gi}")
                out_src = yg
                store_d = y_d
            elif mode == "x_ca":
                def f_fin(e, o=o, n=n):
                    for j in range(n):
                        ins = e.scalar_tensor_tensor(
                            yg[:, (o + j) * d:(o + j + 1) * d],
                            ag[:, (o + j) * d:(o + j + 1) * d], coef,
                            xg[:, (o + j) * d:(o + j + 1) * d],
                            ALU.mult, ALU.add)
                    return ins
                s_y = sc.step("vector", f_fin, deps=[s_ld], note=f"{tag}_f{gi}")
                out_src = yg
                store_d = y_d
            else:
                raise ValueError(mode)
            y_done[gi] = s_y

            def f_st(e, dma, gts=gts, o=o, out_src=out_src, store_d=store_d):
                for j, t in enumerate(gts):
                    dma(out=store_d[t * 128:(t + 1) * 128, 0:d],
                        in_=out_src[:, (o + j) * d:(o + j + 1) * d])
            st_done[gi] = sc.dma("gpsimd", f_st, n, deps=[s_y],
                                 note=f"{tag}_st{gi}")

            if fm_d is not None:
                def f_cast(e, o=o, n=n):
                    for j in range(n):
                        ins = e.tensor_copy(yb[:, (o + j) * d:(o + j + 1) * d],
                                            yg[:, (o + j) * d:(o + j + 1) * d])
                    return ins
                s_cb = sc.step("vector", f_cast, deps=[s_y], note=f"{tag}_cb{gi}")
                for j, t in enumerate(gts):
                    ti = gi * grp + j

                    def f_tp(e, o=o, j=j):
                        for c in range(ncd):
                            ins = e.transpose(
                                ptp[:, c * 128:(c + 1) * 128],
                                yb[:, (o + j) * d + c * 128:(o + j) * d + (c + 1) * 128],
                                p.identb[:, :])
                        return ins
                    deps = [s_cb]
                    if ti >= 1:
                        deps.append(cp_steps.get(ti - 1))
                    tp_steps[ti] = sc.step("tensor", f_tp, deps=deps,
                                           note=f"{tag}_tp{ti}")

                    def f_cp(e, ti=ti):
                        fsl = ti % 2
                        return e.tensor_copy(fms[:, fsl * d:(fsl + 1) * d],
                                             ptp[:, 0:d])
                    deps = [tp_steps[ti]]
                    if ti >= 2:
                        deps.append(fmst_steps.get(ti - 2))
                    cp_steps[ti] = sc.step("vector", f_cp, deps=deps,
                                           note=f"{tag}_cp{ti}")

                    def f_fmst(e, dma, t=t, ti=ti):
                        fsl = ti % 2
                        dma(out=fm_d[0:ncd, 0:128, t * 128:(t + 1) * 128]
                            .rearrange("c p x -> p c x"),
                            in_=fms[:, fsl * d:(fsl + 1) * d]
                            .rearrange("p (c x) -> p c x", c=ncd))
                    fmst_steps[ti] = sc.dma("gpsimd", f_fmst, 1,
                                            deps=[cp_steps[ti]],
                                            note=f"{tag}_fmst{ti}")
    return sc.lasts()



def ph_logits(p, src_tm_d, w_d, E, b_d=None, ln_first=False, mode="store",
              out_d=None, w_is_ExD=False, eps=1e-5, ntiles=None, cap=None,
              tag="lg"):
    cfg = p.cfg
    sc = p.sched
    d, ncd = cfg.d, cfg.nc_d
    nt = cfg.nt if ntiles is None else ntiles
    fence = sc.lasts()
    with ExitStack() as es:
        ten = lambda n, sh, dt: es.enter_context(p.nc.sbuf_tensor(tag + n, sh, dt))
        xg = ten("_x", [128, 2 * d], F32)
        xt = ten("_t", [128, 2 * d], F32)
        wr = ten("_w", [128, ncd * E], F32)
        ro = ten("_r", [128, 2 * E], F32)
        lg = ten("_l", [128, 2 * E], F32)
        sg = ten("_s", [128, 10 * E], F32)
        io = ten("_io", [128, 2 * E], F32)
        equ = ten("_equ", [128, 2 * E], mybir.dt.uint8)
        prevd = {}
        stt = ten("_st", [128, 16], F32)
        ptt = es.enter_context(p.nc.psum_tensor(tag + "_pt", [128, 1024], F32))
        plg = es.enter_context(p.nc.psum_tensor(tag + "_pl", [128, 2 * E], F32))

        def f_w(e, dma):
            for k in range(ncd):
                if w_is_ExD:
                    dma(out=wr[:, k * E:(k + 1) * E],
                        in_=w_d.ap().transpose([1, 0])[k * 128:(k + 1) * 128, 0:E],
                        allow_slow_non_contiguous=True)
                else:
                    dma(out=wr[:, k * E:(k + 1) * E],
                        in_=w_d[k * 128:(k + 1) * 128, 0:E])
        s_w = sc.dma("sync", f_w, ncd, deps=fence, note=tag + "_w")
        pre = [s_w]
        if b_d is not None:
            def f_b(e, dma):
                dma(out=ro[:, 0:E], in_=bcast_row(b_d, 128))
            pre.append(sc.dma("sync", f_b, 1, deps=fence, note=tag + "_b"))
        if mode == "l0top2":
            p.extra_inputs.setdefault("c_iota8", np.arange(8, dtype=np.float32))
            iota_d = p.dram_const("c_iota8", [8], F32)

            def f_i(e, dma):
                dma(out=io[:, 0:E], in_=bcast_row(iota_d, 128))
            pre.append(sc.dma("sync", f_i, 1, deps=fence, note=tag + "_i"))

            def f_big(e):
                return e.memset(io[:, E:2 * E], 1.0e9)
            pre.append(sc.step("vector", f_big, deps=fence, note=tag + "_bg"))

        prev = {}
        for t in range(nt):
            sl = t % 2

            def f_ld(e, dma, t=t, sl=sl):
                dma(out=xg[:, sl * d:(sl + 1) * d],
                    in_=src_tm_d[t * 128:(t + 1) * 128, 0:d])
            deps = fence if t < 2 else [prev[(t - 2, "done")]]
            s_ld = sc.dma("sync", f_ld, 1, deps=list(deps), note=f"{tag}_ld{t}")

            src_buf, src_dep = xg, s_ld
            if ln_first:
                def f_sq(e, sl=sl):
                    return e.activation(xt[:, sl * d:(sl + 1) * d],
                                        xg[:, sl * d:(sl + 1) * d], AF.Square,
                                        accum_out=stt[:, sl * 8 + 1:sl * 8 + 2])
                dd = [s_ld] + ([prev[(t - 2, "tp")]] if t >= 2 else [])
                s_sq = sc.step("scalar", f_sq, deps=dd, note=f"{tag}_sq{t}")

                def f_sm(e, sl=sl):
                    return e.tensor_reduce(stt[:, sl * 8:sl * 8 + 1],
                                           xg[:, sl * d:(sl + 1) * d],
                                           axis=AX.X, op=ALU.add)
                s_sm = sc.step("vector", f_sm, deps=[s_ld], note=f"{tag}_sm{t}")

                def f_v(e, sl=sl):
                    e.tensor_scalar(stt[:, sl * 8:sl * 8 + 1],
                                    stt[:, sl * 8:sl * 8 + 1], 1.0 / d, None,
                                    ALU.mult)
                    return e.tensor_scalar(stt[:, sl * 8 + 1:sl * 8 + 2],
                                           stt[:, sl * 8 + 1:sl * 8 + 2],
                                           1.0 / d, eps, ALU.mult, ALU.add)
                s_v = sc.step("vector", f_v, deps=[s_sm, s_sq], note=f"{tag}_v{t}")

                def f_v2(e, sl=sl):
                    return e.scalar_tensor_tensor(
                        stt[:, sl * 8 + 2:sl * 8 + 3],
                        stt[:, sl * 8:sl * 8 + 1], 1.0,
                        stt[:, sl * 8:sl * 8 + 1], ALU.mult, ALU.mult)
                s_v2 = sc.step("vector", f_v2, deps=[s_v], note=f"{tag}_v2{t}")

                def f_v3(e, sl=sl):
                    return e.tensor_sub(stt[:, sl * 8 + 1:sl * 8 + 2],
                                        stt[:, sl * 8 + 1:sl * 8 + 2],
                                        stt[:, sl * 8 + 2:sl * 8 + 3])
                s_v3 = sc.step("vector", f_v3, deps=[s_v2], note=f"{tag}_v3{t}")

                def f_q(e, sl=sl):
                    return e.sqrt(stt[:, sl * 8 + 3:sl * 8 + 4],
                                  stt[:, sl * 8 + 1:sl * 8 + 2])
                s_q = sc.step("scalar", f_q, deps=[s_v3], note=f"{tag}_qq{t}")

                def f_rc(e, sl=sl):
                    return e.reciprocal(stt[:, sl * 8 + 3:sl * 8 + 4],
                                        stt[:, sl * 8 + 3:sl * 8 + 4])
                s_rc = sc.step("vector", f_rc, deps=[s_q], note=f"{tag}_rc{t}")

                def f_y(e, sl=sl):
                    return e.tensor_scalar(xg[:, sl * d:(sl + 1) * d],
                                           xg[:, sl * d:(sl + 1) * d],
                                           stt[:, sl * 8:sl * 8 + 1],
                                           stt[:, sl * 8 + 3:sl * 8 + 4],
                                           ALU.subtract, ALU.mult)
                    return ins
                src_dep = sc.step("vector", f_y, deps=[s_rc], note=f"{tag}_y{t}")

            tp_cp = []
            for hh in range(2):
                def f_tp(e, sl=sl, hh=hh):
                    for c in range(4):
                        ins = e.transpose(
                            ptt[:, hh * 512 + c * 128:hh * 512 + (c + 1) * 128],
                            xg[:, sl * d + (hh * 4 + c) * 128:sl * d + (hh * 4 + c + 1) * 128],
                            p.identf[:, :])
                    return ins
                dd = [src_dep]
                if t >= 1:
                    dd.append(prev.get((t - 1, "cp%d" % hh)))
                s_tp = sc.step("tensor", f_tp, deps=dd, note=f"{tag}_tp{t}_{hh}")
                prev[(t, "tp")] = s_tp

                def f_cp(e, sl=sl, hh=hh):
                    return e.tensor_copy(xt[:, sl * d + hh * 512:sl * d + (hh + 1) * 512],
                                         ptt[:, hh * 512:(hh + 1) * 512])
                dd = [s_tp]
                if t >= 2:
                    dd.append(prev[(t - 2, "mm")])
                s_cp = sc.step("vector", f_cp, deps=dd, note=f"{tag}_cp{t}_{hh}")
                prev[(t, "cp%d" % hh)] = s_cp
                tp_cp.append(s_cp)

            def f_mm(e, sl=sl):
                for k in range(ncd):
                    ins = e.matmul(plg[:, sl * E:(sl + 1) * E],
                                   xt[:, sl * d + k * 128:sl * d + (k + 1) * 128],
                                   wr[:, k * E:(k + 1) * E],
                                   start=(k == 0), stop=(k == ncd - 1))
                return ins
            dd = tp_cp + pre
            if t >= 2:
                dd.append(prev[(t - 2, "lg")])
            s_mm = sc.step("tensor", f_mm, deps=dd, note=f"{tag}_mm{t}")
            prev[(t, "mm")] = s_mm

            def f_lg(e, sl=sl):
                if b_d is not None:
                    return e.tensor_tensor(lg[:, sl * E:(sl + 1) * E],
                                           plg[:, sl * E:(sl + 1) * E],
                                           ro[:, 0:E], ALU.add)
                return e.tensor_copy(lg[:, sl * E:(sl + 1) * E],
                                     plg[:, sl * E:(sl + 1) * E])
            dd = [s_mm] + ([prev[(t - 2, "done")]] if t >= 2 else [])
            s_lg = sc.step("vector", f_lg, deps=dd, note=f"{tag}_lg{t}")
            prev[(t, "lg")] = s_lg

            if mode == "store":
                def f_st(e, dma, t=t, sl=sl):
                    dma(out=out_d[t * 128:(t + 1) * 128, 0:E],
                        in_=lg[:, sl * E:(sl + 1) * E])
                s_done = sc.dma("gpsimd", f_st, 1, deps=[s_lg],
                                note=f"{tag}_st{t}")
            else:
                s_done = emit_top2(p, sc, tag, t, sl, E, lg, sg, io, stt, out_d,
                                   s_lg, prev, equ=equ, prevd=prevd)
            prev[(t, "done")] = s_done
    return sc.lasts()


def emit_top2(p, sc, tag, t, sl, E, lg, sg, io, stt, gates_d, s_lg, prev,
              equ=None, prevd=None):
    so = sl * 5 * E
    base = 8 if False else 8
    c0 = sl * 4 + 0 + 8

    def f_nm(e):
        return e.reduce_max(stt[:, c0:c0 + 1], lg[:, sl * E:(sl + 1) * E],
                            axis=AX.X, negate=True)
    dd = [s_lg] + ([prev[(t - 2, "done")]] if t >= 2 else [])
    s_nm = sc.step("vector", f_nm, deps=dd, note=f"{tag}_g1_{t}")

    def f_ex(e):
        return e.activation(sg[:, so:so + E], lg[:, sl * E:(sl + 1) * E],
                            AF.Exp, bias=stt[:, c0:c0 + 1],
                            accum_out=stt[:, c0 + 1:c0 + 2])
    s_ex = sc.step("scalar", f_ex, deps=[s_nm], note=f"{tag}_g2_{t}")

    def f_rz(e):
        return e.reciprocal(stt[:, c0 + 1:c0 + 2], stt[:, c0 + 1:c0 + 2])
    s_rz = sc.step("vector", f_rz, deps=[s_ex], note=f"{tag}_g3_{t}")

    def f_pr(e):
        return e.tensor_scalar(sg[:, so:so + E], sg[:, so:so + E],
                               stt[:, c0 + 1:c0 + 2], None, ALU.mult)
    s_pr = sc.step("vector", f_pr, deps=[s_rz], note=f"{tag}_g4_{t}")

    def f_mx(e):
        return e.tensor_scalar(stt[:, c0 + 2:c0 + 3], stt[:, c0:c0 + 1],
                               -1.0, None, ALU.mult)
    s_mx = sc.step("vector", f_mx, deps=[s_nm], note=f"{tag}_g5_{t}")

    def f_eq(e):
        return e.tensor_scalar(equ[:, sl * E:(sl + 1) * E],
                               lg[:, sl * E:(sl + 1) * E],
                               stt[:, c0 + 2:c0 + 3], None, ALU.is_ge)
    s_eq = sc.step("vector", f_eq, deps=[s_mx, prevd.get((t - 2, "eq"))],
                   note=f"{tag}_g6_{t}")
    prevd[(t, "eq")] = s_eq

    def f_cd(e):
        return e.select(sg[:, so + 2 * E:so + 3 * E], equ[:, sl * E:(sl + 1) * E],
                        io[:, 0:E], io[:, E:2 * E], add_drain=True)
    s_cd = sc.step("vector", f_cd, deps=[s_eq], note=f"{tag}_g7_{t}")

    def f_i1(e):
        return e.tensor_reduce(stt[:, c0 + 3:c0 + 4], sg[:, so + 2 * E:so + 3 * E],
                               axis=AX.X, op=ALU.min)
    s_i1 = sc.step("vector", f_i1, deps=[s_cd], note=f"{tag}_g8_{t}")

    def f_o1(e):
        return e.tensor_scalar(sg[:, so + E:so + 2 * E], io[:, 0:E],
                               stt[:, c0 + 3:c0 + 4], None, ALU.is_equal)
    s_o1 = sc.step("vector", f_o1, deps=[s_i1], note=f"{tag}_g9_{t}")

    sc2 = 8 + sl * 4
    def f_p1(e):
        return e.tensor_reduce(sg[:, so + 3 * E:so + 3 * E + 1],
                               sg[:, so:so + E], axis=AX.X, op=ALU.max)
    s_p1 = sc.step("vector", f_p1, deps=[s_pr, s_o1], note=f"{tag}_gA_{t}")

    def f_mp(e):
        return e.scalar_tensor_tensor(sg[:, so + 2 * E:so + 3 * E],
                                      sg[:, so + E:so + 2 * E], -1.0,
                                      io[:, E:2 * E] if False else sg[:, so:so + E],
                                      ALU.mult, ALU.bypass) \
            if False else \
            e.tensor_tensor(sg[:, so + 2 * E:so + 3 * E],
                            sg[:, so:so + E], sg[:, so + E:so + 2 * E],
                            ALU.mult)
    s_mp0 = sc.step("vector", f_mp, deps=[s_pr, s_o1], note=f"{tag}_gB_{t}")

    def f_mp2(e):
        return e.tensor_sub(sg[:, so + 2 * E:so + 3 * E], sg[:, so:so + E],
                            sg[:, so + 2 * E:so + 3 * E])
    s_mp = sc.step("vector", f_mp2, deps=[s_mp0], note=f"{tag}_gC_{t}")

    def f_p2(e):
        return e.tensor_reduce(sg[:, so + 3 * E + 1:so + 3 * E + 2],
                               sg[:, so + 2 * E:so + 3 * E], axis=AX.X,
                               op=ALU.max)
    s_p2 = sc.step("vector", f_p2, deps=[s_mp], note=f"{tag}_gD_{t}")

    def f_e2(e):
        return e.tensor_scalar(equ[:, sl * E:(sl + 1) * E],
                               sg[:, so + 2 * E:so + 3 * E],
                               sg[:, so + 3 * E + 1:so + 3 * E + 2], None,
                               ALU.is_ge)
    s_e2 = sc.step("vector", f_e2, deps=[s_p2, s_cd], note=f"{tag}_gE_{t}")

    def f_cd2(e):
        return e.select(sg[:, so + 2 * E:so + 3 * E], equ[:, sl * E:(sl + 1) * E],
                        io[:, 0:E], io[:, E:2 * E], add_drain=True)
    s_cd2 = sc.step("vector", f_cd2, deps=[s_e2], note=f"{tag}_gF_{t}")

    def f_i2(e):
        return e.tensor_reduce(sg[:, so + 3 * E + 2:so + 3 * E + 3],
                               sg[:, so + 2 * E:so + 3 * E], axis=AX.X,
                               op=ALU.min)
    s_i2 = sc.step("vector", f_i2, deps=[s_cd2], note=f"{tag}_gG_{t}")

    def f_o2(e):
        return e.tensor_scalar(sg[:, so + 4 * E:so + 5 * E], io[:, 0:E],
                               sg[:, so + 3 * E + 2:so + 3 * E + 3], None,
                               ALU.is_equal)
    s_o2 = sc.step("vector", f_o2, deps=[s_i2], note=f"{tag}_gH_{t}")

    def f_d12(e):
        return e.tensor_sub(sg[:, so + 3 * E:so + 3 * E + 1],
                            sg[:, so + 3 * E:so + 3 * E + 1],
                            sg[:, so + 3 * E + 1:so + 3 * E + 2])
    s_d = sc.step("vector", f_d12, deps=[s_p1, s_p2], note=f"{tag}_gI_{t}")

    def f_sg(e):
        return e.activation(sg[:, so + 3 * E:so + 3 * E + 1],
                            sg[:, so + 3 * E:so + 3 * E + 1], AF.Sigmoid)
    s_s = sc.step("scalar", f_sg, deps=[s_d], note=f"{tag}_gJ_{t}")

    def f_g2(e):
        return e.tensor_scalar(sg[:, so + 3 * E + 1:so + 3 * E + 2],
                               sg[:, so + 3 * E:so + 3 * E + 1],
                               -1.0, 1.0, ALU.mult, ALU.add)
    s_g2 = sc.step("vector", f_g2, deps=[s_s], note=f"{tag}_gK_{t}")

    def f_t1(e):
        return e.tensor_scalar(sg[:, so + E:so + 2 * E], sg[:, so + E:so + 2 * E],
                               sg[:, so + 3 * E:so + 3 * E + 1], None, ALU.mult)
    s_t1 = sc.step("vector", f_t1, deps=[s_g2], note=f"{tag}_gL_{t}")

    def f_t2(e):
        return e.scalar_tensor_tensor(sg[:, so:so + E],
                                      sg[:, so + 4 * E:so + 5 * E],
                                      sg[:, so + 3 * E + 1:so + 3 * E + 2],
                                      sg[:, so + E:so + 2 * E],
                                      ALU.mult, ALU.add)
    s_t2 = sc.step("vector", f_t2, deps=[s_t1, s_o2], note=f"{tag}_gM_{t}")

    def f_st(e, dma):
        dma(out=gates_d[t * 128:(t + 1) * 128, 0:E], in_=sg[:, so:so + E])
    return sc.dma("gpsimd", f_st, 1, deps=[s_t2], note=f"{tag}_gst{t}")



def ph_route_l1(p, logits_d, gates_d, Tt, E, cap, tag="rt"):
    cfg = p.cfg
    sc = p.sched
    nt = Tt // 128
    fence = sc.lasts()
    p.extra_inputs.setdefault("c_iota8", np.arange(8, dtype=np.float32))
    p.extra_inputs.setdefault("c_utri",
                              np.triu(np.ones((128, 128), dtype=np.float32)))
    p.extra_inputs.setdefault("c_ones", np.ones((128, 128), dtype=np.float32))
    iota_d = p.dram_const("c_iota8", [8], F32)
    utri_d = p.dram_const("c_utri", [128, 128], F32)
    ones_d = p.dram_const("c_ones", [128, 128], F32)
    with ExitStack() as es:
        ten = lambda n, sh, dt: es.enter_context(p.nc.sbuf_tensor(tag + n, sh, dt))
        lg = ten("_l", [128, 2 * E], F32)
        sg = ten("_s", [128, 8 * E], F32)
        equ = ten("_eq", [128, 2 * E], mybir.dt.uint8)
        io = ten("_io", [128, 2 * E], F32)
        ut = ten("_ut", [128, 128], F32)
        on = ten("_on", [128, 128], F32)
        car = ten("_cr", [128, 2 * E], F32)
        stt = ten("_stt", [128, 16], F32)
        pps = es.enter_context(p.nc.psum_tensor(tag + "_pp", [128, 2 * E], F32))
        pcr = es.enter_context(p.nc.psum_tensor(tag + "_pc", [128, E], F32))

        def f_c(e, dma):
            dma(out=io[:, 0:E], in_=bcast_row(iota_d, 128))
            dma(out=ut[:, :], in_=utri_d[:, :])
            dma(out=on[:, :], in_=ones_d[:, :])
        s_c = sc.dma("sync", f_c, 3, deps=fence, note=tag + "_c")

        def f_big(e):
            return e.memset(io[:, E:2 * E], 1.0e9)
        s_big = sc.step("vector", f_big, deps=fence, note=tag + "_bg")

        def f_c0(e):
            return e.memset(car[:, 0:2 * E], 0.0)
        s_car0 = sc.step("vector", f_c0, deps=fence, note=tag + "_c0")

        prev = {}
        s_colsum_prev = None
        s_carcp_prev = s_car0
        for t in range(nt):
            sl = t % 2

            def f_ld(e, dma, t=t, sl=sl):
                dma(out=lg[:, sl * E:(sl + 1) * E],
                    in_=logits_d[t * 128:(t + 1) * 128, 0:E])
            deps = list(fence) if t < 2 else [prev[(t - 2, "oh")],
                                              prev[(t - 2, "st")]]
            s_ld = sc.dma("sync", f_ld, 1, deps=deps, note=f"{tag}_ld{t}")

            def f_nm(e, sl=sl):
                return e.reduce_max(stt[:, sl * 8:sl * 8 + 1],
                                    lg[:, sl * E:(sl + 1) * E],
                                    axis=AX.X, negate=True)
            dd = [s_ld] + ([prev[(t - 2, "gate")]] if t >= 2 else [])
            s_nm = sc.step("vector", f_nm, deps=dd, note=f"{tag}_nm{t}")

            def f_ex(e, sl=sl):
                return e.activation(sg[:, sl * E:(sl + 1) * E],
                                    lg[:, sl * E:(sl + 1) * E], AF.Exp,
                                    bias=stt[:, sl * 8:sl * 8 + 1],
                                    accum_out=stt[:, sl * 8 + 1:sl * 8 + 2])
            dd = [s_nm] + ([prev[(t - 2, "oh")]] if t >= 2 else [])
            s_ex = sc.step("scalar", f_ex, deps=dd, note=f"{tag}_ex{t}")

            def f_rz(e, sl=sl):
                return e.reciprocal(stt[:, sl * 8 + 1:sl * 8 + 2],
                                    stt[:, sl * 8 + 1:sl * 8 + 2])
            s_rz = sc.step("vector", f_rz, deps=[s_ex], note=f"{tag}_rz{t}")

            def f_mx(e, sl=sl):
                return e.tensor_scalar(stt[:, sl * 8 + 2:sl * 8 + 3],
                                       stt[:, sl * 8:sl * 8 + 1], -1.0, None,
                                       ALU.mult)
            s_mx = sc.step("vector", f_mx, deps=[s_nm], note=f"{tag}_mx{t}")

            def f_eq(e, sl=sl):
                return e.tensor_scalar(equ[:, sl * E:(sl + 1) * E],
                                       lg[:, sl * E:(sl + 1) * E],
                                       stt[:, sl * 8 + 2:sl * 8 + 3], None,
                                       ALU.is_ge)
            dd = [s_mx] + ([prev[(t - 2, "eq")]] if t >= 2 else [])
            s_eq = sc.step("vector", f_eq, deps=dd, note=f"{tag}_eq{t}")
            prev[(t, "eq")] = s_eq

            def f_cd(e, sl=sl):
                return e.select(sg[:, 4 * E:5 * E], equ[:, sl * E:(sl + 1) * E],
                                io[:, 0:E], io[:, E:2 * E], add_drain=True)
            dd = [s_eq, s_big, s_c] + ([prev[(t - 1, "i1")]] if t >= 1 else [])
            s_cd = sc.step("vector", f_cd, deps=dd, note=f"{tag}_cd{t}")

            def f_i1(e, sl=sl):
                return e.tensor_reduce(stt[:, sl * 8 + 3:sl * 8 + 4],
                                       sg[:, 4 * E:5 * E], axis=AX.X, op=ALU.min)
            s_i1 = sc.step("vector", f_i1, deps=[s_cd], note=f"{tag}_i1{t}")
            prev[(t, "i1")] = s_i1

            def f_o1(e, sl=sl):
                return e.tensor_scalar(sg[:, (2 + sl) * E:(3 + sl) * E],
                                       io[:, 0:E],
                                       stt[:, sl * 8 + 3:sl * 8 + 4], None,
                                       ALU.is_equal)
            s_o1 = sc.step("vector", f_o1, deps=[s_i1], note=f"{tag}_o1{t}")
            prev[(t, "oh")] = s_o1

            def f_ps(e, sl=sl):
                e.matmul(pps[:, sl * E:(sl + 1) * E], ut[:, :],
                         sg[:, (2 + sl) * E:(3 + sl) * E], start=True, stop=False)
                return e.matmul(pps[:, sl * E:(sl + 1) * E], on[0:1, 0:128],
                                car[0:1, sl * E:(sl + 1) * E], start=False,
                                stop=True)
            dd = [s_o1, s_c, s_carcp_prev]
            if t >= 2:
                dd.append(prev[(t - 2, "pos")])
            s_ps = sc.step("tensor", f_ps, deps=dd, note=f"{tag}_ps{t}")

            def f_cs(e, sl=sl, t=t):
                return e.matmul(pcr[0:1, 0:E], on[:, 0:1],
                                sg[:, (2 + sl) * E:(3 + sl) * E],
                                start=True, stop=True)
            dd = [s_o1]
            if s_colsum_prev is not None:
                dd.append(prev[(t - 1, "carcp")])
            s_cs = sc.step("tensor", f_cs, deps=dd, note=f"{tag}_cs{t}")
            s_colsum_prev = s_cs

            def f_ccp(e, sl=sl):
                return e.tensor_tensor(car[0:1, (1 - sl) * E:(2 - sl) * E],
                                       car[0:1, sl * E:(sl + 1) * E],
                                       pcr[0:1, 0:E], ALU.add)
            dd = [s_cs, s_carcp_prev]
            s_ccp = sc.step("vector", f_ccp, deps=dd, note=f"{tag}_ccp{t}")
            prev[(t, "carcp")] = s_ccp
            s_carcp_prev = s_ccp

            def f_pm(e, sl=sl):
                return e.tensor_tensor(sg[:, 5 * E:6 * E],
                                       sg[:, (2 + sl) * E:(3 + sl) * E],
                                       pps[:, sl * E:(sl + 1) * E], ALU.mult)
            dd = [s_ps] + ([prev[(t - 1, "pc")]] if t >= 1 else [])
            s_pm = sc.step("vector", f_pm, deps=dd, note=f"{tag}_pm{t}")
            prev[(t, "pos")] = s_pm

            def f_pc(e, sl=sl):
                return e.tensor_reduce(stt[:, sl * 8 + 4:sl * 8 + 5],
                                       sg[:, 5 * E:6 * E], axis=AX.X, op=ALU.add)
            dd = [s_pm] + ([prev[(t - 1, "pc")]] if t >= 1 else [])
            s_pc = sc.step("vector", f_pc, deps=dd, note=f"{tag}_pc{t}")
            prev[(t, "pc")] = s_pc

            def f_kp(e, sl=sl):
                return e.tensor_scalar(stt[:, sl * 8 + 5:sl * 8 + 6],
                                       stt[:, sl * 8 + 4:sl * 8 + 5],
                                       float(cap), None, ALU.is_le)
            s_kp = sc.step("vector", f_kp, deps=[s_pc], note=f"{tag}_kp{t}")

            def f_gt(e, sl=sl):
                return e.tensor_tensor(stt[:, sl * 8 + 6:sl * 8 + 7],
                                       stt[:, sl * 8 + 5:sl * 8 + 6],
                                       stt[:, sl * 8 + 1:sl * 8 + 2], ALU.mult)
            s_gt = sc.step("vector", f_gt, deps=[s_kp, s_rz], note=f"{tag}_gt{t}")

            def f_gg(e, sl=sl):
                return e.tensor_scalar(sg[:, (6 + sl) * E:(7 + sl) * E],
                                       sg[:, (2 + sl) * E:(3 + sl) * E],
                                       stt[:, sl * 8 + 6:sl * 8 + 7], None,
                                       ALU.mult)
            dd = [s_gt] + ([prev[(t - 2, "st")]] if t >= 2 else [])
            s_gg = sc.step("vector", f_gg, deps=dd, note=f"{tag}_gg{t}")
            prev[(t, "gate")] = s_gg

            def f_st(e, dma, t=t, sl=sl):
                dma(out=gates_d[t * 128:(t + 1) * 128, 0:E],
                    in_=sg[:, (6 + sl) * E:(7 + sl) * E])
            prev[(t, "st")] = sc.dma("gpsimd", f_st, 1, deps=[s_gg],
                                     note=f"{tag}_gst{t}")
    return sc.lasts()



def ph_moe(p, xin_fm_d, we1_d, be1_d, we2_d, be2_d, gates_d, h_fm_d,
           acc_init_tm_d, acc_out_tm_d, act="gelu", tag="moe"):
    cfg = p.cfg
    sc = p.sched
    d, F, E, ncd, nt, nF = cfg.d, cfg.F, cfg.E, cfg.nc_d, cfg.nt, cfg.nF
    T = cfg.T
    fence = sc.lasts()
    es_moe = ExitStack()
    acc = es_moe.enter_context(p.nc.sbuf_tensor(tag + "_acc", [128, nt * d], BF16))
    gsb = es_moe.enter_context(p.nc.sbuf_tensor(tag + "_g", [128, nt * E], F32))

    with ExitStack() as es:
        tmp = es.enter_context(p.nc.sbuf_tensor(tag + "_i", [128, 4 * d], F32))
        def f_g(e, dma):
            for t in range(nt):
                dma(out=gsb[:, t * E:(t + 1) * E],
                    in_=gates_d[t * 128:(t + 1) * 128, 0:E])
        s_g = sc.dma("sync", f_g, nt, deps=fence, note=tag + "_gld")
        init_steps = []
        if acc_init_tm_d is not None:
            for gi in range(nt // 4):
                def f_l(e, dma, gi=gi):
                    sl = gi % 2
                    for j in range(4):
                        dma(out=tmp[:, ((sl * 2 + 0) * 2 + j) * d:
                                    ((sl * 2 + 0) * 2 + j + 1) * d]
                            if False else
                            tmp[:, j * d:(j + 1) * d] if sl == 0 else
                            tmp[:, (j) * d:(j + 1) * d],
                            in_=acc_init_tm_d[(gi * 4 + j) * 128:
                                              (gi * 4 + j + 1) * 128, 0:d])
                deps = list(fence) if gi == 0 else [init_steps[-1]]
                s_l = sc.dma("sync", f_l, 4, deps=deps, note=f"{tag}_il{gi}")

                def f_c(e, gi=gi):
                    for j in range(4):
                        ins = e.tensor_copy(
                            acc[:, (gi * 4 + j) * d:(gi * 4 + j + 1) * d],
                            tmp[:, j * d:(j + 1) * d])
                    return ins
                s_cc = sc.step("vector", f_c, deps=[s_l], note=f"{tag}_ic{gi}")
                init_steps.append(s_cc)
        else:
            def f_z(e):
                return e.memset(acc[:, :], 0.0)
            init_steps.append(sc.step("vector", f_z, deps=fence,
                                      note=tag + "_z"))

    for ei in range(E):
        ph_proj(p, xin_fm_d, we1_d[ei], d, F, b_d=be1_d[ei],
                out_fm_d=h_fm_d, act=act, tag=f"{tag}u{ei}")
        ph_moe_down(p, h_fm_d, we2_d[ei], be2_d[ei], gsb, ei, acc,
                    extra_deps=init_steps + [s_g], tag=f"{tag}d{ei}")

    fence2 = sc.lasts()
    for gi in range(nt // 4):
        def f_st(e, dma, gi=gi):
            for j in range(4):
                t = gi * 4 + j
                dma(out=acc_out_tm_d[t * 128:(t + 1) * 128, 0:d],
                    in_=acc[:, t * d:(t + 1) * d])
        sc.dma("gpsimd", f_st, 4, deps=fence2, note=f"{tag}_ast{gi}")
    es_moe.close()
    return sc.lasts()


def ph_moe_down(p, h_fm_d, w_d, b_d, gsb, ei, acc, extra_deps=(), tag="md"):
    cfg = p.cfg
    sc = p.sched
    d, F, E, nF, nt = cfg.d, cfg.F, cfg.E, cfg.nF, cfg.nt
    fence = sc.lasts()
    with ExitStack() as es:
        ten = lambda n, sh, dt: es.enter_context(p.nc.sbuf_tensor(tag + n, sh, dt))
        wbf = ten("_w", [128, nF * d], BF16)
        wst = ten("_ws", [128, 2 * d], F32)
        bro = ten("_br", [128, d], F32)
        xg = ten("_x", [128, 2 * nF * 128], BF16)
        s0 = ten("_s0", [128, 2 * d], F32)
        pj = es.enter_context(p.nc.psum_tensor(tag + "_pj", [128, 2 * d], F32))

        wld, wcast = [], []
        for k in range(nF):
            def f(e, dma, k=k):
                dma(out=wst[:, (k % 2) * d:(k % 2) * d + d],
                    in_=w_d[k * 128:(k + 1) * 128, 0:d])
            deps = list(fence)
            if k >= 2:
                deps.append(wcast[k - 2])
            wld.append(sc.dma("sync", f, 1, deps=deps, note=f"{tag}_wld{k}"))

            def fc(e, k=k):
                return e.tensor_copy(wbf[:, k * d:(k + 1) * d],
                                     wst[:, (k % 2) * d:(k % 2) * d + d])
            wcast.append(sc.step("vector", fc, deps=[wld[k]], note=f"{tag}_wc{k}"))
        w_ready = [wcast[-1]]

        def f_b(e, dma):
            dma(out=bro[:, 0:d], in_=bcast_row(b_d, 128))
        b_ready = [sc.dma("sync", f_b, 1, deps=fence, note=tag + "_b")]

        tst = {}
        for t in range(nt):
            sl = t % 2

            def f_ld(e, dma, t=t, sl=sl):
                q4 = max(nF // 4, 1)
                for pc in range(0, nF, q4):
                    dma(out=xg[:, (sl * nF + pc) * 128:(sl * nF + pc + q4) * 128]
                        .rearrange("p (c x) -> p c x", c=q4),
                        in_=h_fm_d[pc:pc + q4, 0:128, t * 128:(t + 1) * 128]
                        .rearrange("c p x -> p c x"))
            deps = list(fence)
            if t >= 2:
                deps.append(tst[t - 2][0])
            s_ld = sc.dma("sync", f_ld, 4 if nF >= 4 else nF, deps=deps,
                          note=f"{tag}_ld{t}")

            def f_mm(e, t=t, sl=sl):
                for k in range(nF):
                    for cg in range(d // 512):
                        ins = e.matmul(
                            pj[:, (sl * (d // 512) + cg) * 512:
                               (sl * (d // 512) + cg + 1) * 512],
                            xg[:, (sl * nF + k) * 128:(sl * nF + k + 1) * 128],
                            wbf[:, k * d + cg * 512:k * d + (cg + 1) * 512],
                            start=(k == 0), stop=(k == nF - 1))
                return ins
            deps = [s_ld] + w_ready
            if t >= 2:
                deps.append(tst[t - 2][1])
            s_mm = sc.step("tensor", f_mm, deps=deps, note=f"{tag}_mm{t}")

            def f_e1(e, t=t, sl=sl):
                return e.tensor_tensor(s0[:, sl * d:(sl + 1) * d],
                                       pj[:, sl * d:(sl + 1) * d],
                                       bro[:, 0:d], ALU.add)
            deps = [s_mm] + b_ready
            if t >= 2:
                deps.append(tst[t - 2][2])
            s_e1 = sc.step("vector", f_e1, deps=deps, note=f"{tag}_e1{t}")

            def f_e2(e, t=t, sl=sl):
                return e.scalar_tensor_tensor(acc[:, t * d:(t + 1) * d],
                                              s0[:, sl * d:(sl + 1) * d],
                                              gsb[:, t * E + ei:t * E + ei + 1],
                                              acc[:, t * d:(t + 1) * d],
                                              ALU.mult, ALU.add)
            deps = [s_e1] + list(extra_deps)
            s_e2 = sc.step("vector", f_e2, deps=deps, note=f"{tag}_e2{t}")
            tst[t] = (s_e2, s_e1, s_e2)
    return sc.lasts()



def ph_lattn(p, qfm_d, kfm_d, vtm_d, ctxfm_d, tag="la"):
    cfg = p.cfg
    sc = p.sched
    S, d, nh, dh, ncd = cfg.S, cfg.d, cfg.nh, cfg.dh, cfg.nc_d
    C = S // 128
    NQG = S // 512
    fence = sc.lasts()
    if "c_lmask" not in p.extra_inputs:
        m = np.zeros((4, 128, 512), dtype=np.float32)
        kr = np.arange(128)[:, None]
        qc = np.arange(512)[None, :]
        for off in range(4):
            m[off] = (kr <= qc - 128 * off)
        p.extra_inputs["c_lmask"] = m
    lm_d = p.dram_const("c_lmask", [4, 128, 512], F32)
    insts = [(s, h) for s in range(cfg.nseq) for h in range(nh)]
    with ExitStack() as es:
        ten = lambda n, sh, dt: es.enter_context(p.nc.sbuf_tensor(tag + n, sh, dt))
        qsb = ten("_q", [128, 2 * S], BF16)
        ksb = ten("_k", [128, 2 * S], BF16)
        vsb = ten("_v", [128, 2 * C * 64], BF16)
        wtb = ten("_w", [128, C * 512], BF16)
        cst = ten("_c", [128, 2 * 512], F32)
        lmf = ten("_mf", [128, 4 * 512], F32)
        psc = es.enter_context(p.nc.psum_tensor(tag + "_ps", [128, 2 * 512], F32))
        pcx = es.enter_context(p.nc.psum_tensor(tag + "_px", [128, 512], F32))

        def f_m(e, dma):
            dma(out=lmf[:, :].rearrange("p (o x) -> p o x", o=4),
                in_=lm_d[0:4, 0:128, 0:512].rearrange("o p x -> p o x"))
        s_m = sc.dma("sync", f_m, 1, deps=fence, note=tag + "_m")

        s_mb = s_m

        prev_rel = {}
        prev_wev = {}
        prev_ctx = None
        prev_cst = {}
        for ii, (s, h) in enumerate(insts):
            sl = ii % 2
            qo = sl * S
            vo = sl * C * 64

            def f_q(e, dma, s=s, h=h, qo=qo):
                dma(out=qsb[0:64, qo:qo + S],
                    in_=qfm_d[h // 2, (h % 2) * 64:(h % 2) * 64 + 64,
                              s * S:(s + 1) * S])
                dma(out=ksb[0:64, qo:qo + S],
                    in_=kfm_d[h // 2, (h % 2) * 64:(h % 2) * 64 + 64,
                              s * S:(s + 1) * S])

            def f_v(e, dma, s=s, h=h, vo=vo):
                q4 = max(C // 4, 1)
                for pc in range(0, C, q4):
                    dma(out=vsb[:, vo + pc * 64:vo + (pc + q4) * 64],
                        in_=vtm_d[s * S + pc * 128:s * S + (pc + q4) * 128,
                                  h * 64:(h + 1) * 64]
                        .rearrange("(c p) x -> p c x", p=128))
            deps = list(fence)
            if ii >= 2:
                deps.append(prev_rel[ii - 2])
            s_ldq = sc.dma("sync", f_q, 2, deps=deps, note=f"{tag}_lq{ii}")
            s_ldv = sc.dma("sync", f_v, 4 if C >= 4 else C, deps=deps,
                           note=f"{tag}_lv{ii}")

            last = None
            for qg in range(NQG):
                w_evs = []
                for kt in range(4 * qg + 4):
                    off = kt - 4 * qg
                    pb = kt % 2

                    def f_w(e, kt=kt, qg=qg, pb=pb, qo=qo):
                        return e.matmul(psc[:, pb * 512:(pb + 1) * 512],
                                        ksb[0:64, qo + kt * 128:qo + (kt + 1) * 128],
                                        qsb[0:64, qo + qg * 512:qo + (qg + 1) * 512],
                                        start=True, stop=True)
                    deps = [s_ldq]
                    pe_prev = prev_wev.get((ii, qg, kt - 2)) or \
                        (prev_wev.get((ii, qg - 1, 4 * qg - 1)) if kt < 2 and qg else None) or \
                        (prev_wev.get((ii - 1, NQG - 1, 4 * NQG - 1)) if kt < 2 and qg == 0 else None)
                    if pe_prev is not None:
                        deps.append(pe_prev)
                    s_w = sc.step("tensor", f_w, deps=deps, note=f"{tag}_w{ii}_{qg}_{kt}")

                    def f_we(e, kt=kt, off=off, pb=pb):
                        if off >= 0:
                            return e.tensor_tensor(wtb[:, kt * 512:(kt + 1) * 512],
                                                   psc[:, pb * 512:(pb + 1) * 512],
                                                   lmf[:, off * 512:(off + 1) * 512],
                                                   ALU.mult)
                        return e.tensor_copy(wtb[:, kt * 512:(kt + 1) * 512],
                                             psc[:, pb * 512:(pb + 1) * 512])
                    deps = [s_w, s_mb]
                    if prev_ctx is not None:
                        deps.append(prev_ctx)
                    s_we = sc.step("vector", f_we, deps=deps,
                                   note=f"{tag}_we{ii}_{qg}_{kt}")
                    prev_wev[(ii, qg, kt)] = s_we
                    w_evs.append(s_we)

                def f_x(e, qg=qg, vo=vo):
                    n = 4 * qg + 4
                    for kt in range(n):
                        ins = e.matmul(pcx[0:64, 0:512],
                                       vsb[:, vo + kt * 64:vo + (kt + 1) * 64],
                                       wtb[:, kt * 512:(kt + 1) * 512],
                                       start=(kt == 0), stop=(kt == n - 1))
                    return ins
                deps = w_evs + [s_ldv]
                if prev_cst.get(qg % 2):
                    deps.append(prev_cst[qg % 2])
                s_x = sc.step("tensor", f_x, deps=deps, note=f"{tag}_x{ii}_{qg}")
                prev_ctx = s_x
                last = s_x

                csl = qg % 2

                def f_cc(e, csl=csl):
                    return e.tensor_copy(cst[0:64, csl * 512:(csl + 1) * 512],
                                         pcx[0:64, 0:512])
                s_cc = sc.step("vector", f_cc, deps=[s_x], note=f"{tag}_cc{ii}_{qg}")

                def f_st(e, dma, s=s, h=h, qg=qg, csl=csl):
                    dma(out=ctxfm_d[h // 2, (h % 2) * 64:(h % 2) * 64 + 64,
                                    s * S + qg * 512:s * S + (qg + 1) * 512],
                        in_=cst[0:64, csl * 512:(csl + 1) * 512])
                s_stc = sc.dma("gpsimd", f_st, 1, deps=[s_cc],
                               note=f"{tag}_st{ii}_{qg}")
                prev_cst[csl] = s_stc
            prev_rel[ii] = last
    return sc.lasts()



def build_l0ffn(cfg):
    p = PB(cfg)
    nc = p.nc
    d, F, E, ncd, nF, T = cfg.d, cfg.F, cfg.E, cfg.nc_d, cfg.nF, cfg.T
    x_d = nc.dram_tensor("x", [T, d], F32, kind="ExternalInput")
    wq_d = nc.dram_tensor("wq", [d, d], F32, kind="ExternalInput")
    wk_d = nc.dram_tensor("wk", [d, d], F32, kind="ExternalInput")
    wv_d = nc.dram_tensor("wv", [d, d], F32, kind="ExternalInput")
    wo_d = nc.dram_tensor("wo", [d, d], F32, kind="ExternalInput")
    bo_d = nc.dram_tensor("bo", [d], F32, kind="ExternalInput")
    wr_d = nc.dram_tensor("wr", [d, E], F32, kind="ExternalInput")
    we1_d = nc.dram_tensor("we1", [E, d, F], F32, kind="ExternalInput")
    be1_d = nc.dram_tensor("be1", [E, F], F32, kind="ExternalInput")
    we2_d = nc.dram_tensor("we2", [E, F, d], F32, kind="ExternalInput")
    be2_d = nc.dram_tensor("be2", [E, d], F32, kind="ExternalInput")
    g1_d = nc.dram_tensor("ln1g", [d], F32, kind="ExternalInput")
    b1_d = nc.dram_tensor("ln1b", [d], F32, kind="ExternalInput")
    g2_d = nc.dram_tensor("ln2g", [d], F32, kind="ExternalInput")
    b2_d = nc.dram_tensor("ln2b", [d], F32, kind="ExternalInput")
    y_d = nc.dram_tensor("y", [T, d], F32, kind="ExternalOutput")
    xin_fm = nc.dram_tensor("s_xin", [ncd, 128, T], BF16)
    qfm = nc.dram_tensor("s_q", [ncd, 128, T], BF16)
    kfm = nc.dram_tensor("s_k", [ncd, 128, T], BF16)
    vtm = nc.dram_tensor("s_v", [T, d], BF16)
    ctxfm = nc.dram_tensor("s_ctx", [ncd, 128, T], BF16)
    ao_tm = nc.dram_tensor("s_ao", [T, d], BF16)
    out1_tm = nc.dram_tensor("s_o1", [T, d], F32)
    out1_fm = nc.dram_tensor("s_o1f", [ncd, 128, T], BF16)
    gates = nc.dram_tensor("s_gt", [T, E], F32)
    h_fm = nc.dram_tensor("s_h", [nF, 128, T], BF16)
    acc_tm = nc.dram_tensor("s_acc", [T, d], BF16)

    p.start()
    with nc.Block() as block:
        with ExitStack() as es0:
            sb = mk_ln_sb(p, "ln0", es0)
            ph_ln(p, x_d, fm_d=xin_fm, eps=1e-5, tag="ln0", sb=sb)
        p.sched.emit_pending(block)
        ph_proj(p, xin_fm, wq_d, d, d, out_fm_d=qfm, tag="pq")
        p.sched.emit_pending(block)
        ph_proj(p, xin_fm, wk_d, d, d, out_fm_d=kfm, tag="pk")
        p.sched.emit_pending(block)
        ph_proj(p, xin_fm, wv_d, d, d, out_tm_d=vtm, tag="pv")
        p.sched.emit_pending(block)
        ph_attn(p, qfm, kfm, vtm, ctxfm, "std", 1.0 / math.sqrt(cfg.dh), "at")
        p.sched.emit_pending(block)
        ph_proj(p, ctxfm, wo_d, d, d, b_d=bo_d, out_tm_d=ao_tm, tag="po")
        p.sched.emit_pending(block)
        ph_resid(p, "sum_ln", x_d, ao_tm, tm_d=out1_tm, fm_d=out1_fm,
                 eps=1e-5, eps2=1e-6, g_d=g1_d, b_d=b1_d, adt=BF16, tag="r1")
        p.sched.emit_pending(block)
        ph_logits(p, out1_tm, wr_d, E, mode="l0top2", out_d=gates, tag="lg")
        p.sched.emit_pending(block)
        ph_moe(p, out1_fm, we1_d, be1_d, we2_d, be2_d, gates, h_fm,
               out1_tm, acc_tm, act="gelu", tag="mo")
        p.sched.emit_pending(block)
        ph_resid(p, "x_cln", x_d, acc_tm, y_d=y_d, coef=0.5, eps=1e-6,
                 g_d=g2_d, b_d=b2_d, adt=BF16, tag="r2")
        p.sched.emit_pending(block)
        p.sched.emit_final(block)
    return p


def mk_ln_sb(p, tag, es):
    t = lambda n, sh, dt: es.enter_context(p.nc.sbuf_tensor(tag + n, sh, dt))
    return {
        "xg": t("sxg", [128, 2 * 4 * p.cfg.d], F32),
        "yg": t("syg", [128, 2 * 4 * p.cfg.d], F32),
        "yb": t("syb", [128, 2 * 4 * p.cfg.d], BF16),
        "stt": t("sst", [128, 2 * 8 * 4], F32),
        "gb": t("sgb", [128, 2 * p.cfg.d], F32),
        "fms": t("sfm", [128, 2 * p.cfg.d], BF16),
        "ptp": es.enter_context(p.nc.psum_tensor(tag + "spt", [128, 1024], BF16)),
    }


def build_l0attn(cfg):
    p = PB(cfg)
    nc = p.nc
    d, ncd, T = cfg.d, cfg.nc_d, cfg.T
    x_d = nc.dram_tensor("x", [T, d], F32, kind="ExternalInput")
    names = ["wq", "bq", "wk", "bk", "wv", "bv", "wo", "bo", "gamma", "beta"]
    shp = {"w": [d, d], "b": [d], "g": [d]}
    hd = {}
    for n in names:
        sh = [d, d] if n.startswith("w") else [d]
        hd[n] = nc.dram_tensor(n, sh, F32, kind="ExternalInput")
    y_d = nc.dram_tensor("y", [T, d], F32, kind="ExternalOutput")
    xin_fm = nc.dram_tensor("s_xin", [ncd, 128, T], BF16)
    qfm = nc.dram_tensor("s_q", [ncd, 128, T], BF16)
    kfm = nc.dram_tensor("s_k", [ncd, 128, T], BF16)
    vtm = nc.dram_tensor("s_v", [T, d], BF16)
    ctxfm = nc.dram_tensor("s_ctx", [ncd, 128, T], BF16)
    ao_tm = nc.dram_tensor("s_ao", [T, d], BF16)
    out1_tm = nc.dram_tensor("s_o1", [T, d], F32)

    p.start()
    with nc.Block() as block:
        with ExitStack() as es0:
            sb = mk_ln_sb(p, "ln0", es0)
            ph_ln(p, x_d, fm_d=xin_fm, eps=1e-5, tag="ln0", sb=sb)
        p.sched.emit_pending(block)
        ph_proj(p, xin_fm, hd["wq"], d, d, b_d=hd["bq"], out_fm_d=qfm, tag="pq")
        p.sched.emit_pending(block)
        ph_proj(p, xin_fm, hd["wk"], d, d, b_d=hd["bk"], out_fm_d=kfm, tag="pk")
        p.sched.emit_pending(block)
        ph_proj(p, xin_fm, hd["wv"], d, d, b_d=hd["bv"], out_tm_d=vtm, tag="pv")
        p.sched.emit_pending(block)
        ph_attn(p, qfm, kfm, vtm, ctxfm, "resh", 1.0 / 8.0, "at")
        p.sched.emit_pending(block)
        ph_proj(p, ctxfm, hd["wo"], d, d, b_d=hd["bo"], out_tm_d=ao_tm, tag="po")
        p.sched.emit_pending(block)
        ph_resid(p, "sum_ln", x_d, ao_tm, tm_d=out1_tm, eps=1e-5, eps2=1e-5,
                 g_d=hd["gamma"], b_d=hd["beta"], adt=BF16, tag="r1")
        p.sched.emit_pending(block)
        ph_resid(p, "x_ca", x_d, out1_tm, y_d=y_d, coef=1.0, adt=F32, tag="r2")
        p.sched.emit_pending(block)
        p.sched.emit_final(block)
    return p


def build_l1log(cfg):
    p = PB(cfg)
    nc = p.nc
    d, E, T = cfg.d, cfg.E, cfg.T
    x_d = nc.dram_tensor("x", [T, d], F32, kind="ExternalInput")
    gw_d = nc.dram_tensor("gw", [E, d], F32, kind="ExternalInput")
    gb_d = nc.dram_tensor("gb", [E], F32, kind="ExternalInput")
    lo_d = nc.dram_tensor("lo", [T, E], F32, kind="ExternalOutput")
    p.start()
    with nc.Block() as block:
        ph_logits(p, x_d, gw_d, E, b_d=gb_d, ln_first=True, mode="store",
                  out_d=lo_d, w_is_ExD=True, eps=1e-5, tag="lg")
        p.sched.emit_pending(block)
        p.sched.emit_final(block)
    return p


def build_l1route(cfg):
    p = PB(cfg)
    nc = p.nc
    E, Tt = cfg.E, cfg.Ttot
    lo_d = nc.dram_tensor("lo", [Tt, E], F32, kind="ExternalInput")
    ga_d = nc.dram_tensor("ga", [Tt, E], F32, kind="ExternalOutput")
    p.start()
    with nc.Block() as block:
        ph_route_l1(p, lo_d, ga_d, Tt, E, cap=cfg.cap, tag="rt")
        p.sched.emit_pending(block)
        p.sched.emit_final(block)
    return p


def build_l1ffn(cfg):
    p = PB(cfg)
    nc = p.nc
    d, F, E, ncd, nF, T = cfg.d, cfg.F, cfg.E, cfg.nc_d, cfg.nF, cfg.T
    x_d = nc.dram_tensor("x", [T, d], F32, kind="ExternalInput")
    ga_d = nc.dram_tensor("ga", [T, E], F32, kind="ExternalInput")
    w1_d = nc.dram_tensor("w1", [E, d, F], F32, kind="ExternalInput")
    b1_d = nc.dram_tensor("b1", [E, F], F32, kind="ExternalInput")
    w2_d = nc.dram_tensor("w2", [E, F, d], F32, kind="ExternalInput")
    b2_d = nc.dram_tensor("b2", [E, d], F32, kind="ExternalInput")
    y_d = nc.dram_tensor("y", [T, d], F32, kind="ExternalOutput")
    xin_fm = nc.dram_tensor("s_xin", [ncd, 128, T], BF16)
    h_fm = nc.dram_tensor("s_h", [nF, 128, T], BF16)
    acc_tm = nc.dram_tensor("s_acc", [T, d], BF16)
    p.start()
    with nc.Block() as block:
        with ExitStack() as es0:
            sb = mk_ln_sb(p, "ln0", es0)
            ph_ln(p, x_d, fm_d=xin_fm, eps=1e-5, tag="ln0", sb=sb)
        p.sched.emit_pending(block)
        ph_moe(p, xin_fm, w1_d, b1_d, w2_d, b2_d, ga_d, h_fm, None, acc_tm,
               act="relu", tag="mo")
        p.sched.emit_pending(block)
        ph_resid(p, "x_ca", x_d, acc_tm, y_d=y_d, coef=0.5, adt=BF16, tag="r2")
        p.sched.emit_pending(block)
        p.sched.emit_final(block)
    return p


def build_l1attn(cfg):
    p = PB(cfg)
    nc = p.nc
    d, ncd, T = cfg.d, cfg.nc_d, cfg.T
    x_d = nc.dram_tensor("x", [T, d], F32, kind="ExternalInput")
    hd = {}
    for n in ["wq", "bq", "wk", "bk", "wv", "bv", "wo", "bo", "lng", "lnb"]:
        sh = [d, d] if n.startswith("w") else [d]
        hd[n] = nc.dram_tensor(n, sh, F32, kind="ExternalInput")
    y_d = nc.dram_tensor("y", [T, d], F32, kind="ExternalOutput")
    xin_fm = nc.dram_tensor("s_xin", [ncd, 128, T], BF16)
    qfm = nc.dram_tensor("s_q", [ncd, 128, T], BF16)
    kfm = nc.dram_tensor("s_k", [ncd, 128, T], BF16)
    vtm = nc.dram_tensor("s_v", [T, d], BF16)
    ctxf = nc.dram_tensor("s_ctx", [ncd, 128, T], F32)
    o_tm = nc.dram_tensor("s_o", [T, d], F32)
    on_fm = nc.dram_tensor("s_on", [ncd, 128, T], BF16)
    pr_tm = nc.dram_tensor("s_pr", [T, d], BF16)
    p.start()
    with nc.Block() as block:
        with ExitStack() as es0:
            sb = mk_ln_sb(p, "ln0", es0)
            ph_ln(p, x_d, fm_d=xin_fm, eps=1e-5, tag="ln0", sb=sb)
        p.sched.emit_pending(block)
        ph_proj(p, xin_fm, hd["wq"], d, d, b_d=hd["bq"], out_fm_d=qfm,
                act="elu1", w_transposed=True, tag="pq")
        p.sched.emit_pending(block)
        ph_proj(p, xin_fm, hd["wk"], d, d, b_d=hd["bk"], out_fm_d=kfm,
                act="elu1", w_transposed=True, tag="pk")
        p.sched.emit_pending(block)
        ph_proj(p, xin_fm, hd["wv"], d, d, b_d=hd["bv"], out_tm_d=vtm,
                w_transposed=True, tag="pv")
        p.sched.emit_pending(block)
        ph_lattn(p, qfm, kfm, vtm, ctxf, tag="la")
        p.sched.emit_pending(block)
        ph_fm2tm(p, ctxf, o_tm, dt=F32, tag="f2")
        p.sched.emit_pending(block)
        with ExitStack() as es1:
            sb2 = mk_ln_sb(p, "ln1", es1)
            ph_ln(p, o_tm, fm_d=on_fm, eps=1e-5, g_d=hd["lng"], b_d=hd["lnb"],
                  tag="ln1", sb=sb2)
        p.sched.emit_pending(block)
        ph_proj(p, on_fm, hd["wo"], d, d, b_d=hd["bo"], out_tm_d=pr_tm,
                w_transposed=True, tag="po")
        p.sched.emit_pending(block)
        ph_resid(p, "x_ca", x_d, pr_tm, y_d=y_d, coef=1.0, adt=BF16, tag="r2")
        p.sched.emit_pending(block)
        p.sched.emit_final(block)
    return p



_PROGS = {}


def _get_prog(name, cfg):
    if name not in _PROGS:
        _PROGS[name] = {
            "l0ffn": build_l0ffn, "l0attn": build_l0attn,
            "l1log": build_l1log, "l1route": build_l1route,
            "l1ffn": build_l1ffn, "l1attn": build_l1attn,
            "fused": build_fused,
        }[name](cfg)
    return _PROGS[name]


def _run(name, cfg, per_core_maps):
    p = _get_prog(name, cfg)
    for m in per_core_maps:
        for cn, cv in p.extra_inputs.items():
            m[cn] = cv
    res = run_bass_kernel_spmd(p.nc, per_core_maps, core_ids=list(range(NCORES)))
    return res.results


def kernel(**inputs):
    cfg = Cfg()
    T = cfg.T
    x0 = np.ascontiguousarray(np.asarray(inputs["L0_attn_x"], dtype=np.float32))
    B, S, d = x0.shape
    xs = [np.ascontiguousarray(x0[2 * c:2 * c + 2].reshape(T, d))
          for c in range(NCORES)]

    def f32(n):
        return np.ascontiguousarray(np.asarray(inputs[n], dtype=np.float32))

    def l0ffn_maps(xc):
        base = {"wq": f32("L0_ffn_Wq"), "wk": f32("L0_ffn_Wk"),
                "wv": f32("L0_ffn_Wv"), "wo": f32("L0_ffn_Wo"),
                "bo": f32("L0_ffn_bo"), "wr": f32("L0_ffn_Wr"),
                "we1": f32("L0_ffn_We1"), "be1": f32("L0_ffn_be1"),
                "we2": f32("L0_ffn_We2"), "be2": f32("L0_ffn_be2"),
                "ln1g": f32("L0_ffn_ln1_g"), "ln1b": f32("L0_ffn_ln1_b"),
                "ln2g": f32("L0_ffn_ln2_g"), "ln2b": f32("L0_ffn_ln2_b")}
        return [dict(base, x=xc[c]) for c in range(NCORES)]

    def l0attn_maps(xc):
        base = {"wq": f32("L0_attn_wq"), "bq": f32("L0_attn_bq"),
                "wk": f32("L0_attn_wk"), "bk": f32("L0_attn_bk"),
                "wv": f32("L0_attn_wv"), "bv": f32("L0_attn_bv"),
                "wo": f32("L0_attn_wo"), "bo": f32("L0_attn_bo"),
                "gamma": f32("L0_attn_gamma"), "beta": f32("L0_attn_beta")}
        return [dict(base, x=xc[c]) for c in range(NCORES)]

    def l1log_maps(xc):
        base = {"gw": f32("L1_ffn_gate_w"), "gb": f32("L1_ffn_gate_b")}
        return [dict(base, x=xc[c]) for c in range(NCORES)]

    def l1ffn_maps(xc, gc):
        base = {"w1": f32("L1_ffn_w1"), "b1": f32("L1_ffn_b1"),
                "w2": f32("L1_ffn_w2"), "b2": f32("L1_ffn_b2")}
        return [dict(base, x=xc[c], ga=gc[c]) for c in range(NCORES)]

    def l1attn_maps(xc):
        base = {"wq": f32("L1_attn_Wq"), "bq": f32("L1_attn_bq"),
                "wk": f32("L1_attn_Wk"), "bk": f32("L1_attn_bk"),
                "wv": f32("L1_attn_Wv"), "bv": f32("L1_attn_bv"),
                "wo": f32("L1_attn_Wo"), "bo": f32("L1_attn_bo"),
                "lng": f32("L1_attn_ln_g"), "lnb": f32("L1_attn_ln_b")}
        return [dict(base, x=xc[c]) for c in range(NCORES)]

    def l1ffn_step(xc):
        lg = _run("l1log", cfg, l1log_maps(xc))
        lg_full = np.ascontiguousarray(
            np.concatenate([lg[c]["lo"] for c in range(NCORES)], axis=0))
        ga = _run("l1route", cfg, [{"lo": lg_full} for _ in range(NCORES)])
        ga_full = ga[0]["ga"]
        gc = [np.ascontiguousarray(ga_full[c * T:(c + 1) * T])
              for c in range(NCORES)]
        r = _run("l1ffn", cfg, l1ffn_maps(xc, gc))
        return [r[c]["y"] for c in range(NCORES)]

    r = _run("l0ffn", cfg, l0ffn_maps(xs))
    xs = [r[c]["y"] for c in range(NCORES)]
    r = _run("l0attn", cfg, l0attn_maps(xs))
    xs = [r[c]["y"] for c in range(NCORES)]
    r = _run("l0ffn", cfg, l0ffn_maps(xs))
    xs = [r[c]["y"] for c in range(NCORES)]
    xs = l1ffn_step(xs)
    r = _run("l1attn", cfg, l1attn_maps(xs))
    xs = [r[c]["y"] for c in range(NCORES)]
    xs = l1ffn_step(xs)

    out = np.concatenate([x.reshape(2, S, d) for x in xs], axis=0)
    return out.astype(np.float32)



class LP:

    GD = 6
    LV = 3
    LA = 2
    LT = 2

    def __init__(self, p):
        self.p = p
        self.nc = p.nc
        self.s = {}
        for i in range(self.GD):
            self.s[f"gd{i}"] = [p.es.enter_context(p.nc.semaphore(f"gd{i}")), 0]
        for i in range(self.LV):
            self.s[f"lv{i}"] = [p.es.enter_context(p.nc.semaphore(f"lv{i}")), 0]
        for i in range(self.LA):
            self.s[f"la{i}"] = [p.es.enter_context(p.nc.semaphore(f"la{i}")), 0]
        for i in range(self.LT):
            self.s[f"lt{i}"] = [p.es.enter_context(p.nc.semaphore(f"lt{i}")), 0]
        for i in range(2):
            self.s[f"ls{i}"] = [p.es.enter_context(p.nc.semaphore(f"ls{i}")), 0]
            self.s[f"gp{i}"] = [p.es.enter_context(p.nc.semaphore(f"gp{i}")), 0]
        self.s["ph"] = [p.es.enter_context(p.nc.semaphore("ph")), 0]
        self.nph = 0

    def h(self, n):
        return self.s[n][0]

    def base(self, n):
        return self.s[n][1]

    def bump(self, n, total):
        self.s[n][1] += total
        return self.s[n][1]

    def phase_end(self, closer="vector"):
        eng = getattr(self.nc, closer)
        for n, (hd, v) in self.s.items():
            if n != "ph" and v > 0:
                eng.wait_ge(hd, v)
        sc = self.p.sched
        for n in sorted(sc.sems):
            hd, v = sc.sems[n]
            if v > 0:
                eng.wait_ge(hd, v)
        self.nph += 1
        eng.nop().then_inc(self.s["ph"][0], 1)
        self.s["ph"][1] = self.nph

    def phase_begin(self, engines=("sync", "gpsimd", "tensor", "vector", "scalar")):
        for en in engines:
            getattr(self.nc, en).wait_ge(self.s["ph"][0], self.nph)

    def sched_sync(self):
        sc = self.p.sched
        for n, (hd, v) in self.s.items():
            if v > 0:
                if n not in sc.sems:
                    sc.sems[n] = [hd, 0]
                if sc.sems[n][1] != v:
                    sc.sems[n][1] = v
                    st = Step()
                    st.eng, st.sem, st.val = "gpsimd", n, v
                    st.idx = len(sc.steps)
                    st.fn, st.deps, st.n_dma, st.note = None, {}, None, "lp"
                    sc.steps.append(st)
                    sc.last_by_sem[n] = st



def prebump(nc, lp, pairs, eng="vector"):
    e = getattr(nc, eng)
    out = {}
    for name, delta in pairs:
        rem = delta
        while rem > 0:
            step = min(rem, 7)
            e.sem_inc(lp.h(name), step)
            rem -= step
        if delta > 0:
            out[name] = lp.bump(name, delta)
        else:
            out[name] = lp.base(name)
    return out


def lp_ln(p, lp, rc, x_d, fm_d=None, tm_d=None, eps=1e-5, g_d=None, b_d=None,
          tag="ln"):
    nc = p.nc
    cfg = p.cfg
    d, ncd = cfg.d, cfg.nc_d
    nt = cfg.nt
    grp = 4
    assert nt % (2 * grp) == 0
    niter = nt // (2 * grp)
    affine = g_d is not None
    fm = fm_d is not None
    tm = tm_d is not None
    es = ExitStack()
    t_ = lambda n, sh, dt: es.enter_context(nc.sbuf_tensor(tag + n, sh, dt))
    xg = t_("_x", [128, 2 * grp * d], F32)
    yg = t_("_y", [128, 2 * grp * d], F32)
    stt = t_("_st", [128, 2 * 8 * grp], F32)
    yb = t_("_yb", [128, 2 * grp * d], BF16) if fm else None
    gb = t_("_gb", [128, 2 * d], F32) if affine else None
    fms = t_("_fm", [128, 2 * grp * d], BF16) if fm else None
    ptp = es.enter_context(nc.psum_tensor(tag + "_pt", [128, 1024], BF16)) \
        if fm else None

    GD, GS, LV, LA, LT, GP = "gd0", "gd1", "lv0", "la0", "lt0", "gp0"
    vh = 6 + (2 if affine else 0) + ((1 + grp) if fm else 0)
    ah = 2
    th = grp if fm else 0
    gh = (grp if tm else 0) * 16 + (grp if fm else 0) * 16
    ph_ = 1 if gh else 0
    D_LV, D_LA, D_LT, D_GD, D_GS, D_GP = 2 * vh, 2 * ah, 2 * th, 2 * grp * 16, \
        2 * gh, 2 * ph_
    if affine:
        nc.sync.dma_start(out=gb[:, 0:d], in_=bcast_row(g_d, 128)) \
            .then_inc(lp.h("ls0"), 16)
        nc.sync.dma_start(out=gb[:, d:2 * d], in_=bcast_row(b_d, 128)) \
            .then_inc(lp.h("ls0"), 16)
        bLS = lp.bump("ls0", 32)

    bb = prebump(nc, lp, [(LV, D_LV), (LA, D_LA), (LT, D_LT), (GP, D_GP)])
    bLV, bLA, bLT, bGP = bb[LV], bb[LA], bb[LT], bb[GP]
    bGD, bGS = lp.base(GD), lp.base(GS)
    if affine:
        nc.vector.wait_ge(lp.h("ls0"), bLS)

    lv_off = lambda hf, k: hf * vh + k
    la_off = lambda hf, k: hf * ah + k
    lt_off = lambda hf, k: hf * th + k
    gd_off = lambda hf: (hf + 1) * grp * 16
    gs_off = lambda hf, k: hf * gh + 16 * k
    o_y = 6
    o_g = 7
    o_b = 8
    o_cast = 6 + (2 if affine else 0) + 1
    o_cp0 = o_cast

    with fori2(p, niter) as i:
        for hf in range(2):
            o = hf * grp
            so = hf * 8 * grp
            rc.wait("gpsimd", lp.h(LV), i, D_LV, bLV - D_LV + lv_off(hf, o_y))
            if gh:
                rc.wait("gpsimd", lp.h(GP), i, D_GP, bGP - D_GP + hf * ph_ + ph_)
            for j in range(grp):
                row = rc.off("gpsimd", i, 2 * grp * 128 * d,
                             (hf * grp + j) * 128 * d, (p.cfg.T - 128) * d)
                dyn_dma(nc, "gpsimd", xg[:, (o + j) * d:(o + j + 1) * d],
                        dyn_ap(x_d, row, [[d, 128], [1, d]]), lp.h(GD))
            rc.wait("scalar", lp.h(GD), i, D_GD, bGD + gd_off(hf))
            rc.wait("scalar", lp.h(LV), i, D_LV, bLV - D_LV + lv_off(hf, o_cast if fm else o_y))
            if gh:
                rc.wait("scalar", lp.h(GP), i, D_GP, bGP - D_GP + hf * ph_ + ph_)
            for j in range(grp):
                ins = nc.scalar.activation(
                    yg[:, (o + j) * d:(o + j + 1) * d],
                    xg[:, (o + j) * d:(o + j + 1) * d], AF.Square,
                    accum_out=stt[:, so + grp + j:so + grp + j + 1])
            ins.then_inc(lp.h(LA), 1)
            rc.wait("vector", lp.h(GD), i, D_GD, bGD + gd_off(hf))
            for j in range(grp):
                ins = nc.vector.tensor_reduce(
                    stt[:, so + j:so + j + 1],
                    xg[:, (o + j) * d:(o + j + 1) * d], axis=AX.X, op=ALU.add)
            ins.then_inc(lp.h(LV), 1)
            rc.wait("vector", lp.h(LV), i, D_LV, bLV + lv_off(hf, 1))
            nc.vector.tensor_scalar(stt[:, so:so + grp], stt[:, so:so + grp],
                                    1.0 / d, None, ALU.mult).then_inc(lp.h(LV), 1)
            rc.wait("vector", lp.h(LA), i, D_LA, bLA + la_off(hf, 1))
            rc.wait("vector", lp.h(LV), i, D_LV, bLV + lv_off(hf, 2))
            ins = nc.vector.tensor_scalar(
                stt[:, so + 3 * grp:so + 4 * grp], stt[:, so + grp:so + 2 * grp],
                1.0 / d, eps, ALU.mult, ALU.add)
            nc.vector.tensor_tensor(
                stt[:, so + 2 * grp:so + 3 * grp], stt[:, so:so + grp],
                stt[:, so:so + grp], ALU.mult).then_inc(lp.h(LV), 1)
            rc.wait("vector", lp.h(LV), i, D_LV, bLV + lv_off(hf, 3))
            nc.vector.tensor_sub(
                stt[:, so + 3 * grp:so + 4 * grp],
                stt[:, so + 3 * grp:so + 4 * grp],
                stt[:, so + 2 * grp:so + 3 * grp]).then_inc(lp.h(LV), 1)
            rc.wait("scalar", lp.h(LV), i, D_LV, bLV + lv_off(hf, 4))
            nc.scalar.sqrt(stt[:, so + 4 * grp:so + 5 * grp],
                           stt[:, so + 3 * grp:so + 4 * grp]).then_inc(lp.h(LA), 1)
            rc.wait("vector", lp.h(LA), i, D_LA, bLA + la_off(hf, 2))
            nc.vector.reciprocal(stt[:, so + 4 * grp:so + 5 * grp],
                                 stt[:, so + 4 * grp:so + 5 * grp]).then_inc(lp.h(LV), 1)
            rc.wait("vector", lp.h(LV), i, D_LV, bLV + lv_off(hf, 5))
            for j in range(grp):
                ins = nc.vector.tensor_scalar(
                    yg[:, (o + j) * d:(o + j + 1) * d],
                    xg[:, (o + j) * d:(o + j + 1) * d],
                    stt[:, so + j:so + j + 1],
                    stt[:, so + 4 * grp + j:so + 4 * grp + j + 1],
                    ALU.subtract, ALU.mult)
            ins.then_inc(lp.h(LV), 1)
            prev_lv = o_y
            if affine:
                rc.wait("vector", lp.h(LV), i, D_LV, bLV + lv_off(hf, o_y))
                for j in range(grp):
                    ins = nc.vector.tensor_tensor(
                        yg[:, (o + j) * d:(o + j + 1) * d],
                        yg[:, (o + j) * d:(o + j + 1) * d], gb[:, 0:d], ALU.mult)
                ins.then_inc(lp.h(LV), 1)
                rc.wait("vector", lp.h(LV), i, D_LV, bLV + lv_off(hf, o_g))
                for j in range(grp):
                    ins = nc.vector.tensor_tensor(
                        yg[:, (o + j) * d:(o + j + 1) * d],
                        yg[:, (o + j) * d:(o + j + 1) * d], gb[:, d:2 * d], ALU.add)
                ins.then_inc(lp.h(LV), 1)
                prev_lv = o_b
            if fm:
                rc.wait("vector", lp.h(LV), i, D_LV, bLV + lv_off(hf, prev_lv))
                rc.wait("vector", lp.h(LT), i, D_LT, bLT - D_LT + lt_off(hf, th))
                for j in range(grp):
                    ins = nc.vector.tensor_copy(
                        yb[:, (o + j) * d:(o + j + 1) * d],
                        yg[:, (o + j) * d:(o + j + 1) * d])
                ins.then_inc(lp.h(LV), 1)
                for j in range(grp):
                    rc.wait("tensor", lp.h(LV), i, D_LV, bLV + lv_off(hf, o_cast))
                    if j > 0:
                        rc.wait("tensor", lp.h(LV), i, D_LV, bLV + lv_off(hf, o_cp0 + j))
                    elif hf == 1:
                        rc.wait("tensor", lp.h(LV), i, D_LV, bLV + lv_off(0, o_cp0 + grp))
                    else:
                        rc.wait("tensor", lp.h(LV), i, D_LV, bLV - D_LV + lv_off(1, o_cp0 + grp))
                    for c in range(ncd):
                        ins = nc.tensor.transpose(
                            ptp[:, c * 128:(c + 1) * 128],
                            yb[:, (o + j) * d + c * 128:(o + j) * d + (c + 1) * 128],
                            p.identb[:, :])
                    ins.then_inc(lp.h(LT), 1)
                    rc.wait("vector", lp.h(LT), i, D_LT, bLT + lt_off(hf, j + 1))
                    if gh:
                        rc.wait("vector", lp.h(GP), i, D_GP, bGP - D_GP + hf * ph_ + ph_)
                    nc.vector.tensor_copy(fms[:, (o + j) * d:(o + j + 1) * d],
                                          ptp[:, 0:d]).then_inc(lp.h(LV), 1)
            nst = 0
            if tm:
                rc.wait("gpsimd", lp.h(LV), i, D_LV, bLV + lv_off(hf, prev_lv))
                for j in range(grp):
                    row = rc.off("gpsimd", i, 2 * grp * 128 * d,
                                 (hf * grp + j) * 128 * d, (p.cfg.T - 128) * d)
                    dyn_dma(nc, "gpsimd", dyn_ap(tm_d, row, [[d, 128], [1, d]]),
                            yg[:, (o + j) * d:(o + j + 1) * d], lp.h(GS))
                    nst += 1
            if fm:
                for j in range(grp):
                    rc.wait("gpsimd", lp.h(LV), i, D_LV, bLV + lv_off(hf, o_cp0 + j + 1))
                    dyn_dma(nc, "gpsimd",
                            dyn_ap(fm_d,
                                   rc.off("gpsimd", i, 2 * grp * 128,
                                          hf * grp * 128 + j * 128,
                                          p.cfg.T - 128),
                                   [[p.cfg.T, 128], [128 * p.cfg.T, ncd],
                                    [1, 128]]),
                            fms[:, (o + j) * d:(o + j + 1) * d]
                            .rearrange("p (c x) -> p c x", c=ncd), lp.h(GS))
                    nst += 1
            if nst:
                rc.wait("gpsimd", lp.h(GS), i, D_GS, bGS + gs_off(hf, nst))
                nc.gpsimd.sem_inc(lp.h(GP), 1)
    lp.bump(GD, niter * D_GD)
    lp.bump(GS, niter * D_GS)
    lp.bump(LV, niter * D_LV)
    lp.bump(LA, niter * D_LA)
    lp.bump(LT, niter * D_LT)
    lp.bump(GP, niter * D_GP)
    es.close()


def finalize_lp(p, lp):
    nc = p.nc
    for n, (h, v) in sorted(lp.s.items()):
        if v > 0:
            nc.gpsimd.wait_ge(h, v)
    for n in sorted(p.sched.sems):
        h, v = p.sched.sems[n]
        if v > 0 and n not in lp.s:
            nc.gpsimd.wait_ge(h, v)



def lp_w_setup(p, lp, nc, w_d, din, dout, wbf, wst, wst_nb, wsl):
    nci = din // 128
    LS, LV = "ls1", "lv1"
    bLS = lp.base(LS)
    nwv = 0
    for k in range(nci):
        if k >= 1:
            nc.sync.wait_ge(lp.h(LV), lp.base(LV) + nwv)
        nc.sync.dma_start(out=wst[:, (k % wst_nb) * wsl:(k % wst_nb) * wsl + dout],
                          in_=w_d[k * 128:(k + 1) * 128, 0:dout]) \
            .then_inc(lp.h(LS), 16)
        nc.vector.wait_ge(lp.h(LS), bLS + 16 * (k + 1))
        nc.vector.tensor_copy(wbf[:, k * dout:(k + 1) * dout],
                              wst[:, (k % wst_nb) * wsl:(k % wst_nb) * wsl + dout]) \
            .then_inc(lp.h(LV), 1)
        nwv += 1
    lp.bump(LS, 16 * nci)
    return lp.bump(LV, nwv)


def lp_proj(p, lp, rc, xfm_d, w_d, din, dout, b_d=None, out_fm_d=None,
            out_tm_d=None, act=None, tag="pj"):
    nc = p.nc
    cfg = p.cfg
    nci, nco = din // 128, dout // 128
    T = cfg.T
    es = ExitStack()
    t_ = lambda n, sh, dt: es.enter_context(nc.sbuf_tensor(tag + n, sh, dt))
    wbf = t_("_w", [128, nci * dout], BF16)
    wsl = max(din, dout)
    wst_nb = 2
    wst = t_("_ws", [128, wst_nb * wsl], F32)
    pj = es.enter_context(nc.psum_tensor(tag + "_pj", [128, 8 * 512], F32))
    LS, GD, GS, LV, LA, PE, GP = "ls1", "gd2", "gd3", "lv1", "la1", "lt1", "gp1"

    bW = lp_w_setup(p, lp, nc, w_d, din, dout, wbf, wst, wst_nb, wsl)
    nbs = 0
    bcol = t_("_bc", [128, max(nco, 8)], F32)
    bro = t_("_br", [128, dout], F32) if out_tm_d is not None else None
    if b_d is not None:
        if out_fm_d is not None:
            for m in range(nco):
                nc.sync.dma_start(out=bcol[:, m:m + 1],
                                  in_=b_d[m * 128:(m + 1) * 128].unsqueeze(1)) \
                    .then_inc(lp.h(LS), 16)
                nbs += 16
        if out_tm_d is not None:
            nc.sync.dma_start(out=bro[:, 0:dout], in_=bcast_row(b_d, 128)) \
                .then_inc(lp.h(LS), 16)
            nbs += 16
    bB = lp.bump(LS, nbs)
    AFm = {None: AF.Identity, "gelu": AF.Gelu, "relu": AF.Relu}[act]

    if out_fm_d is not None:
        GPH = T // 512
        HPB = 2
        niter = GPH // HPB
        NSG = nco // 4
        SGB = HPB * NSG
        og = t_("_o", [128, 2 * nco * 512], BF16) if nco <= 8 else \
            t_("_o", [128, 2 * 32 // 2 * 512], BF16)
        nog = 2 * nco if nco <= 8 else 32
        og = t_("_og", [128, nog * 512], BF16)
        xg = t_("_x", [128, HPB * nci * 512], BF16)
        D_GD, D_PE, D_GS, D_GP = HPB * nci * 16, SGB, HPB * nco * 16, HPB
        D_LA = SGB
        bb = prebump(nc, lp, [(PE, D_PE), (LA, D_LA), (GP, D_GP)])
        bPE, bLA, bGP = bb[PE], bb[LA], bb[GP]
        bGD, bGS = lp.base(GD), lp.base(GS)
        nc.scalar.wait_ge(lp.h(LS), max(bB, 1))
        nc.scalar.wait_ge(lp.h(LV), bW)
        nc.tensor.wait_ge(lp.h(LV), bW)
        with fori2(p, niter) as i:
            for hf in range(HPB):
                rc.wait("gpsimd", lp.h(PE), i, D_PE, bPE - D_PE + hf * NSG + NSG)
                for k in range(nci):
                    dyn_dma(nc, "gpsimd",
                            xg[:, (hf * nci + k) * 512:(hf * nci + k + 1) * 512],
                            dyn_ap(xfm_d,
                                   rc.off("gpsimd", i, HPB * 512,
                                          hf * 512 + k * 128 * T,
                                          (nci - 1) * 128 * T + T - 512),
                                   [[T, 128], [1, 512]]),
                            lp.h(GD))
                for sg in range(NSG):
                    s_in_body = hf * NSG + sg
                    pbank = (s_in_body % 2) * 4
                    (rc.wait("tensor", lp.h(LA), i, D_LA, bLA - D_LA + s_in_body - 2 + D_LA + 1) if s_in_body - 2 < 0 else rc.wait("tensor", lp.h(LA), i, D_LA, bLA + s_in_body - 2 + 1))
                    rc.wait("tensor", lp.h(GD), i, D_GD, bGD + (hf + 1) * nci * 16)
                    for mj in range(4):
                        m = sg * 4 + mj
                        for k in range(nci):
                            ins = nc.tensor.matmul(
                                pj[:, (pbank + mj) * 512:(pbank + mj + 1) * 512],
                                wbf[:, k * dout + m * 128:k * dout + (m + 1) * 128],
                                xg[:, (hf * nci + k) * 512:(hf * nci + k + 1) * 512],
                                start=(k == 0), stop=(k == nci - 1))
                    ins.then_inc(lp.h(PE), 1)
                    rc.wait("scalar", lp.h(PE), i, D_PE, bPE + s_in_body + 1)
                    rc.wait("scalar", lp.h(GP), i, D_GP, bGP - D_GP + hf + 1)
                    for mj in range(4):
                        m = sg * 4 + mj
                        osl = (hf * nco + m) if nco <= 8 else m
                        kw = {}
                        if b_d is not None:
                            kw["bias"] = bcol[:, m:m + 1]
                        ins = nc.scalar.activation(
                            og[:, osl * 512:(osl + 1) * 512],
                            pj[:, (pbank + mj) * 512:(pbank + mj + 1) * 512],
                            AFm, **kw)
                    ins.then_inc(lp.h(LA), 1)
                    rc.wait("gpsimd", lp.h(LA), i, D_LA, bLA + s_in_body + 1)
                    for mj in range(4):
                        m = sg * 4 + mj
                        osl = (hf * nco + m) if nco <= 8 else m
                        dyn_dma(nc, "gpsimd",
                                dyn_ap(out_fm_d,
                                       rc.off("gpsimd", i, HPB * 512,
                                              hf * 512 + m * 128 * T,
                                              (nco - 1) * 128 * T + T - 512),
                                       [[T, 128], [1, 512]]),
                                og[:, osl * 512:(osl + 1) * 512], lp.h(GS))
                rc.wait("gpsimd", lp.h(GS), i, D_GS, bGS + (hf + 1) * nco * 16)
                nc.gpsimd.sem_inc(lp.h(GP), 1)
        lp.bump(GD, niter * D_GD)
        lp.bump(PE, niter * D_PE)
        lp.bump(LA, niter * D_LA)
        lp.bump(GS, niter * D_GS)
        lp.bump(GP, niter * D_GP)

    if out_tm_d is not None:
        og2 = t_("_o2", [128, 2 * dout], BF16)
        xg2 = t_("_x2", [128, 2 * nci * 128], BF16)
        ncg = dout // 512
        niter = (T // 128) // 2
        nxd = max(nci // 8, 1)
        D_GD, D_PE, D_GS, D_GP, D_LV = 2 * nxd * 16, 2, 2 * 16, 2, 2
        bb = prebump(nc, lp, [(PE, D_PE), (LV, D_LV), (GP, D_GP)])
        bPE, bLV, bGP = bb[PE], bb[LV], bb[GP]
        bGD2, bGS2 = lp.base(GD), lp.base(GS)
        nc.vector.wait_ge(lp.h(LS), max(bB, 1))
        nc.vector.wait_ge(lp.h(LV), bW)
        nc.tensor.wait_ge(lp.h(LV), bW)
        with fori2(p, niter) as i:
            for hf in range(2):
                rc.wait("gpsimd", lp.h(PE), i, D_PE, bPE - D_PE + hf + 1)
                q8 = min(nci, 8)
                for dk in range(nxd):
                    dyn_dma(nc, "gpsimd",
                            xg2[:, (hf * nci + dk * q8) * 128:
                                (hf * nci + dk * q8 + q8) * 128]
                            .rearrange("p (c x) -> p c x", c=q8),
                            dyn_ap(xfm_d,
                                   rc.off("gpsimd", i, 256,
                                          hf * 128 + dk * q8 * 128 * T,
                                          (nci - 1) * 128 * T + T - 128),
                                   [[T, 128], [128 * T, q8], [1, 128]]),
                            lp.h(GD))
                rc.wait("tensor", lp.h(GD), i, D_GD, bGD2 + (hf + 1) * nxd * 16)
                rc.wait("tensor", lp.h(LV), i, D_LV, bLV - D_LV + hf + 1)
                for cg in range(ncg):
                    for k in range(nci):
                        ins = nc.tensor.matmul(
                            pj[:, (hf * ncg + cg) * 512:(hf * ncg + cg + 1) * 512],
                            xg2[:, (hf * nci + k) * 128:(hf * nci + k + 1) * 128],
                            wbf[:, k * dout + cg * 512:k * dout + (cg + 1) * 512],
                            start=(k == 0), stop=(k == nci - 1))
                ins.then_inc(lp.h(PE), 1)
                rc.wait("vector", lp.h(PE), i, D_PE, bPE + hf + 1)
                rc.wait("vector", lp.h(GP), i, D_GP, bGP - D_GP + hf + 1)
                if b_d is not None:
                    nc.vector.tensor_tensor(og2[:, hf * dout:(hf + 1) * dout],
                                            pj[:, hf * dout:(hf + 1) * dout],
                                            bro[:, 0:dout], ALU.add) \
                        .then_inc(lp.h(LV), 1)
                else:
                    nc.vector.tensor_copy(og2[:, hf * dout:(hf + 1) * dout],
                                          pj[:, hf * dout:(hf + 1) * dout]) \
                        .then_inc(lp.h(LV), 1)
                rc.wait("gpsimd", lp.h(LV), i, D_LV, bLV + hf + 1)
                dyn_dma(nc, "gpsimd",
                        dyn_ap(out_tm_d,
                               rc.off("gpsimd", i, 256 * dout, hf * 128 * dout,
                                      (T - 128) * dout),
                               [[dout, 128], [1, dout]]),
                        og2[:, hf * dout:(hf + 1) * dout], lp.h(GS))
                rc.wait("gpsimd", lp.h(GS), i, D_GS, bGS2 + (hf + 1) * 16)
                nc.gpsimd.sem_inc(lp.h(GP), 1)
        lp.bump(GD, niter * D_GD)
        lp.bump(PE, niter * D_PE)
        lp.bump(LV, niter * D_LV)
        lp.bump(GS, niter * D_GS)
        lp.bump(GP, niter * D_GP)
    es.close()



from contextlib import contextmanager


@contextmanager
def fori2(p, n, level=0):
    nc = p.nc
    if not hasattr(p, "_fori_regs"):
        p._fori_regs = {}
        p._fori_ctr = 0
    if n == 1:
        p._fori_ctr += 1
        yield 0
        return
    if level not in p._fori_regs:
        p._fori_regs[level] = nc.alloc_registers(f"fori2_l{level}")
    registers = p._fori_regs[level]
    name = f"f2_{p._fori_ctr}"
    p._fori_ctr += 1
    loop_start, loop_end = name + "_loop", name + "_end"
    nc.regs_mov(registers, 0)
    nc.br(loop_start)
    with nc.body(loop_start):
        sv = nc.snap(registers, min_val=0, max_val=n - 1)
        yield sv
        nc.regs_alu(registers, registers, 1, op=ALU.add)
        nc.br_lt(registers, n, on_true=loop_start, on_false=loop_end)
    nc.switch_bb(loop_end)
    _ENG_BY_ET = {"Pool": "gpsimd", "DVE": "vector", "Activation": "scalar",
                  "PE": "tensor", "SP": "sync"}
    if not hasattr(sv, "val") or type(sv.val).__name__ != "RegisterHandles":
        return
    for h in sv.val.handles:
        hh = h
        if type(hh).__name__ == "Register64Pair":
            continue
        en = _ENG_BY_ET.get(str(hh.engine).split(".")[-1])
        if en:
            try:
                getattr(nc, en).free_register(hh)
            except Exception:
                pass


class RC:
    KEEP = 4

    def __init__(self, p):
        self.p = p
        self.r = {}
        self.pend = {}

    def _get(self, eng):
        if eng not in self.r:
            e = getattr(self.p.nc, eng)
            self.r[eng] = ([e.alloc_register(f"rc_{eng}_{k}") for k in range(3)],
                           [None, None])
            self.pend[eng] = []
        return self.r[eng]

    def _mul(self, eng, i, D):
        e = getattr(self.p.nc, eng)
        regs, cache = self._get(eng)
        key = (id(i), D, getattr(self.p, "_fori_ctr", 0))
        if cache[0] != key:
            e.reg_alu(regs[0], i, D, ALU.mult)
            cache[0] = key
        return regs[0]

    def wait(self, eng, sem_h, i, D, c):
        e = getattr(self.p.nc, eng)
        regs, _ = self._get(eng)
        r0 = self._mul(eng, i, D)
        e.reg_alu(regs[1], r0, c, ALU.add)
        e.wait_ge(sem_h, regs[1])

    def _snap(self, eng, ro, maxv):
        e = getattr(self.p.nc, eng)
        s = e.snap(ro, min_val=0, max_val=maxv)
        pend = self.pend[eng]
        pend.append(s.val)
        if len(pend) > self.KEEP:
            e.free_register(pend.pop(0))
        return s

    def off(self, eng, i, D, c, maxv):
        e = getattr(self.p.nc, eng)
        regs, _ = self._get(eng)
        r0 = self._mul(eng, i, D)
        e.reg_alu(regs[2], r0, c, ALU.add)
        return self._snap(eng, regs[2], maxv)

    def off2(self, eng, i, Di, j, Dj, c, maxv):
        e = getattr(self.p.nc, eng)
        regs, _ = self._get(eng)
        r0 = self._mul(eng, i, Di)
        e.reg_alu(regs[1], j, Dj, ALU.mult)
        e.reg_alu(regs[1], regs[1], c, ALU.add)
        e.reg_alu(regs[2], r0, regs[1], ALU.add)
        return self._snap(eng, regs[2], maxv)


from concourse.bass_primitives_rust import RegisterHandle as _RegH

_ENG_ET = {"gpsimd": mybir.EngineType.Pool, "sync": mybir.EngineType.SP,
           "vector": mybir.EngineType.DVE, "scalar": mybir.EngineType.Activation,
           "tensor": mybir.EngineType.PE}


def dyn_ap(handle, off, pattern):
    h = handle.tensor if isinstance(handle, bass.AP) else handle
    return bass.AP(h, off, pattern)


def dyn_dma(nc, eng, out, in_, sem=None, inc=16, **kw):
    e = getattr(nc, eng)
    ins = e.dma_start(out=out, in_=in_, **kw)
    if sem is not None:
        ins.then_inc(sem, inc)
    et = _ENG_ET[eng]
    for a in list(ins.ins.ins) + list(ins.ins.outs):
        if type(a).__name__ == "RegisterAccessPattern" and "_tmp_" in a.regref:
            e.free_register(_RegH(a.regref, et))
    return ins



def lp_moe(p, lp, rc, xin_fm_d, we1_d, be1_d, we2_d, be2_d, gates_d, h_fm_d,
           ye_ds, acc_init_tm_d, acc_out_tm_d, act="gelu", tag="moe"):
    cfg = p.cfg
    d, F, E = cfg.d, cfg.F, cfg.E
    for ei in range(E):
        lp_proj(p, lp, rc, xin_fm_d, we1_d[ei], d, F, b_d=be1_d[ei],
                out_fm_d=h_fm_d, act=act, tag=f"{tag}u{ei}")
        lp.phase_end()
        lp.phase_begin()
        lp_proj(p, lp, rc, h_fm_d, we2_d[ei], F, d, b_d=be2_d[ei],
                out_tm_d=ye_ds[ei], tag=f"{tag}d{ei}")
        lp.phase_end()
        lp.phase_begin()


def ph_moe_combine(p, gates_d, ye_ds, acc_init_tm_d, acc_out_tm_d, tag="mc"):
    cfg = p.cfg
    sc = p.sched
    d, E, nt = cfg.d, cfg.E, cfg.nt
    fence = sc.lasts()
    with ExitStack() as es:
        t_ = lambda n, sh, dt: es.enter_context(p.nc.sbuf_tensor(tag + n, sh, dt))
        ag = t_("_a", [128, 2 * d], F32)
        yg = t_("_y", [128, 2 * E * d], BF16)
        gg = t_("_g", [128, 2 * E], F32)
        prev = {}
        for t in range(nt):
            sl = t % 2

            def f_ld(e, dma, t=t, sl=sl):
                if acc_init_tm_d is not None:
                    dma(out=ag[:, sl * d:(sl + 1) * d],
                        in_=acc_init_tm_d[t * 128:(t + 1) * 128, 0:d])
                dma(out=gg[:, sl * E:(sl + 1) * E],
                    in_=gates_d[t * 128:(t + 1) * 128, 0:E])
                for ei in range(E):
                    dma(out=yg[:, (sl * E + ei) * d:(sl * E + ei + 1) * d],
                        in_=ye_ds[ei][t * 128:(t + 1) * 128, 0:d])
            deps = list(fence)
            if t >= 2:
                deps.append(prev[(t - 2, "st")])
            nd = (1 if acc_init_tm_d is not None else 0) + 1 + E
            s_ld = sc.dma("sync", f_ld, nd, deps=deps, note=f"{tag}_ld{t}")

            if acc_init_tm_d is None:
                def f_z(e, sl=sl):
                    return e.memset(ag[:, sl * d:(sl + 1) * d], 0.0)
                dd = [s_ld] + ([prev[(t - 2, "st")]] if t >= 2 else [])
                s_prev = sc.step("vector", f_z, deps=dd, note=f"{tag}_z{t}")
            else:
                s_prev = s_ld
            for ei in range(E):
                def f_acc(e, t=t, sl=sl, ei=ei):
                    return e.scalar_tensor_tensor(
                        ag[:, sl * d:(sl + 1) * d],
                        yg[:, (sl * E + ei) * d:(sl * E + ei + 1) * d],
                        gg[:, sl * E + ei:sl * E + ei + 1],
                        ag[:, sl * d:(sl + 1) * d], ALU.mult, ALU.add)
                s_prev = sc.step("vector", f_acc, deps=[s_prev, s_ld],
                                 note=f"{tag}_a{t}_{ei}")

            def f_st(e, dma, t=t, sl=sl):
                dma(out=acc_out_tm_d[t * 128:(t + 1) * 128, 0:d],
                    in_=ag[:, sl * d:(sl + 1) * d])
            prev[(t, "st")] = sc.dma("gpsimd", f_st, 1, deps=[s_prev],
                                     note=f"{tag}_st{t}")
    return sc.lasts()



def lp_attn(p, lp, rc, qfm_d, kfm_d, vtm_d, ctx_tm_d, kind, scale, tag):
    nc = p.nc
    cfg = p.cfg
    S, d, nh, dh, ncd = cfg.S, cfg.d, cfg.nh, cfg.dh, cfg.nc_d
    T = cfg.T
    if kind == "std":
        KEYS = S
        insts = [(s, h) for s in range(cfg.nseq) for h in range(nh)]
    else:
        KEYS = (d // dh) * 128
        insts = [(s, w) for s in range(cfg.nseq) for w in range(S // 128)]
    RB = KEYS // 128
    C = KEYS // 128
    es = ExitStack()
    t_ = lambda n, sh, dt: es.enter_context(nc.sbuf_tensor(tag + n, sh, dt))
    ksb = t_("_k", [128, 2 * KEYS], BF16)
    vsb = t_("_v", [128, 2 * C * 64], BF16)
    qt = t_("_qt", [128, 128], BF16)
    atn = t_("_at", [128, KEYS], BF16)
    att = t_("_tt", [128, C * 128], BF16)
    cst = t_("_cs", [128, 64], BF16)
    stt = t_("_st", [128, 8], F32)
    psc = es.enter_context(nc.psum_tensor(tag + "_ps", [128, KEYS], F32))
    ptp = es.enter_context(nc.psum_tensor(tag + "_pt", [128, 512], BF16))
    pcx = es.enter_context(nc.psum_tensor(tag + "_px", [128, 64], F32))

    LS, QG, CS, PEA, LVA, LAA = "ls0", "gd4", "gd5", "lt0", "lv2", "la0"
    NTG = C // 4
    dQG, dCS, dLA = 16, 16, 1
    dPE = 1 + NTG + 1
    dLV = 3 + NTG + 1
    bb = prebump(nc, lp, [(PEA, RB * dPE), (LVA, RB * dLV), (LAA, RB * dLA)],
                 eng="vector")
    for ii, (s, hw) in enumerate(insts):
        sl = ii % 2
        qo, vo = sl * KEYS, sl * C * 64
        bQG, bCS = lp.base(QG), lp.base(CS)
        bPE, bLV, bLA = lp.base(PEA), lp.base(LVA), lp.base(LAA)
        if ii >= 2:
            nc.sync.wait_ge(lp.h(PEA), bPE - RB * dPE)
        if kind == "std":
            h = hw
            nc.sync.dma_start(out=ksb[0:64, qo:qo + S],
                              in_=kfm_d[h // 2, (h % 2) * 64:(h % 2) * 64 + 64,
                                        s * S:(s + 1) * S]).then_inc(lp.h(LS), 16)
            q4 = max(C // 4, 1)
            for pc in range(0, C, q4):
                nc.sync.dma_start(
                    out=vsb[:, vo + pc * 64:vo + (pc + q4) * 64],
                    in_=vtm_d[s * S + pc * 128:s * S + (pc + q4) * 128,
                              h * 64:(h + 1) * 64]
                    .rearrange("(c p) x -> p c x", p=128)).then_inc(lp.h(LS), 16)
            nld = 1 + (C // q4)
        else:
            w = hw
            nc.sync.dma_start(out=ksb[0:64, qo:qo + KEYS]
                              .rearrange("p (c h x) -> p c h x", c=ncd, h=2),
                              in_=kfm_d[0:ncd, 0:128,
                                        s * S + w * 128:s * S + (w + 1) * 128]
                              .rearrange("c (h p) x -> p c h x", h=2)) \
                .then_inc(lp.h(LS), 16)
            nc.sync.dma_start(out=vsb[:, vo:vo + C * 64],
                              in_=vtm_d[s * S + w * 128:s * S + (w + 1) * 128, 0:d]) \
                .then_inc(lp.h(LS), 16)
            nld = 2
        bLS = lp.bump(LS, nld * 16)

        with fori2(p, RB, level=1) as rb:
            rc.wait("gpsimd", lp.h(PEA), rb, dPE, bPE - dPE + 1)
            if kind == "std":
                dyn_dma(nc, "gpsimd", qt[0:64, 0:128],
                        dyn_ap(qfm_d,
                               rc.off("gpsimd", rb, 128, h * 64 * T + s * S,
                                      d * T - 128), [[T, 64], [1, 128]]),
                        lp.h(QG))
            else:
                dyn_dma(nc, "gpsimd", qt[0:64, 0:128],
                        dyn_ap(qfm_d,
                               rc.off("gpsimd", rb, 64 * T,
                                      s * S + w * 128, d * T - 128),
                               [[T, 64], [1, 128]]),
                        lp.h(QG))
            nc.tensor.wait_ge(lp.h(LS), bLS)
            rc.wait("tensor", lp.h(QG), rb, dQG, bQG + 16)
            rc.wait("tensor", lp.h(LAA), rb, dLA, bLA - dLA + 1)
            for cg in range(KEYS // 512):
                ins = nc.tensor.matmul(psc[:, cg * 512:(cg + 1) * 512],
                                       qt[0:64, 0:128],
                                       ksb[0:64, qo + cg * 512:qo + (cg + 1) * 512],
                                       start=True, stop=True)
            ins.then_inc(lp.h(PEA), 1)
            rc.wait("vector", lp.h(PEA), rb, dPE, bPE + 1)
            nc.vector.reduce_max(stt[:, 0:1], psc[:, 0:KEYS], axis=AX.X,
                                 negate=True).then_inc(lp.h(LVA), 1)
            rc.wait("vector", lp.h(LVA), rb, dLV, bLV + 1)
            nc.vector.tensor_scalar(stt[:, 1:2], stt[:, 0:1], scale, None,
                                    ALU.mult).then_inc(lp.h(LVA), 1)
            rc.wait("scalar", lp.h(LVA), rb, dLV, bLV + 2)
            rc.wait("scalar", lp.h(LVA), rb, dLV, bLV - dLV + 3 + NTG)
            nc.scalar.activation(atn[:, 0:KEYS], psc[:, 0:KEYS], AF.Exp,
                                 bias=stt[:, 1:2], scale=scale,
                                 accum_out=stt[:, 2:3]).then_inc(lp.h(LAA), 1)
            rc.wait("vector", lp.h(LAA), rb, dLA, bLA + 1)
            nc.vector.reciprocal(stt[:, 3:4], stt[:, 2:3]).then_inc(lp.h(LVA), 1)
            for tg in range(NTG):
                rc.wait("tensor", lp.h(LAA), rb, dLA, bLA + 1)
                if tg > 0:
                    rc.wait("tensor", lp.h(LVA), rb, dLV, bLV + 3 + tg)
                else:
                    rc.wait("tensor", lp.h(LVA), rb, dLV, bLV - dLV + 3 + NTG)
                for j in range(4):
                    kc = tg * 4 + j
                    ins = nc.tensor.transpose(ptp[:, j * 128:(j + 1) * 128],
                                              atn[:, kc * 128:(kc + 1) * 128],
                                              p.identb[:, :])
                ins.then_inc(lp.h(PEA), 1)
                rc.wait("vector", lp.h(PEA), rb, dPE, bPE + 1 + tg + 1)
                nc.vector.tensor_copy(att[:, tg * 512:(tg + 1) * 512],
                                      ptp[:, 0:512]).then_inc(lp.h(LVA), 1)
            rc.wait("tensor", lp.h(LVA), rb, dLV, bLV + 3 + NTG)
            rc.wait("tensor", lp.h(LVA), rb, dLV, bLV - dLV + dLV)
            for kc in range(C):
                ins = nc.tensor.matmul(pcx[0:128, 0:64] if False else pcx[:, 0:64],
                                       att[:, kc * 128:(kc + 1) * 128],
                                       vsb[:, vo + kc * 64:vo + (kc + 1) * 64],
                                       start=(kc == 0), stop=(kc == C - 1))
            ins.then_inc(lp.h(PEA), 1)
            rc.wait("vector", lp.h(PEA), rb, dPE, bPE + dPE)
            rc.wait("vector", lp.h(CS), rb, dCS, bCS - dCS + 16)
            nc.vector.tensor_scalar(cst[:, 0:64], pcx[:, 0:64], stt[:, 3:4],
                                    None, ALU.mult).then_inc(lp.h(LVA), 1)
            rc.wait("gpsimd", lp.h(LVA), rb, dLV, bLV + dLV)
            if kind == "std":
                dyn_dma(nc, "gpsimd",
                        dyn_ap(ctx_tm_d,
                               rc.off("gpsimd", rb, 128 * d,
                                      s * S * d + h * 64, (T - 1) * d),
                               [[d, 128], [1, 64]]),
                        cst[:, 0:64], lp.h(CS))
            else:
                dyn_dma(nc, "gpsimd",
                        dyn_ap(ctx_tm_d,
                               rc.off("gpsimd", rb, 64,
                                      (s * S + w * 128) * d, (T - 1) * d),
                               [[d, 128], [1, 64]]),
                        cst[:, 0:64], lp.h(CS))
        lp.bump(QG, RB * dQG)
        lp.bump(CS, RB * dCS)
        lp.bump(PEA, RB * dPE)
        lp.bump(LVA, RB * dLV)
        lp.bump(LAA, RB * dLA)
    es.close()



def lp_tm2fm(p, lp, rc, tm_d, fm_d, tag="t2f"):
    nc = p.nc
    cfg = p.cfg
    d, ncd, nt, T = cfg.d, cfg.nc_d, cfg.nt, cfg.T
    es = ExitStack()
    t_ = lambda n, sh, dt: es.enter_context(nc.sbuf_tensor(tag + n, sh, dt))
    xg = t_("_x", [128, 2 * d], BF16)
    fg = t_("_f", [128, 2 * d], BF16)
    ptp = es.enter_context(nc.psum_tensor(tag + "_pt", [128, 1024], BF16))
    GD, GS, LV, PE = "gd4", "gd5", "lv2", "lt0"
    D_GD, D_GS, D_LV, D_PE = 2 * 16, 2 * 16, 2 * 2, 2 * 2
    bb = prebump(nc, lp, [(PE, D_PE), (LV, D_LV)])
    bPE, bLV = bb[PE], bb[LV]
    bGD, bGS = lp.base(GD), lp.base(GS)
    with fori2(p, nt // 2) as i:
        for hf in range(2):
            rc.wait("gpsimd", lp.h(PE), i, D_PE, bPE - D_PE + hf * 2 + 2)
            dyn_dma(nc, "gpsimd", xg[:, hf * d:(hf + 1) * d],
                    dyn_ap(tm_d, rc.off("gpsimd", i, 256 * d, hf * 128 * d,
                                        (T - 128) * d),
                           [[d, 128], [1, d]]), lp.h(GD))
            for g2 in range(2):
                rc.wait("tensor", lp.h(GD), i, D_GD, bGD + (hf + 1) * 16)
                if hf == 0 and g2 == 0:
                    rc.wait("tensor", lp.h(LV), i, D_LV, bLV - D_LV + 4)
                else:
                    rc.wait("tensor", lp.h(LV), i, D_LV, bLV + hf * 2 + g2)
                for c in range(4):
                    ins = nc.tensor.transpose(
                        ptp[:, g2 * 512 + c * 128:g2 * 512 + (c + 1) * 128],
                        xg[:, hf * d + (g2 * 4 + c) * 128:hf * d + (g2 * 4 + c + 1) * 128],
                        p.identb[:, :])
                ins.then_inc(lp.h(PE), 1)
                rc.wait("vector", lp.h(PE), i, D_PE, bPE + hf * 2 + g2 + 1)
                rc.wait("vector", lp.h(GS), i, D_GS, bGS - D_GS + (hf + 1) * 16)
                nc.vector.tensor_copy(fg[:, hf * d + g2 * 512:hf * d + (g2 + 1) * 512],
                                      ptp[:, g2 * 512:(g2 + 1) * 512]) \
                    .then_inc(lp.h(LV), 1)
            rc.wait("gpsimd", lp.h(LV), i, D_LV, bLV + (hf + 1) * 2)
            dyn_dma(nc, "gpsimd",
                    dyn_ap(fm_d,
                           rc.off("gpsimd", i, 256, hf * 128, T - 128),
                           [[T, 128], [128 * T, ncd], [1, 128]]),
                    fg[:, hf * d:(hf + 1) * d].rearrange("p (c x) -> p c x", c=ncd),
                    lp.h(GS))
    lp.bump(GD, (nt // 2) * D_GD)
    lp.bump(GS, (nt // 2) * D_GS)
    lp.bump(LV, (nt // 2) * D_LV)
    lp.bump(PE, (nt // 2) * D_PE)
    es.close()



def ph_gsel(p, gfull_d, sel_d, gown_d, tag="gs"):
    cfg = p.cfg
    sc = p.sched
    E, nt, T = cfg.E, cfg.nt, cfg.T
    fence = sc.lasts()
    with ExitStack() as es:
        t_ = lambda n, sh, dt: es.enter_context(p.nc.sbuf_tensor(tag + n, sh, dt))
        sel = t_("_s", [128, 8], F32)
        cand = t_("_c", [128, 2 * 8 * E], F32)
        acc = t_("_a", [128, 2 * E], F32)

        def f_s(e, dma):
            dma(out=sel[:, :], in_=bcast_row(sel_d, 128))
        s_sel = sc.dma("sync", f_s, 1, deps=fence, note=tag + "_sel")
        prev = {}
        for t in range(nt):
            sl = t % 2

            def f_ld(e, dma, t=t, sl=sl):
                for c in range(NCORES):
                    dma(out=cand[:, (sl * 8 + c) * E:(sl * 8 + c + 1) * E],
                        in_=gfull_d[c * T + t * 128:c * T + (t + 1) * 128, 0:E])
            deps = list(fence)
            if t >= 2:
                deps.append(prev[(t - 2, "st")])
            s_ld = sc.dma("sync", f_ld, NCORES, deps=deps, note=f"{tag}_ld{t}")

            def f_z(e, t=t, sl=sl):
                return e.memset(acc[:, sl * E:(sl + 1) * E], 0.0)
            dd = [s_ld] + ([prev[(t - 2, "st")]] if t >= 2 else [])
            s_p = sc.step("vector", f_z, deps=dd, note=f"{tag}_z{t}")
            for c in range(NCORES):
                def f_a(e, t=t, sl=sl, c=c):
                    return e.scalar_tensor_tensor(
                        acc[:, sl * E:(sl + 1) * E],
                        cand[:, (sl * 8 + c) * E:(sl * 8 + c + 1) * E],
                        sel[:, c:c + 1], acc[:, sl * E:(sl + 1) * E],
                        ALU.mult, ALU.add)
                s_p = sc.step("vector", f_a, deps=[s_p, s_sel, s_ld],
                              note=f"{tag}_a{t}_{c}")

            def f_st(e, dma, t=t, sl=sl):
                dma(out=gown_d[t * 128:(t + 1) * 128, 0:E],
                    in_=acc[:, sl * E:(sl + 1) * E])
            prev[(t, "st")] = sc.dma("gpsimd", f_st, 1, deps=[s_p],
                                     note=f"{tag}_st{t}")
    return sc.lasts()


def build_fused(cfg):
    p = PB(cfg, race=False)
    nc = p.nc
    d, F, E, ncd, nF, T, Tt = cfg.d, cfg.F, cfg.E, cfg.nc_d, cfg.nF, cfg.T, \
        cfg.Ttot
    IN = {}
    specs = {
        "L0_ffn_Wq": [d, d], "L0_ffn_Wk": [d, d], "L0_ffn_Wv": [d, d],
        "L0_ffn_Wo": [d, d], "L0_ffn_bo": [d], "L0_ffn_Wr": [d, E],
        "L0_ffn_We1": [E, d, F], "L0_ffn_be1": [E, F],
        "L0_ffn_We2": [E, F, d], "L0_ffn_be2": [E, d],
        "L0_ffn_ln1_g": [d], "L0_ffn_ln1_b": [d],
        "L0_ffn_ln2_g": [d], "L0_ffn_ln2_b": [d],
        "L0_attn_wq": [d, d], "L0_attn_bq": [d], "L0_attn_wk": [d, d],
        "L0_attn_bk": [d], "L0_attn_wv": [d, d], "L0_attn_bv": [d],
        "L0_attn_wo": [d, d], "L0_attn_bo": [d],
        "L0_attn_gamma": [d], "L0_attn_beta": [d],
        "L1_ffn_gate_w": [E, d], "L1_ffn_gate_b": [E],
        "L1_ffn_w1": [E, d, F], "L1_ffn_b1": [E, F],
        "L1_ffn_w2": [E, F, d], "L1_ffn_b2": [E, d],
        "L1_attn_Wq": [d, d], "L1_attn_bq": [d], "L1_attn_Wk": [d, d],
        "L1_attn_bk": [d], "L1_attn_Wv": [d, d], "L1_attn_bv": [d],
        "L1_attn_Wo": [d, d], "L1_attn_bo": [d],
        "L1_attn_ln_g": [d], "L1_attn_ln_b": [d],
    }
    for n, sh in specs.items():
        IN[n] = nc.dram_tensor(n, sh, F32, kind="ExternalInput")
    x_d = nc.dram_tensor("x", [T, d], F32, kind="ExternalInput")
    sel_d = nc.dram_tensor("c_gsel", [NCORES], F32, kind="ExternalInput")
    y_d = nc.dram_tensor("y", [T, d], F32, kind="ExternalOutput")
    S_ = lambda n, sh, dt: nc.dram_tensor(
        n, sh, dt, kind="ExternalOutput" if n in DBG_OUTS else "Internal")
    xin_fm = S_("s_xin", [ncd, 128, T], BF16)
    qfm = S_("s_q", [ncd, 128, T], BF16)
    kfm = S_("s_k", [ncd, 128, T], BF16)
    vtm = S_("s_v", [T, d], BF16)
    ctm = S_("s_ctm", [T, d], BF16)
    cfm = S_("s_cfm", [ncd, 128, T], BF16)
    ao_tm = S_("s_ao", [T, d], BF16)
    o1_tm = S_("s_o1", [T, d], F32)
    o1_fm = S_("s_o1f", [ncd, 128, T], BF16)
    gates = S_("s_gt", [T, E], F32)
    h_fm = S_("s_h", [nF, 128, T], BF16)
    ye_ds = [S_(f"s_ye{e}", [T, d], BF16) for e in range(E)]
    acc_tm = S_("s_acc", [T, d], F32)
    xc = [S_(f"s_x{i}", [T, d], F32) for i in range(6)]
    lg_loc = S_("s_lgl", [T, E], F32)
    lg_full = S_("s_lgf", [Tt, E], F32)
    g_full = S_("s_gf", [Tt, E], F32)
    ctxf = S_("s_ctxf", [ncd, 128, T], F32)
    o_tm2 = S_("s_otm", [T, d], F32)
    on_fm = S_("s_on", [ncd, 128, T], BF16)
    pr_tm = S_("s_pr", [T, d], BF16)

    p.start()
    lp = LP(p)
    rc = RC(p)

    def static_block():
        with nc.Block() as b:
            p.sched.emit_pending(b)
        lp.sched_sync()

    static_block()
    lp.phase_end()
    lp.phase_begin()

    _prbctr = [0]

    def _nfree(eng):
        hs = []
        try:
            while True:
                _prbctr[0] += 1
                hs.append(eng.alloc_register(name=f"_prb{_prbctr[0]}"))
        except Exception:
            pass
        for h in hs:
            eng.free_register(h)
        return len(hs)

    def _dbg(tag):
        if DBGREG:
            print((tag, _nfree(nc.gpsimd), _nfree(nc.vector),
                   _nfree(nc.scalar), _nfree(nc.tensor), _nfree(nc.sync)),
                  flush=True)

    def LPH(fn, *a, **kw):
        fn(p, lp, rc, *a, **kw)
        lp.phase_end()
        lp.phase_begin()
        lp.sched_sync()
        _dbg(("lp", kw.get("tag", getattr(fn, "__name__", "?"))))

    def SPH(fn, *a, **kw):
        fn(p, *a, **kw)
        static_block()
        lp.phase_end()
        lp.phase_begin()
        _dbg(("sp", kw.get("tag", getattr(fn, "__name__", "?"))))

    cci = [0]

    def emit_l0ffn(xi, yo, tagp):
        LPH(lp_ln, xi, fm_d=xin_fm, eps=1e-5, tag=tagp + "ln")
        LPH(lp_proj, xin_fm, IN["L0_ffn_Wq"], d, d, out_fm_d=qfm, tag=tagp + "q")
        LPH(lp_proj, xin_fm, IN["L0_ffn_Wk"], d, d, out_fm_d=kfm, tag=tagp + "k")
        LPH(lp_proj, xin_fm, IN["L0_ffn_Wv"], d, d, out_tm_d=vtm, tag=tagp + "v")
        LPH(lp_attn, qfm, kfm, vtm, ctm, "std", 1.0 / 8.0, tagp + "a")
        LPH(lp_tm2fm, ctm, cfm, tag=tagp + "t")
        LPH(lp_proj, cfm, IN["L0_ffn_Wo"], d, d, b_d=IN["L0_ffn_bo"],
            out_tm_d=ao_tm, tag=tagp + "o")
        SPH(ph_resid, "sum_ln", xi, ao_tm, tm_d=o1_tm, fm_d=o1_fm, eps=1e-5,
            eps2=1e-6, g_d=IN["L0_ffn_ln1_g"], b_d=IN["L0_ffn_ln1_b"],
            adt=BF16, tag=tagp + "r1")
        SPH(ph_logits, o1_tm, IN["L0_ffn_Wr"], E, mode="l0top2", out_d=gates,
            tag=tagp + "lg")
        LPH(lp_moe, xin_fm if False else o1_fm, IN["L0_ffn_We1"],
            IN["L0_ffn_be1"], IN["L0_ffn_We2"], IN["L0_ffn_be2"], gates,
            h_fm, ye_ds, o1_tm, acc_tm, act="gelu", tag=tagp + "m")
        SPH(ph_moe_combine, gates, ye_ds, o1_tm, acc_tm, tag=tagp + "mc")
        SPH(ph_resid, "x_cln", xi, acc_tm, y_d=yo, coef=0.5, eps=1e-6,
            g_d=IN["L0_ffn_ln2_g"], b_d=IN["L0_ffn_ln2_b"], adt=F32,
            tag=tagp + "r2")

    def emit_l0attn(xi, yo, tagp):
        LPH(lp_ln, xi, fm_d=xin_fm, eps=1e-5, tag=tagp + "ln")
        LPH(lp_proj, xin_fm, IN["L0_attn_wq"], d, d, b_d=IN["L0_attn_bq"],
            out_fm_d=qfm, tag=tagp + "q")
        LPH(lp_proj, xin_fm, IN["L0_attn_wk"], d, d, b_d=IN["L0_attn_bk"],
            out_fm_d=kfm, tag=tagp + "k")
        LPH(lp_proj, xin_fm, IN["L0_attn_wv"], d, d, b_d=IN["L0_attn_bv"],
            out_tm_d=vtm, tag=tagp + "v")
        LPH(lp_attn, qfm, kfm, vtm, ctm, "resh", 1.0 / 8.0, tagp + "a")
        LPH(lp_tm2fm, ctm, cfm, tag=tagp + "t")
        LPH(lp_proj, cfm, IN["L0_attn_wo"], d, d, b_d=IN["L0_attn_bo"],
            out_tm_d=ao_tm, tag=tagp + "o")
        SPH(ph_resid, "sum_ln", xi, ao_tm, tm_d=o1_tm, eps=1e-5, eps2=1e-5,
            g_d=IN["L0_attn_gamma"], b_d=IN["L0_attn_beta"], adt=BF16,
            tag=tagp + "r1")
        SPH(ph_resid, "x_ca", xi, o1_tm, y_d=yo, coef=1.0, adt=F32,
            tag=tagp + "r2")

    def emit_l1ffn(xi, yo, tagp):
        LPH(lp_ln, xi, fm_d=xin_fm, eps=1e-5, tag=tagp + "ln")
        SPH(ph_logits, xi, IN["L1_ffn_gate_w"], E, b_d=IN["L1_ffn_gate_b"],
            ln_first=True, mode="store", out_d=lg_loc, w_is_ExD=True,
            eps=1e-5, tag=tagp + "lg")
        ccs = p.es.enter_context(nc.semaphore(f"cc{cci[0]}"))
        cci[0] += 1
        nc.gpsimd.collective_compute(
            "AllGather", ALU.bypass, replica_groups=[list(range(NCORES))],
            ins=[lg_loc.ap().opt()], outs=[lg_full.ap().opt()]).then_inc(ccs, 1)
        nc.gpsimd.wait_ge(ccs, 1)
        nc.gpsimd.sem_inc(lp.h("gp0"), 1)
        lp.bump("gp0", 1)
        lp.phase_end("gpsimd")
        lp.phase_begin()
        lp.sched_sync()
        SPH(ph_route_l1, lg_full, g_full, Tt, E, cap=cfg.cap, tag=tagp + "rt")
        SPH(ph_gsel, g_full, sel_d, gates, tag=tagp + "gs")
        LPH(lp_moe, xin_fm, IN["L1_ffn_w1"], IN["L1_ffn_b1"], IN["L1_ffn_w2"],
            IN["L1_ffn_b2"], gates, h_fm, ye_ds, None, acc_tm, act="relu",
            tag=tagp + "m")
        SPH(ph_moe_combine, gates, ye_ds, None, acc_tm, tag=tagp + "mc")
        SPH(ph_resid, "x_ca", xi, acc_tm, y_d=yo, coef=0.5, adt=F32,
            tag=tagp + "r2")

    def emit_l1attn(xi, yo, tagp):
        LPH(lp_ln, xi, fm_d=xin_fm, eps=1e-5, tag=tagp + "ln")
        SPH(ph_proj, xin_fm, IN["L1_attn_Wq"], d, d, b_d=IN["L1_attn_bq"],
            out_fm_d=qfm, act="elu1", w_transposed=True, tag=tagp + "q")
        SPH(ph_proj, xin_fm, IN["L1_attn_Wk"], d, d, b_d=IN["L1_attn_bk"],
            out_fm_d=kfm, act="elu1", w_transposed=True, tag=tagp + "k")
        SPH(ph_proj, xin_fm, IN["L1_attn_Wv"], d, d, b_d=IN["L1_attn_bv"],
            out_tm_d=vtm, w_transposed=True, tag=tagp + "v")
        SPH(ph_lattn, qfm, kfm, vtm, ctxf, tag=tagp + "la")
        SPH(ph_fm2tm, ctxf, o_tm2, dt=F32, tag=tagp + "f2")
        LPH(lp_ln, o_tm2, fm_d=on_fm, eps=1e-5, g_d=IN["L1_attn_ln_g"],
            b_d=IN["L1_attn_ln_b"], tag=tagp + "l2")
        SPH(ph_proj, on_fm, IN["L1_attn_Wo"], d, d, b_d=IN["L1_attn_bo"],
            out_tm_d=pr_tm, w_transposed=True, tag=tagp + "o")
        SPH(ph_resid, "x_ca", xi, pr_tm, y_d=yo, coef=1.0, adt=BF16,
            tag=tagp + "r2")

    emit_l0ffn(x_d, xc[0], "A")
    emit_l0attn(xc[0], xc[1], "B")
    emit_l0ffn(xc[1], xc[2], "C")
    emit_l1ffn(xc[2], xc[3], "D")
    emit_l1attn(xc[3], xc[4], "E")
    emit_l1ffn(xc[4], y_d, "G")
    finalize_lp(p, lp)
    return p


def kernel(**inputs):
    cfg = Cfg()
    T = cfg.T
    x0 = np.ascontiguousarray(np.asarray(inputs["L0_attn_x"], dtype=np.float32))
    B, S, d = x0.shape
    xs = [np.ascontiguousarray(x0[2 * c:2 * c + 2].reshape(T, d))
          for c in range(NCORES)]

    def f32(n):
        return np.ascontiguousarray(np.asarray(inputs[n], dtype=np.float32))

    p = _get_prog("fused", cfg)
    base = {n: f32(n) for n in inputs if n != "L0_attn_x"}
    maps = []
    for c in range(NCORES):
        m = dict(base)
        m["x"] = xs[c]
        sel = np.zeros(NCORES, dtype=np.float32)
        sel[c] = 1.0
        m["c_gsel"] = sel
        for cn, cv in p.extra_inputs.items():
            m[cn] = cv
        maps.append(m)
    res = run_bass_kernel_spmd(p.nc, maps, core_ids=list(range(NCORES)))
    out = np.concatenate([res.results[c]["y"].reshape(2, S, d)
                          for c in range(NCORES)], axis=0)
    return out.astype(np.float32)


# version 07 (final)
# speedup vs baseline: 1.1440x; 1.1440x over previous
import math
from contextlib import ExitStack

import numpy as np

import concourse.bass as bass
import concourse.mybir as mybir
from concourse.bass_utils import run_bass_kernel_spmd

F32 = mybir.dt.float32
BF16 = mybir.dt.bfloat16
AF = mybir.ActivationFunctionType
ALU = mybir.AluOpType
AX = mybir.AxisListType

NCORES = 8
DBGREG = False
DBG_OUTS = ()
ENGINES = ("sync", "gpsimd", "tensor", "vector", "scalar")
ENG_SEM = {"sync": "ld", "gpsimd": "st", "tensor": "pe",
           "vector": "ve", "scalar": "ac"}


class Cfg:
    def __init__(self, d=1024, nh=16, S=2048, nseq=2, F=4096, E=8,
                 cap=5120, Ttot=32768):
        self.d, self.nh, self.S, self.nseq = d, nh, S, nseq
        self.dh = d // nh
        self.T = S * nseq
        self.F, self.E, self.cap, self.Ttot = F, E, cap, Ttot
        self.nc_d = d // 128
        self.nt = self.T // 128
        self.nt_s = S // 128
        self.nF = F // 128


class Step:
    __slots__ = ("eng", "sem", "val", "idx", "fn", "deps", "n_dma", "note")


DMA_POOL = 8


class Sched:
    def __init__(self, nc):
        self.nc = nc
        self.es = ExitStack()
        self.sems = {}
        for n in ("pe", "ve", "ac"):
            self.sems[n] = [self.es.enter_context(nc.semaphore(n)), 0]
        for q in ("ld", "st"):
            for i in range(DMA_POOL):
                n = f"{q}{i}"
                self.sems[n] = [self.es.enter_context(nc.semaphore(n)), 0]
        self.steps = []
        self.pending = {e: [] for e in ENGINES}
        self.dma_rr = {"sync": 0, "gpsimd": 0}
        self.last_by_sem = {}

    def step(self, eng, fn, deps=(), note=""):
        s = Step()
        s.eng, s.sem, s.fn, s.n_dma, s.note = eng, ENG_SEM[eng], fn, None, note
        s.idx = len(self.steps)
        s.deps = self._mkdeps(s, deps)
        self.sems[s.sem][1] += 1
        s.val = self.sems[s.sem][1]
        self.steps.append(s)
        self.pending[eng].append(s)
        self.last_by_sem[s.sem] = s
        return s

    def dma(self, eng, fn, n_dma, deps=(), note=""):
        assert eng in ("sync", "gpsimd")
        q = "ld" if eng == "sync" else "st"
        i = self.dma_rr[eng]
        self.dma_rr[eng] = (i + 1) % DMA_POOL
        s = Step()
        s.eng, s.sem, s.fn, s.n_dma, s.note = eng, f"{q}{i}", fn, n_dma, note
        s.idx = len(self.steps)
        s.deps = self._mkdeps(s, deps)
        prev = self.sems[s.sem][1]
        if prev > 0 and (s.sem not in s.deps or s.deps[s.sem] < prev):
            s.deps[s.sem] = prev
        self.sems[s.sem][1] += 16 * n_dma
        s.val = self.sems[s.sem][1]
        self.steps.append(s)
        self.pending[eng].append(s)
        self.last_by_sem[s.sem] = s
        return s

    def _mkdeps(self, s, deps):
        out = {}
        for dp in deps:
            if dp is None:
                continue
            assert dp.idx < s.idx, f"forward dep {dp.note} -> {s.note}"
            if dp.sem not in out or dp.val > out[dp.sem]:
                out[dp.sem] = dp.val
        return out

    def lasts(self):
        return list(self.last_by_sem.values())

    def emit_final(self, block):
        def body(e):
            for n in sorted(self.sems):
                h, v = self.sems[n]
                if v > 0:
                    e.wait_ge(h, v)
        block.gpsimd(body)

    def emit_pending(self, block):
        for eng in ENGINES:
            steps = self.pending[eng]
            if not steps:
                continue
            self.pending[eng] = []

            def body(e, steps=steps, eng=eng):
                for s in steps:
                    for semname, val in sorted(s.deps.items()):
                        e.wait_ge(self.sems[semname][0], val)
                    if s.n_dma is None:
                        last = s.fn(e)
                        assert last is not None, f"step {s.note} returned None"
                        last.then_inc(self.sems[s.sem][0], 1)
                    else:
                        cnt = [0]

                        def dma(out, in_, **kw):
                            ins = e.dma_start(out=out, in_=in_, **kw)
                            ins.then_inc(self.sems[s.sem][0], 16)
                            cnt[0] += 1
                            return ins
                        s.fn(e, dma)
                        assert cnt[0] == s.n_dma, \
                            f"step {s.note}: {cnt[0]} != {s.n_dma} DMAs"
            getattr(block, eng)(body)


def bcast_row(dram_vec, n, lo=None, hi=None):
    ap = dram_vec if isinstance(dram_vec, bass.AP) else dram_vec.ap()
    if ap.ndim == 2:
        ap = ap.squeeze(0) if ap.shape[0] == 1 else ap
    if lo is not None:
        ap = ap[lo:hi]
    return ap.unsqueeze(0).broadcast_to([n, ap.shape[-1]])



class PB:
    def __init__(self, cfg, race=True):
        self.cfg = cfg
        self.nc = bass.Bass("TRN2", target_bir_lowering=False,
                            detect_race_conditions=race)
        self.es = ExitStack()
        self.sched = None
        self.extra_inputs = {}

    def sbuf(self, name, shape, dtype):
        return self.es.enter_context(self.nc.sbuf_tensor(name, shape, dtype))

    def psum(self, name, shape, dtype=F32):
        return self.es.enter_context(self.nc.psum_tensor(name, shape, dtype))

    def dram_const(self, name, shape, dtype):
        if not hasattr(self, "_dconsts"):
            self._dconsts = {}
        if name not in self._dconsts:
            self._dconsts[name] = self.nc.dram_tensor(name, shape, dtype,
                                                      kind="ExternalInput")
        return self._dconsts[name]

    def start(self):
        self.sched = Sched(self.nc)
        self.extra_inputs["c_identb"] = np.eye(128, dtype=np.float32)
        identb_d = self.nc.dram_tensor("c_identb", [128, 128], F32,
                                       kind="ExternalInput")
        self.identb = self.sbuf("identb", [128, 128], BF16)
        self.identf = self.sbuf("identf", [128, 128], F32)

        def f(e, dma):
            dma(out=self.identf[:, :], in_=identb_d[:, :])
        s1 = self.sched.dma("sync", f, 1, note="ident_ld")
        self.sched.step("vector",
                        lambda e: e.tensor_copy(self.identb[:, :],
                                                self.identf[:, :]),
                        deps=[s1], note="ident_cast")



def ph_ln(p, x_d, fm_d=None, tm_d=None, eps=1e-5, g_d=None, b_d=None,
          tag="ln", tiles=None, sb=None):
    cfg = p.cfg
    d, ncd = cfg.d, cfg.nc_d
    sc = p.sched
    if tiles is None:
        tiles = list(range(cfg.nt))
    grp = 4
    groups = [tiles[i:i + grp] for i in range(0, len(tiles), grp)]
    xg, yg, yb, stt = sb["xg"], sb["yg"], sb["yb"], sb["stt"]
    ptp = sb["ptp"]
    affine = g_d is not None
    fence = sc.lasts()

    if affine:
        gb = sb["gb"]

        def f(e, dma):
            dma(out=gb[:, 0:d], in_=bcast_row(g_d, 128))
            dma(out=gb[:, d:2 * d], in_=bcast_row(b_d, 128))
        s_gb = sc.dma("sync", f, 2, deps=fence, note=tag + "_gb")
    else:
        s_gb = None

    ld_steps = [None] * len(groups)
    y_steps = [None] * len(groups)
    tp_steps = {}
    cp_steps = {}
    st_steps = {}
    tmst_steps = [None] * len(groups)
    out_steps = []

    for gi, gts in enumerate(groups):
        sl = gi % 2
        o, so = sl * grp, sl * 8 * grp
        n = len(gts)

        def f_ld(e, dma, gts=gts, o=o):
            for j, t in enumerate(gts):
                dma(out=xg[:, (o + j) * d:(o + j + 1) * d],
                    in_=x_d[t * 128:(t + 1) * 128, 0:d])
        deps = list(fence)
        if gi >= 2:
            deps.append(y_steps[gi - 2])
            if tm_d is not None:
                deps.append(tmst_steps[gi - 2])
        ld_steps[gi] = sc.dma("sync", f_ld, n, deps=deps, note=f"{tag}_ld{gi}")

        deps = [ld_steps[gi]]
        if gi >= 2:
            deps.append(y_steps[gi - 2])
            if tm_d is not None:
                deps.append(tmst_steps[gi - 2])
            lastt = (gi - 2) * grp + (len(groups[gi - 2]) - 1)
            deps.append(tp_steps[lastt])

        def f_sq(e, gts=gts, o=o, so=so, n=n):
            for j in range(n):
                ins = e.activation(yg[:, (o + j) * d:(o + j + 1) * d],
                                   xg[:, (o + j) * d:(o + j + 1) * d],
                                   AF.Square,
                                   accum_out=stt[:, so + grp + j:so + grp + j + 1])
            return ins
        s_sq = sc.step("scalar", f_sq, deps=deps, note=f"{tag}_sq{gi}")

        def f_sum(e, gts=gts, o=o, so=so, n=n):
            for j in range(n):
                ins = e.tensor_reduce(stt[:, so + j:so + j + 1],
                                      xg[:, (o + j) * d:(o + j + 1) * d],
                                      axis=AX.X, op=ALU.add)
            return ins
        s_sum = sc.step("vector", f_sum, deps=[ld_steps[gi]],
                        note=f"{tag}_sum{gi}")

        def f_mean(e, so=so, n=n):
            return e.tensor_scalar(stt[:, so:so + n], stt[:, so:so + n],
                                   1.0 / d, None, ALU.mult)
        s_mean = sc.step("vector", f_mean, deps=[s_sum],
                         note=f"{tag}_mean{gi}")

        def f_var1(e, so=so, n=n):
            e.tensor_scalar(stt[:, so + 3 * grp:so + 3 * grp + n],
                            stt[:, so + grp:so + grp + n],
                            1.0 / d, eps, ALU.mult, ALU.add)
            return e.tensor_tensor(stt[:, so + 2 * grp:so + 2 * grp + n],
                                   stt[:, so:so + n], stt[:, so:so + n],
                                   ALU.mult)
        s_var1 = sc.step("vector", f_var1, deps=[s_sq, s_mean],
                         note=f"{tag}_var1_{gi}")

        def f_var2(e, so=so, n=n):
            return e.tensor_sub(stt[:, so + 3 * grp:so + 3 * grp + n],
                                stt[:, so + 3 * grp:so + 3 * grp + n],
                                stt[:, so + 2 * grp:so + 2 * grp + n])
        s_var2 = sc.step("vector", f_var2, deps=[s_var1],
                         note=f"{tag}_var2_{gi}")

        def f_sqrt(e, so=so, n=n):
            return e.sqrt(stt[:, so + 4 * grp:so + 4 * grp + n],
                          stt[:, so + 3 * grp:so + 3 * grp + n])
        s_rs1 = sc.step("scalar", f_sqrt, deps=[s_var2], note=f"{tag}_sqrt{gi}")

        def f_rec(e, so=so, n=n):
            return e.reciprocal(stt[:, so + 4 * grp:so + 4 * grp + n],
                                stt[:, so + 4 * grp:so + 4 * grp + n])
        s_rs2 = sc.step("vector", f_rec, deps=[s_rs1], note=f"{tag}_rec{gi}")

        def f_y(e, o=o, so=so, n=n):
            for j in range(n):
                ins = e.tensor_scalar(yg[:, (o + j) * d:(o + j + 1) * d],
                                      xg[:, (o + j) * d:(o + j + 1) * d],
                                      stt[:, so + j:so + j + 1],
                                      stt[:, so + 4 * grp + j:so + 4 * grp + j + 1],
                                      ALU.subtract, ALU.mult)
            return ins
        s_y = sc.step("vector", f_y, deps=[s_rs2, s_sq], note=f"{tag}_y{gi}")
        if affine:
            def f_g(e, o=o, n=n):
                for j in range(n):
                    ins = e.tensor_tensor(yg[:, (o + j) * d:(o + j + 1) * d],
                                          yg[:, (o + j) * d:(o + j + 1) * d],
                                          gb[:, 0:d], ALU.mult)
                return ins
            s_y = sc.step("vector", f_g, deps=[s_y, s_gb], note=f"{tag}_yg{gi}")

            def f_b(e, o=o, n=n):
                for j in range(n):
                    ins = e.tensor_tensor(yg[:, (o + j) * d:(o + j + 1) * d],
                                          yg[:, (o + j) * d:(o + j + 1) * d],
                                          gb[:, d:2 * d], ALU.add)
                return ins
            s_y = sc.step("vector", f_b, deps=[s_y], note=f"{tag}_yb{gi}")

        if fm_d is not None:
            def f_cast(e, o=o, n=n):
                for j in range(n):
                    ins = e.tensor_copy(yb[:, (o + j) * d:(o + j + 1) * d],
                                        yg[:, (o + j) * d:(o + j + 1) * d])
                return ins
            s_y = sc.step("vector", f_cast, deps=[s_y], note=f"{tag}_cast{gi}")
        y_steps[gi] = s_y

        if tm_d is not None:
            def f_st(e, dma, gts=gts, o=o, n=n):
                for j, t in enumerate(gts):
                    dma(out=tm_d[t * 128:(t + 1) * 128, 0:d],
                        in_=yg[:, (o + j) * d:(o + j + 1) * d])
            tmst_steps[gi] = sc.dma("gpsimd", f_st, n, deps=[s_y],
                                    note=f"{tag}_tmst{gi}")
            out_steps.append(tmst_steps[gi])

        if fm_d is not None:
            for j, t in enumerate(gts):
                ti = gi * grp + j

                def f_tp(e, o=o, j=j):
                    for c in range(ncd):
                        ins = e.transpose(
                            ptp[:, c * 128:(c + 1) * 128],
                            yb[:, (o + j) * d + c * 128:(o + j) * d + (c + 1) * 128],
                            p.identb[:, :])
                    return ins
                deps = [s_y]
                if ti >= 1:
                    deps.append(cp_steps.get(ti - 1))
                tp_steps[ti] = sc.step("tensor", f_tp, deps=deps,
                                       note=f"{tag}_tp{ti}")

                fms = sb["fms"]

                def f_cp(e, ti=ti):
                    fsl = ti % 2
                    return e.tensor_copy(fms[:, fsl * d:(fsl + 1) * d],
                                         ptp[:, 0:d])
                deps = [tp_steps[ti]]
                if ti >= 2:
                    deps.append(st_steps.get(ti - 2))
                cp_steps[ti] = sc.step("vector", f_cp, deps=deps,
                                       note=f"{tag}_cp{ti}")

                def f_fmst(e, dma, t=t, ti=ti):
                    fsl = ti % 2
                    dma(out=fm_d[0:ncd, 0:128, t * 128:(t + 1) * 128]
                        .rearrange("c p x -> p c x"),
                        in_=fms[:, fsl * d:(fsl + 1) * d]
                        .rearrange("p (c x) -> p c x", c=ncd))
                st_steps[ti] = sc.dma("gpsimd", f_fmst, 1, deps=[cp_steps[ti]],
                                      note=f"{tag}_fmst{ti}")
                out_steps.append(st_steps[ti])
    return out_steps


def np_dt(dtype):
    import ml_dtypes
    return {F32: np.float32, BF16: ml_dtypes.bfloat16}[dtype]



def ph_proj(p, xfm_d, w_d, din, dout, b_d=None, out_fm_d=None, out_tm_d=None,
            act=None, w_transposed=False, tag="pj", tokens=None):
    cfg = p.cfg
    sc = p.sched
    nci, nco = din // 128, dout // 128
    T = cfg.T if tokens is None else tokens
    ngr = T // 512
    fence = sc.lasts()

    wst_nb = 1 if max(din, dout) > 2048 else 2
    with ExitStack() as es:
        wbf = es.enter_context(p.nc.sbuf_tensor(tag + "_wbf", [128, nci * dout], BF16))
        wst = es.enter_context(p.nc.sbuf_tensor(tag + "_wst", [128, wst_nb * max(din, dout)], F32))
        bcol = es.enter_context(p.nc.sbuf_tensor(tag + "_bc", [128, max(nco, 8)], F32))
        bro = es.enter_context(p.nc.sbuf_tensor(tag + "_br", [128, dout], F32))
        xg = es.enter_context(p.nc.sbuf_tensor(tag + "_xg", [128, 2 * nci * 512], BF16))
        og = es.enter_context(p.nc.sbuf_tensor(tag + "_og", [128, 4 * 512], BF16))
        s0 = es.enter_context(p.nc.sbuf_tensor(tag + "_s0", [128, 8 * 512], F32)) \
            if act == "elu1" else None
        pj = es.enter_context(p.nc.psum_tensor(tag + "_pj", [128, 4 * 512], F32))
        ptw = es.enter_context(p.nc.psum_tensor(tag + "_ptw", [128, 1024], BF16))

        wld = []
        wcast = []
        wtp = []
        if not w_transposed:
            wsl = max(din, dout)
            for k in range(nci):
                def f(e, dma, k=k):
                    dma(out=wst[:, (k % wst_nb) * wsl:(k % wst_nb) * wsl + dout],
                        in_=w_d[k * 128:(k + 1) * 128, 0:dout])
                deps = list(fence)
                if k >= wst_nb:
                    deps.append(wcast[k - wst_nb])
                wld.append(sc.dma("sync", f, 1, deps=deps, note=f"{tag}_wld{k}"))

                def fc(e, k=k):
                    return e.tensor_copy(wbf[:, k * dout:(k + 1) * dout],
                                         wst[:, (k % wst_nb) * wsl:(k % wst_nb) * wsl + dout])
                wcast.append(sc.step("vector", fc, deps=[wld[k]],
                                     note=f"{tag}_wc{k}"))
            w_ready = wcast[-1:]
        else:
            cast_tmp = es.enter_context(
                p.nc.sbuf_tensor(tag + "_wtt", [128, 2 * 1024], BF16))
            wsl = max(din, dout)
            for r in range(nco):
                def f(e, dma, r=r):
                    dma(out=wst[:, (r % 2) * wsl:(r % 2) * wsl + din],
                        in_=w_d[r * 128:(r + 1) * 128, 0:din])
                deps = list(fence)
                if r >= 2:
                    deps.append(wtp[(r - 2) * nci + nci - 1])
                wld.append(sc.dma("sync", f, 1, deps=deps, note=f"{tag}_wld{r}"))

                def fc(e, r=r):
                    return e.tensor_copy(
                        cast_tmp[:, (r % 2) * 1024:(r % 2) * 1024 + din],
                        wst[:, (r % 2) * wsl:(r % 2) * wsl + din])
                wcast.append(sc.step("vector", fc, deps=[wld[r]],
                                     note=f"{tag}_wc{r}"))
                for k in range(nci):
                    def ft(e, r=r, k=k):
                        return e.transpose(
                            ptw[:, (k % 4) * 128:(k % 4) * 128 + 128],
                            cast_tmp[:, (r % 2) * 1024 + k * 128:
                                     (r % 2) * 1024 + (k + 1) * 128],
                            p.identb[:, :])
                    deps = [wcast[r]]
                    idx = r * nci + k
                    if idx >= 1:
                        deps.append(wtp_cp[idx - 1])
                    wtp.append(sc.step("tensor", ft, deps=deps,
                                       note=f"{tag}_wtp{idx}"))

                    def fcp(e, r=r, k=k):
                        return e.tensor_copy(
                            wbf[:, k * dout + r * 128:k * dout + (r + 1) * 128],
                            ptw[:, (k % 4) * 128:(k % 4) * 128 + 128])
                    if idx == 0:
                        wtp_cp = []
                    wtp_cp.append(sc.step("vector", fcp, deps=[wtp[idx]],
                                          note=f"{tag}_wtpc{idx}"))
            w_ready = wtp_cp[-1:]

        b_ready = []
        if b_d is not None:
            if out_fm_d is not None:
                def f(e, dma):
                    for m in range(nco):
                        dma(out=bcol[:, m:m + 1],
                            in_=b_d[m * 128:(m + 1) * 128].unsqueeze(1))
                b_ready.append(sc.dma("sync", f, nco, deps=fence,
                                      note=tag + "_bcol"))
            if out_tm_d is not None:
                def f2(e, dma):
                    dma(out=bro[:, 0:dout], in_=bcast_row(b_d, 128))
                b_ready.append(sc.dma("sync", f2, 1, deps=fence,
                                      note=tag + "_brow"))

        xld = [None] * ngr
        xrel = {}
        if out_fm_d is not None:
            ost = {}
            for g in range(ngr):
                def f(e, dma, g=g):
                    sl = g % 2
                    for k in range(nci):
                        dma(out=xg[:, (sl * nci + k) * 512:(sl * nci + k + 1) * 512],
                            in_=xfm_d[k, 0:128, g * 512:(g + 1) * 512])
                deps = list(fence)
                if g >= 2:
                    deps.append(xrel[g - 2])
                xld[g] = sc.dma("sync", f, nci, deps=deps, note=f"{tag}_xld{g}")
                for m in range(nco):
                    i = g * nco + m
                    pb = i % 4

                    def fmm(e, g=g, m=m, pb=pb):
                        sl = g % 2
                        for k in range(nci):
                            ins = e.matmul(
                                pj[:, pb * 512:(pb + 1) * 512],
                                wbf[:, k * dout + m * 128:k * dout + (m + 1) * 128],
                                xg[:, (sl * nci + k) * 512:(sl * nci + k + 1) * 512],
                                start=(k == 0), stop=(k == nci - 1))
                        return ins
                    deps = [xld[g]] + w_ready
                    if i >= 4:
                        deps.append(ost[i - 4][1])
                    s_mm = sc.step("tensor", fmm, deps=deps, note=f"{tag}_mm{i}")
                    ost[i] = emit_evac_fm(p, sc, tag, i, pb, m, g, pj, og, s0,
                                          bcol, act, b_d, out_fm_d, s_mm, ost,
                                          b_ready=b_ready)
                xrel[g] = ost[g * nco + nco - 1][0]

        if out_tm_d is not None:
            nt = T // 128
            tst = {}
            tld = [None] * nt
            for t in range(nt):
                def f(e, dma, t=t):
                    sl = t % 2
                    for k in range(nci):
                        dma(out=xg[:, (sl * nci + k) * 512 + 0:(sl * nci + k) * 512 + 128]
                            if False else
                            xg[:, (sl * nci + k) * 512:(sl * nci + k) * 512 + 128],
                            in_=xfm_d[k, 0:128, t * 128:(t + 1) * 128])
                deps = list(fence)
                if t >= 2:
                    deps.append(tst[t - 2][0])
                tld[t] = sc.dma("sync", f, nci, deps=deps, note=f"{tag}_tld{t}")
                ncol = dout // 512
                pb = t % 2

                def fmm(e, t=t, pb=pb, ncol=ncol):
                    sl = t % 2
                    for k in range(nci):
                        for cgi in range(ncol):
                            ins = e.matmul(
                                pj[:, (pb * ncol + cgi) * 512:(pb * ncol + cgi + 1) * 512],
                                xg[:, (sl * nci + k) * 512:(sl * nci + k) * 512 + 128],
                                wbf[:, k * dout + cgi * 512:k * dout + (cgi + 1) * 512],
                                start=(k == 0), stop=(k == nci - 1))
                    return ins
                deps = [tld[t]] + w_ready
                if t >= 2:
                    deps.append(tst[t - 2][0])
                s_mm = sc.step("tensor", fmm, deps=deps, note=f"{tag}_tmm{t}")
                osl = t % 2

                def fev(e, t=t, pb=pb, osl=osl, ncol=ncol):
                    if b_d is not None:
                        return e.tensor_tensor(
                            og[:, osl * 1024:osl * 1024 + dout],
                            pj[:, pb * ncol * 512:pb * ncol * 512 + dout],
                            bro[:, 0:dout], ALU.add)
                    return e.tensor_copy(og[:, osl * 1024:osl * 1024 + dout],
                                         pj[:, pb * ncol * 512:pb * ncol * 512 + dout])
                deps = [s_mm] + b_ready
                if t >= 2:
                    deps.append(tst[t - 2][1])
                s_ev = sc.step("vector", fev, deps=deps, note=f"{tag}_tev{t}")

                def fst(e, dma, t=t, osl=osl):
                    dma(out=out_tm_d[t * 128:(t + 1) * 128, 0:dout],
                        in_=og[:, osl * 1024:osl * 1024 + dout])
                s_st = sc.dma("gpsimd", fst, 1, deps=[s_ev], note=f"{tag}_tst{t}")
                tst[t] = (s_st, s_ev)
    return sc.lasts()


def emit_evac_fm(p, sc, tag, i, pb, m, g, pj, og, s0, bcol, act, b_d,
                 out_fm_d, s_mm, ost, b_ready=()):
    osl = i % 4
    deps = [s_mm] + list(b_ready)
    if i >= 4:
        deps.append(ost[i - 4][1])

    if act == "elu1":
        def fA(e):
            if b_d is not None:
                return e.tensor_scalar(s0[:, osl * 512:(osl + 1) * 512],
                                       pj[:, pb * 512:(pb + 1) * 512],
                                       bcol[:, m:m + 1], None, ALU.add)
            return e.tensor_copy(s0[:, osl * 512:(osl + 1) * 512],
                                 pj[:, pb * 512:(pb + 1) * 512])
        if i >= 4:
            deps.append(ost[i - 4][2])
        sA = sc.step("vector", fA, deps=deps, note=f"{tag}_eA{i}")

        def fB(e):
            return e.tensor_scalar(s0[:, (4 + osl) * 512:(5 + osl) * 512],
                                   s0[:, osl * 512:(osl + 1) * 512],
                                   0.0, None, ALU.min)
        sB = sc.step("vector", fB, deps=[sA], note=f"{tag}_eB{i}")

        def fC(e):
            return e.activation(s0[:, (4 + osl) * 512:(5 + osl) * 512],
                                s0[:, (4 + osl) * 512:(5 + osl) * 512], AF.Exp)
        sC = sc.step("scalar", fC, deps=[sB], note=f"{tag}_eC{i}")

        def fD(e):
            return e.scalar_tensor_tensor(og[:, osl * 512:(osl + 1) * 512],
                                          s0[:, osl * 512:(osl + 1) * 512],
                                          0.0,
                                          s0[:, (4 + osl) * 512:(5 + osl) * 512],
                                          ALU.max, ALU.add)
        dD = [sC] + ([ost[i - 4][0]] if i >= 4 else [])
        s_ev = sc.step("vector", fD, deps=dD, note=f"{tag}_eD{i}")
        last_reader = sA
    else:
        AFm = {None: AF.Identity, "gelu": AF.Gelu, "relu": AF.Relu}[act]

        def f1(e):
            kw = {}
            if b_d is not None:
                kw["bias"] = bcol[:, m:m + 1]
            return e.activation(og[:, osl * 512:(osl + 1) * 512],
                                pj[:, pb * 512:(pb + 1) * 512], AFm, **kw)
        if i >= 4:
            deps.append(ost[i - 4][0])
        s_ev = sc.step("scalar", f1, deps=deps, note=f"{tag}_ev{i}")
        last_reader = s_ev

    def f2(e, dma):
        dma(out=out_fm_d[m, 0:128, g * 512:(g + 1) * 512],
            in_=og[:, osl * 512:(osl + 1) * 512])
    s_st = sc.dma("gpsimd", f2, 1, deps=[s_ev], note=f"{tag}_st{i}")
    return (s_st, last_reader, s_ev)



def ph_attn(p, qfm_d, kfm_d, vtm_d, ctxfm_d, kind, scale, tag):
    cfg = p.cfg
    sc = p.sched
    S, d, nh, dh = cfg.S, cfg.d, cfg.nh, cfg.dh
    ncd = cfg.nc_d
    fence = sc.lasts()
    if kind == "std":
        KEYS = S
        insts = [(s, h) for s in range(cfg.nseq) for h in range(nh)]
    else:
        KEYS = (d // dh) * 128
        insts = [(s, w) for s in range(cfg.nseq) for w in range(S // 128)]
    RB = KEYS // 128
    NQG = KEYS // 512
    C = KEYS // 128

    with ExitStack() as es:
        ten = lambda n, sh, dt: es.enter_context(p.nc.sbuf_tensor(tag + n, sh, dt))
        qsb = ten("_q", [128, 2 * KEYS], BF16)
        ksb = ten("_k", [128, 2 * KEYS], BF16)
        vsb = ten("_v", [128, 2 * C * 64], BF16)
        atn = ten("_at", [128, KEYS], BF16)
        att = ten("_att", [128, C * 512], BF16)
        cst = ten("_cst", [128, 2 * 512], BF16)
        stt = ten("_stt", [128, 4 * RB], F32)
        psc = es.enter_context(p.nc.psum_tensor(tag + "_psc", [128, KEYS], F32))
        ptp = es.enter_context(p.nc.psum_tensor(tag + "_ptp", [128, 512], BF16))
        pcx = es.enter_context(p.nc.psum_tensor(tag + "_pcx", [128, 512], F32))

        prev_inst_rel = {}
        prev_ctx_mm = None
        prev_exp = None
        prev_norm = None
        prev_tp_last = None
        prev_cst = {}

        for ii, (s, hw) in enumerate(insts):
            sl = ii % 2
            qo = sl * KEYS
            vo = sl * C * 64
            if kind == "std":
                def f_q(e, dma, s=s, h=hw, qo=qo):
                    dma(out=qsb[0:64, qo:qo + S],
                        in_=qfm_d[h // 2, (h % 2) * 64:(h % 2) * 64 + 64,
                                  s * S:(s + 1) * S])
                    dma(out=ksb[0:64, qo:qo + S],
                        in_=kfm_d[h // 2, (h % 2) * 64:(h % 2) * 64 + 64,
                                  s * S:(s + 1) * S])

                def f_v(e, dma, s=s, h=hw, vo=vo):
                    q4 = max(C // 4, 1)
                    for pc in range(0, C, q4):
                        dma(out=vsb[:, vo + pc * 64:vo + (pc + q4) * 64],
                            in_=vtm_d[s * S + pc * 128:s * S + (pc + q4) * 128,
                                      h * 64:(h + 1) * 64]
                            .rearrange("(c p) x -> p c x", p=128))
            else:
                def f_q(e, dma, s=s, w=hw, qo=qo):
                    dma(out=qsb[0:64, qo:qo + KEYS]
                        .rearrange("p (c h x) -> p c h x", c=ncd, h=2),
                        in_=qfm_d[0:ncd, 0:128,
                                  s * S + w * 128:s * S + (w + 1) * 128]
                        .rearrange("c (h p) x -> p c h x", h=2))
                    dma(out=ksb[0:64, qo:qo + KEYS]
                        .rearrange("p (c h x) -> p c h x", c=ncd, h=2),
                        in_=kfm_d[0:ncd, 0:128,
                                  s * S + w * 128:s * S + (w + 1) * 128]
                        .rearrange("c (h p) x -> p c h x", h=2))

                def f_v(e, dma, s=s, w=hw, vo=vo):
                    dma(out=vsb[:, vo:vo + C * 64],
                        in_=vtm_d[s * S + w * 128:s * S + (w + 1) * 128, 0:d])
            deps = list(fence)
            if ii >= 2:
                deps.append(prev_inst_rel[ii - 2])
            s_ldq = sc.dma("sync", f_q, 2, deps=deps, note=f"{tag}_ldq{ii}")
            nv = (4 if C >= 4 else C) if kind == "std" else 1
            s_ldv = sc.dma("sync", f_v, nv, deps=deps, note=f"{tag}_ldv{ii}")

            def lhsT_ap(rb, qo=qo):
                return qsb[0:64, qo + rb * 128:qo + (rb + 1) * 128]

            last_ctx_mm_of_inst = None
            for qg in range(NQG):
                tp_steps = []
                for rbi in range(4):
                    rb = qg * 4 + rbi
                    def f_s(e, rb=rb, qo=qo, lhsT_ap=lhsT_ap):
                        for cg in range(KEYS // 512):
                            ins = e.matmul(psc[:, cg * 512:(cg + 1) * 512],
                                           lhsT_ap(rb),
                                           ksb[0:64, qo + cg * 512:qo + (cg + 1) * 512],
                                           start=True, stop=True)
                        return ins
                    deps = [s_ldq]
                    if prev_exp is not None:
                        deps.append(prev_exp)
                    s_s = sc.step("tensor", f_s, deps=deps, note=f"{tag}_s{ii}_{rb}")

                    def f_m(e, rb=rb):
                        return e.reduce_max(stt[:, rb:rb + 1], psc[:, 0:KEYS],
                                            axis=AX.X, negate=True)
                    s_m = sc.step("vector", f_m, deps=[s_s], note=f"{tag}_m{ii}_{rb}")

                    def f_ms(e, rb=rb):
                        return e.tensor_scalar(stt[:, RB + rb:RB + rb + 1],
                                               stt[:, rb:rb + 1], scale, None,
                                               ALU.mult)
                    s_ms = sc.step("vector", f_ms, deps=[s_m], note=f"{tag}_ms{ii}_{rb}")

                    def f_e(e, rb=rb):
                        return e.activation(atn[:, 0:KEYS], psc[:, 0:KEYS],
                                            AF.Exp, bias=stt[:, RB + rb:RB + rb + 1],
                                            scale=scale,
                                            accum_out=stt[:, 2 * RB + rb:2 * RB + rb + 1])
                    deps = [s_ms]
                    if prev_norm is not None:
                        deps.append(prev_norm)
                    if prev_tp_last is not None:
                        deps.append(prev_tp_last)
                    s_e = sc.step("scalar", f_e, deps=deps, note=f"{tag}_e{ii}_{rb}")
                    prev_exp = s_e

                    def f_r(e, rb=rb):
                        return e.reciprocal(stt[:, 3 * RB + rb:3 * RB + rb + 1],
                                            stt[:, 2 * RB + rb:2 * RB + rb + 1])
                    s_r = sc.step("vector", f_r, deps=[s_e], note=f"{tag}_r{ii}_{rb}")

                    def f_n(e, rb=rb):
                        return e.tensor_scalar(atn[:, 0:KEYS], atn[:, 0:KEYS],
                                               stt[:, 3 * RB + rb:3 * RB + rb + 1],
                                               None, ALU.mult)
                    s_n = sc.step("vector", f_n, deps=[s_r], note=f"{tag}_n{ii}_{rb}")
                    prev_norm = s_n

                    prev_cp = None
                    for tg in range(C // 4):
                        def f_t(e, tg=tg):
                            for j in range(4):
                                kc = tg * 4 + j
                                ins = e.transpose(ptp[:, j * 128:(j + 1) * 128],
                                                  atn[:, kc * 128:(kc + 1) * 128],
                                                  p.identb[:, :])
                            return ins
                        deps = [s_n]
                        if prev_cp is not None:
                            deps.append(prev_cp)
                        elif prev_ctx_mm is not None:
                            deps.append(prev_ctx_mm)
                        s_t = sc.step("tensor", f_t, deps=deps,
                                      note=f"{tag}_t{ii}_{rb}_{tg}")

                        def f_c(e, tg=tg, rbi=rbi):
                            return e.tensor_copy(
                                att[:, 0:512]
                                if False else
                                att.ap()[:, :].rearrange(
                                    "p (c x) -> p c x", c=C)[:, tg * 4:tg * 4 + 4,
                                                            rbi * 128:(rbi + 1) * 128],
                                ptp[:, 0:512].rearrange("p (c x) -> p c x", c=4))
                        deps2 = [s_t]
                        if prev_ctx_mm is not None:
                            deps2.append(prev_ctx_mm)
                        s_c = sc.step("vector", f_c, deps=deps2,
                                      note=f"{tag}_c{ii}_{rb}_{tg}")
                        prev_cp = s_c
                        prev_tp_last = s_t
                        tp_steps.append(s_c)

                def f_x(e, vo=vo):
                    for kc in range(C):
                        ins = e.matmul(pcx[0:64, 0:512],
                                       vsb[:, vo + kc * 64:vo + (kc + 1) * 64],
                                       att[:, kc * 512:(kc + 1) * 512],
                                       start=(kc == 0), stop=(kc == C - 1))
                    return ins
                deps = tp_steps + [s_ldv]
                if prev_cst.get(qg % 2) is not None:
                    deps.append(prev_cst[qg % 2])
                s_x = sc.step("tensor", f_x, deps=deps, note=f"{tag}_x{ii}_{qg}")
                prev_ctx_mm = s_x
                last_ctx_mm_of_inst = s_x

                csl = qg % 2

                def f_cc(e, csl=csl):
                    return e.tensor_copy(cst[0:64, csl * 512:(csl + 1) * 512],
                                         pcx[0:64, 0:512])
                s_cc = sc.step("vector", f_cc, deps=[s_x], note=f"{tag}_cc{ii}_{qg}")

                if kind == "std":
                    def f_st(e, dma, s=s, h=hw, qg=qg, csl=csl):
                        dma(out=ctxfm_d[h // 2, (h % 2) * 64:(h % 2) * 64 + 64,
                                        s * S + qg * 512:s * S + (qg + 1) * 512],
                            in_=cst[0:64, csl * 512:(csl + 1) * 512])
                    s_stc = sc.dma("gpsimd", f_st, 1, deps=[s_cc],
                                   note=f"{tag}_st{ii}_{qg}")
                else:
                    def f_st(e, dma, s=s, w=hw, qg=qg, csl=csl):
                        for gi in range(4):
                            g = qg * 4 + gi
                            dma(out=ctxfm_d[g // 2, (g % 2) * 64:(g % 2) * 64 + 64,
                                            s * S + w * 128:s * S + (w + 1) * 128],
                                in_=cst[0:64, csl * 512 + gi * 128:csl * 512 + (gi + 1) * 128])
                    s_stc = sc.dma("gpsimd", f_st, 4, deps=[s_cc],
                                   note=f"{tag}_st{ii}_{qg}")
                prev_cst[csl] = s_stc
            prev_inst_rel[ii] = last_ctx_mm_of_inst
    return sc.lasts()



def ph_fm2tm(p, src_fm_d, dst_tm_d, dt=F32, tag="f2t"):
    cfg = p.cfg
    sc = p.sched
    d, ncd, nt = cfg.d, cfg.nc_d, cfg.nt
    fence = sc.lasts()
    with ExitStack() as es:
        xin = es.enter_context(p.nc.sbuf_tensor(tag + "_x", [128, 2 * d], dt))
        out = es.enter_context(p.nc.sbuf_tensor(tag + "_o", [128, 2 * d], F32))
        ptt = es.enter_context(p.nc.psum_tensor(tag + "_pt", [128, 2 * 512], dt))
        ident = p.identb if dt == BF16 else p.identf
        prev_cp = {}
        prev_st = {}
        prev_tp = {}
        for t in range(nt):
            sl = t % 2

            def f_ld(e, dma, t=t, sl=sl):
                dma(out=xin[:, sl * d:(sl + 1) * d]
                    .rearrange("p (c x) -> p c x", c=ncd),
                    in_=src_fm_d[0:ncd, 0:128, t * 128:(t + 1) * 128]
                    .rearrange("c p x -> p c x"))
            deps = list(fence)
            if t >= 2:
                deps.append(prev_tp[t - 2])
            s_ld = sc.dma("sync", f_ld, 1, deps=deps, note=f"{tag}_ld{t}")

            half_steps = []
            last_cp = prev_cp.get((t - 1, 1)) if t >= 1 else None
            for hh in range(2):
                def f_tp(e, t=t, sl=sl, hh=hh):
                    for c in range(4):
                        ins = e.transpose(
                            ptt[:, hh * 512 + c * 128:hh * 512 + (c + 1) * 128],
                            xin[:, sl * d + (hh * 4 + c) * 128:sl * d + (hh * 4 + c + 1) * 128],
                            ident[:, :])
                    return ins
                deps = [s_ld]
                if hh == 1:
                    deps.append(prev_cp[(t, 0)])
                elif last_cp is not None:
                    deps.append(last_cp)
                s_tp = sc.step("tensor", f_tp, deps=deps, note=f"{tag}_tp{t}_{hh}")
                prev_tp[t] = s_tp

                def f_cp(e, t=t, sl=sl, hh=hh):
                    return e.tensor_copy(out[:, sl * d + hh * 512:sl * d + (hh + 1) * 512],
                                         ptt[:, hh * 512:(hh + 1) * 512])
                deps = [s_tp]
                if t >= 2:
                    deps.append(prev_st[t - 2])
                s_cp = sc.step("vector", f_cp, deps=deps, note=f"{tag}_cp{t}_{hh}")
                prev_cp[(t, hh)] = s_cp
                half_steps.append(s_cp)

            def f_st(e, dma, t=t, sl=sl):
                dma(out=dst_tm_d[t * 128:(t + 1) * 128, 0:d],
                    in_=out[:, sl * d:(sl + 1) * d])
            prev_st[t] = sc.dma("gpsimd", f_st, 1, deps=half_steps,
                                note=f"{tag}_st{t}")
    return sc.lasts()



def ph_resid(p, mode, x_d, a_d, y_d=None, tm_d=None, fm_d=None, coef=1.0,
             eps=1e-5, eps2=1e-5, g_d=None, b_d=None, adt=BF16, tag="rs"):
    cfg = p.cfg
    sc = p.sched
    d, ncd, nt = cfg.d, cfg.nc_d, cfg.nt
    fence = sc.lasts()
    grp = 4
    groups = [list(range(i, min(i + grp, nt))) for i in range(0, nt, grp)]
    with ExitStack() as es:
        ten = lambda n, sh, dt: es.enter_context(p.nc.sbuf_tensor(tag + n, sh, dt))
        xg = ten("_x", [128, 2 * grp * d], F32)
        ag = ten("_a", [128, 2 * grp * d], adt)
        ug = ten("_u", [128, 2 * grp * d], F32)
        yg = ten("_y", [128, 2 * grp * d], F32)
        yb = ten("_yb", [128, 2 * grp * d], BF16) if fm_d is not None else None
        fms = ten("_fms", [128, 2 * d], BF16) if fm_d is not None else None
        stt = ten("_stt", [128, 2 * 12 * grp], F32)
        gb = ten("_gb", [128, 2 * d], F32) if g_d is not None else None
        ptp = es.enter_context(p.nc.psum_tensor(tag + "_ptp", [128, 1024], BF16)) \
            if fm_d is not None else None

        if g_d is not None:
            def f(e, dma):
                dma(out=gb[:, 0:d], in_=bcast_row(g_d, 128))
                dma(out=gb[:, d:2 * d], in_=bcast_row(b_d, 128))
            s_gb = sc.dma("sync", f, 2, deps=fence, note=tag + "_gb")
        else:
            s_gb = None

        y_done = [None] * len(groups)
        st_done = [None] * len(groups)
        tp_steps = {}
        cp_steps = {}
        fmst_steps = {}
        for gi, gts in enumerate(groups):
            sl = gi % 2
            o, so = sl * grp, sl * 12 * grp
            n = len(gts)

            def f_ld(e, dma, gts=gts, o=o):
                for j, t in enumerate(gts):
                    dma(out=xg[:, (o + j) * d:(o + j + 1) * d],
                        in_=x_d[t * 128:(t + 1) * 128, 0:d])
                    dma(out=ag[:, (o + j) * d:(o + j + 1) * d],
                        in_=a_d[t * 128:(t + 1) * 128, 0:d])
            deps = list(fence)
            if gi >= 2:
                deps.append(y_done[gi - 2])
                deps.append(st_done[gi - 2])
                if fm_d is not None:
                    lastt = (gi - 2) * grp + len(groups[gi - 2]) - 1
                    deps.append(tp_steps[lastt])
            s_ld = sc.dma("sync", f_ld, 2 * n, deps=deps, note=f"{tag}_ld{gi}")

            def ln_stats(src_buf, so_m, so_ssq, so_var, so_rstd, dep, eps_v,
                         sq_scratch, so_msq=None, nn=n, oo=o):
                if so_msq is None:
                    so_msq = so_rstd + grp * 4
                def f_sq(e):
                    for j in range(nn):
                        ins = e.activation(sq_scratch[:, (oo + j) * d:(oo + j + 1) * d],
                                           src_buf[:, (oo + j) * d:(oo + j + 1) * d],
                                           AF.Square,
                                           accum_out=stt[:, so_ssq + j:so_ssq + j + 1])
                    return ins
                s_sq = sc.step("scalar", f_sq, deps=dep, note=f"{tag}_sq")

                def f_sum(e):
                    for j in range(nn):
                        ins = e.tensor_reduce(stt[:, so_m + j:so_m + j + 1],
                                              src_buf[:, (oo + j) * d:(oo + j + 1) * d],
                                              axis=AX.X, op=ALU.add)
                    return ins
                s_sum = sc.step("vector", f_sum, deps=dep, note=f"{tag}_sm")

                def f_mean(e):
                    return e.tensor_scalar(stt[:, so_m:so_m + nn],
                                           stt[:, so_m:so_m + nn],
                                           1.0 / d, None, ALU.mult)
                s_mean = sc.step("vector", f_mean, deps=[s_sum], note=f"{tag}_mn")

                def f_v1(e):
                    e.tensor_scalar(stt[:, so_var:so_var + nn],
                                    stt[:, so_ssq:so_ssq + nn],
                                    1.0 / d, eps_v, ALU.mult, ALU.add)
                    return e.tensor_tensor(stt[:, so_msq:so_msq + nn],
                                           stt[:, so_m:so_m + nn],
                                           stt[:, so_m:so_m + nn], ALU.mult)
                s_v1 = sc.step("vector", f_v1, deps=[s_sq, s_mean],
                               note=f"{tag}_v1")

                def f_v2(e):
                    return e.tensor_sub(stt[:, so_var:so_var + nn],
                                        stt[:, so_var:so_var + nn],
                                        stt[:, so_msq:so_msq + nn])
                s_v2 = sc.step("vector", f_v2, deps=[s_v1], note=f"{tag}_v2")

                def f_sqrt(e):
                    return e.sqrt(stt[:, so_rstd:so_rstd + nn],
                                  stt[:, so_var:so_var + nn])
                s_q = sc.step("scalar", f_sqrt, deps=[s_v2], note=f"{tag}_q")

                def f_rec(e):
                    return e.reciprocal(stt[:, so_rstd:so_rstd + nn],
                                        stt[:, so_rstd:so_rstd + nn])
                return sc.step("vector", f_rec, deps=[s_q], note=f"{tag}_rc")

            if mode == "sum_ln":
                s_r1 = ln_stats(xg, so, so + grp, so + 2 * grp, so + 3 * grp,
                                [s_ld], eps, ug)

                def f_u(e, o=o, so=so, n=n):
                    for j in range(n):
                        ins = e.tensor_scalar(ug[:, (o + j) * d:(o + j + 1) * d],
                                              xg[:, (o + j) * d:(o + j + 1) * d],
                                              stt[:, so + j:so + j + 1],
                                              stt[:, so + 3 * grp + j:so + 3 * grp + j + 1],
                                              ALU.subtract, ALU.mult)
                    return ins
                s_u = sc.step("vector", f_u, deps=[s_r1], note=f"{tag}_u{gi}")

                def f_u2(e, o=o, n=n):
                    for j in range(n):
                        ins = e.tensor_tensor(ug[:, (o + j) * d:(o + j + 1) * d],
                                              ug[:, (o + j) * d:(o + j + 1) * d],
                                              ag[:, (o + j) * d:(o + j + 1) * d],
                                              ALU.add)
                    return ins
                s_u2 = sc.step("vector", f_u2, deps=[s_u, s_ld], note=f"{tag}_u2{gi}")
                s_r2 = ln_stats(ug, so + 4 * grp, so + 5 * grp, so + 6 * grp,
                                so + 7 * grp, [s_u2], eps2, yg)

                def f_y(e, o=o, so=so, n=n):
                    for j in range(n):
                        ins = e.tensor_scalar(yg[:, (o + j) * d:(o + j + 1) * d],
                                              ug[:, (o + j) * d:(o + j + 1) * d],
                                              stt[:, so + 4 * grp + j:so + 4 * grp + j + 1],
                                              stt[:, so + 7 * grp + j:so + 7 * grp + j + 1],
                                              ALU.subtract, ALU.mult)
                    return ins
                s_y = sc.step("vector", f_y, deps=[s_r2], note=f"{tag}_y{gi}")

                def f_g(e, o=o, n=n):
                    for j in range(n):
                        ins = e.tensor_tensor(yg[:, (o + j) * d:(o + j + 1) * d],
                                              yg[:, (o + j) * d:(o + j + 1) * d],
                                              gb[:, 0:d], ALU.mult)
                    return ins
                s_y = sc.step("vector", f_g, deps=[s_y, s_gb], note=f"{tag}_g{gi}")

                def f_b(e, o=o, n=n):
                    for j in range(n):
                        ins = e.tensor_tensor(yg[:, (o + j) * d:(o + j + 1) * d],
                                              yg[:, (o + j) * d:(o + j + 1) * d],
                                              gb[:, d:2 * d], ALU.add)
                    return ins
                s_y = sc.step("vector", f_b, deps=[s_y], note=f"{tag}_b{gi}")
                out_src = yg
                store_d = tm_d
            elif mode == "x_cln":
                def cast_a(e, o=o, n=n):
                    for j in range(n):
                        ins = e.tensor_copy(yg[:, (o + j) * d:(o + j + 1) * d],
                                            ag[:, (o + j) * d:(o + j + 1) * d])
                    return ins
                s_ca = sc.step("vector", cast_a, deps=[s_ld], note=f"{tag}_ca{gi}")
                s_r1 = ln_stats(yg, so, so + grp, so + 2 * grp, so + 3 * grp,
                                [s_ca], eps, ug)

                def f_y(e, o=o, so=so, n=n):
                    for j in range(n):
                        ins = e.tensor_scalar(yg[:, (o + j) * d:(o + j + 1) * d],
                                              yg[:, (o + j) * d:(o + j + 1) * d],
                                              stt[:, so + j:so + j + 1],
                                              stt[:, so + 3 * grp + j:so + 3 * grp + j + 1],
                                              ALU.subtract, ALU.mult)
                    return ins
                s_y = sc.step("vector", f_y, deps=[s_r1], note=f"{tag}_y{gi}")

                def f_g(e, o=o, n=n):
                    for j in range(n):
                        ins = e.tensor_tensor(yg[:, (o + j) * d:(o + j + 1) * d],
                                              yg[:, (o + j) * d:(o + j + 1) * d],
                                              gb[:, 0:d], ALU.mult)
                    return ins
                s_y = sc.step("vector", f_g, deps=[s_y, s_gb], note=f"{tag}_g{gi}")

                def f_b(e, o=o, n=n):
                    for j in range(n):
                        ins = e.tensor_tensor(yg[:, (o + j) * d:(o + j + 1) * d],
                                              yg[:, (o + j) * d:(o + j + 1) * d],
                                              gb[:, d:2 * d], ALU.add)
                    return ins
                s_y = sc.step("vector", f_b, deps=[s_y], note=f"{tag}_b{gi}")

                def f_fin(e, o=o, n=n):
                    for j in range(n):
                        ins = e.scalar_tensor_tensor(
                            yg[:, (o + j) * d:(o + j + 1) * d],
                            yg[:, (o + j) * d:(o + j + 1) * d], coef,
                            xg[:, (o + j) * d:(o + j + 1) * d],
                            ALU.mult, ALU.add)
                    return ins
                s_y = sc.step("vector", f_fin, deps=[s_y], note=f"{tag}_f{gi}")
                out_src = yg
                store_d = y_d
            elif mode == "x_ca":
                def f_fin(e, o=o, n=n):
                    for j in range(n):
                        ins = e.scalar_tensor_tensor(
                            yg[:, (o + j) * d:(o + j + 1) * d],
                            ag[:, (o + j) * d:(o + j + 1) * d], coef,
                            xg[:, (o + j) * d:(o + j + 1) * d],
                            ALU.mult, ALU.add)
                    return ins
                s_y = sc.step("vector", f_fin, deps=[s_ld], note=f"{tag}_f{gi}")
                out_src = yg
                store_d = y_d
            else:
                raise ValueError(mode)
            y_done[gi] = s_y

            def f_st(e, dma, gts=gts, o=o, out_src=out_src, store_d=store_d):
                for j, t in enumerate(gts):
                    dma(out=store_d[t * 128:(t + 1) * 128, 0:d],
                        in_=out_src[:, (o + j) * d:(o + j + 1) * d])
            st_done[gi] = sc.dma("gpsimd", f_st, n, deps=[s_y],
                                 note=f"{tag}_st{gi}")

            if fm_d is not None:
                def f_cast(e, o=o, n=n):
                    for j in range(n):
                        ins = e.tensor_copy(yb[:, (o + j) * d:(o + j + 1) * d],
                                            yg[:, (o + j) * d:(o + j + 1) * d])
                    return ins
                s_cb = sc.step("vector", f_cast, deps=[s_y], note=f"{tag}_cb{gi}")
                for j, t in enumerate(gts):
                    ti = gi * grp + j

                    def f_tp(e, o=o, j=j):
                        for c in range(ncd):
                            ins = e.transpose(
                                ptp[:, c * 128:(c + 1) * 128],
                                yb[:, (o + j) * d + c * 128:(o + j) * d + (c + 1) * 128],
                                p.identb[:, :])
                        return ins
                    deps = [s_cb]
                    if ti >= 1:
                        deps.append(cp_steps.get(ti - 1))
                    tp_steps[ti] = sc.step("tensor", f_tp, deps=deps,
                                           note=f"{tag}_tp{ti}")

                    def f_cp(e, ti=ti):
                        fsl = ti % 2
                        return e.tensor_copy(fms[:, fsl * d:(fsl + 1) * d],
                                             ptp[:, 0:d])
                    deps = [tp_steps[ti]]
                    if ti >= 2:
                        deps.append(fmst_steps.get(ti - 2))
                    cp_steps[ti] = sc.step("vector", f_cp, deps=deps,
                                           note=f"{tag}_cp{ti}")

                    def f_fmst(e, dma, t=t, ti=ti):
                        fsl = ti % 2
                        dma(out=fm_d[0:ncd, 0:128, t * 128:(t + 1) * 128]
                            .rearrange("c p x -> p c x"),
                            in_=fms[:, fsl * d:(fsl + 1) * d]
                            .rearrange("p (c x) -> p c x", c=ncd))
                    fmst_steps[ti] = sc.dma("gpsimd", f_fmst, 1,
                                            deps=[cp_steps[ti]],
                                            note=f"{tag}_fmst{ti}")
    return sc.lasts()



def ph_logits(p, src_tm_d, w_d, E, b_d=None, ln_first=False, mode="store",
              out_d=None, w_is_ExD=False, eps=1e-5, ntiles=None, cap=None,
              tag="lg"):
    cfg = p.cfg
    sc = p.sched
    d, ncd = cfg.d, cfg.nc_d
    nt = cfg.nt if ntiles is None else ntiles
    fence = sc.lasts()
    with ExitStack() as es:
        ten = lambda n, sh, dt: es.enter_context(p.nc.sbuf_tensor(tag + n, sh, dt))
        xg = ten("_x", [128, 2 * d], F32)
        xt = ten("_t", [128, 2 * d], F32)
        wr = ten("_w", [128, ncd * E], F32)
        ro = ten("_r", [128, 2 * E], F32)
        lg = ten("_l", [128, 2 * E], F32)
        sg = ten("_s", [128, 10 * E], F32)
        io = ten("_io", [128, 2 * E], F32)
        equ = ten("_equ", [128, 2 * E], mybir.dt.uint8)
        prevd = {}
        stt = ten("_st", [128, 16], F32)
        ptt = es.enter_context(p.nc.psum_tensor(tag + "_pt", [128, 1024], F32))
        plg = es.enter_context(p.nc.psum_tensor(tag + "_pl", [128, 2 * E], F32))

        def f_w(e, dma):
            for k in range(ncd):
                if w_is_ExD:
                    dma(out=wr[:, k * E:(k + 1) * E],
                        in_=w_d.ap().transpose([1, 0])[k * 128:(k + 1) * 128, 0:E],
                        allow_slow_non_contiguous=True)
                else:
                    dma(out=wr[:, k * E:(k + 1) * E],
                        in_=w_d[k * 128:(k + 1) * 128, 0:E])
        s_w = sc.dma("sync", f_w, ncd, deps=fence, note=tag + "_w")
        pre = [s_w]
        if b_d is not None:
            def f_b(e, dma):
                dma(out=ro[:, 0:E], in_=bcast_row(b_d, 128))
            pre.append(sc.dma("sync", f_b, 1, deps=fence, note=tag + "_b"))
        if mode == "l0top2":
            p.extra_inputs.setdefault("c_iota8", np.arange(8, dtype=np.float32))
            iota_d = p.dram_const("c_iota8", [8], F32)

            def f_i(e, dma):
                dma(out=io[:, 0:E], in_=bcast_row(iota_d, 128))
            pre.append(sc.dma("sync", f_i, 1, deps=fence, note=tag + "_i"))

            def f_big(e):
                return e.memset(io[:, E:2 * E], 1.0e9)
            pre.append(sc.step("vector", f_big, deps=fence, note=tag + "_bg"))

        prev = {}
        for t in range(nt):
            sl = t % 2

            def f_ld(e, dma, t=t, sl=sl):
                dma(out=xg[:, sl * d:(sl + 1) * d],
                    in_=src_tm_d[t * 128:(t + 1) * 128, 0:d])
            deps = fence if t < 2 else [prev[(t - 2, "done")]]
            s_ld = sc.dma("sync", f_ld, 1, deps=list(deps), note=f"{tag}_ld{t}")

            src_buf, src_dep = xg, s_ld
            if ln_first:
                def f_sq(e, sl=sl):
                    return e.activation(xt[:, sl * d:(sl + 1) * d],
                                        xg[:, sl * d:(sl + 1) * d], AF.Square,
                                        accum_out=stt[:, sl * 8 + 1:sl * 8 + 2])
                dd = [s_ld] + ([prev[(t - 2, "tp")]] if t >= 2 else [])
                s_sq = sc.step("scalar", f_sq, deps=dd, note=f"{tag}_sq{t}")

                def f_sm(e, sl=sl):
                    return e.tensor_reduce(stt[:, sl * 8:sl * 8 + 1],
                                           xg[:, sl * d:(sl + 1) * d],
                                           axis=AX.X, op=ALU.add)
                s_sm = sc.step("vector", f_sm, deps=[s_ld], note=f"{tag}_sm{t}")

                def f_v(e, sl=sl):
                    e.tensor_scalar(stt[:, sl * 8:sl * 8 + 1],
                                    stt[:, sl * 8:sl * 8 + 1], 1.0 / d, None,
                                    ALU.mult)
                    return e.tensor_scalar(stt[:, sl * 8 + 1:sl * 8 + 2],
                                           stt[:, sl * 8 + 1:sl * 8 + 2],
                                           1.0 / d, eps, ALU.mult, ALU.add)
                s_v = sc.step("vector", f_v, deps=[s_sm, s_sq], note=f"{tag}_v{t}")

                def f_v2(e, sl=sl):
                    return e.scalar_tensor_tensor(
                        stt[:, sl * 8 + 2:sl * 8 + 3],
                        stt[:, sl * 8:sl * 8 + 1], 1.0,
                        stt[:, sl * 8:sl * 8 + 1], ALU.mult, ALU.mult)
                s_v2 = sc.step("vector", f_v2, deps=[s_v], note=f"{tag}_v2{t}")

                def f_v3(e, sl=sl):
                    return e.tensor_sub(stt[:, sl * 8 + 1:sl * 8 + 2],
                                        stt[:, sl * 8 + 1:sl * 8 + 2],
                                        stt[:, sl * 8 + 2:sl * 8 + 3])
                s_v3 = sc.step("vector", f_v3, deps=[s_v2], note=f"{tag}_v3{t}")

                def f_q(e, sl=sl):
                    return e.sqrt(stt[:, sl * 8 + 3:sl * 8 + 4],
                                  stt[:, sl * 8 + 1:sl * 8 + 2])
                s_q = sc.step("scalar", f_q, deps=[s_v3], note=f"{tag}_qq{t}")

                def f_rc(e, sl=sl):
                    return e.reciprocal(stt[:, sl * 8 + 3:sl * 8 + 4],
                                        stt[:, sl * 8 + 3:sl * 8 + 4])
                s_rc = sc.step("vector", f_rc, deps=[s_q], note=f"{tag}_rc{t}")

                def f_y(e, sl=sl):
                    return e.tensor_scalar(xg[:, sl * d:(sl + 1) * d],
                                           xg[:, sl * d:(sl + 1) * d],
                                           stt[:, sl * 8:sl * 8 + 1],
                                           stt[:, sl * 8 + 3:sl * 8 + 4],
                                           ALU.subtract, ALU.mult)
                    return ins
                src_dep = sc.step("vector", f_y, deps=[s_rc], note=f"{tag}_y{t}")

            tp_cp = []
            for hh in range(2):
                def f_tp(e, sl=sl, hh=hh):
                    for c in range(4):
                        ins = e.transpose(
                            ptt[:, hh * 512 + c * 128:hh * 512 + (c + 1) * 128],
                            xg[:, sl * d + (hh * 4 + c) * 128:sl * d + (hh * 4 + c + 1) * 128],
                            p.identf[:, :])
                    return ins
                dd = [src_dep]
                if t >= 1:
                    dd.append(prev.get((t - 1, "cp%d" % hh)))
                s_tp = sc.step("tensor", f_tp, deps=dd, note=f"{tag}_tp{t}_{hh}")
                prev[(t, "tp")] = s_tp

                def f_cp(e, sl=sl, hh=hh):
                    return e.tensor_copy(xt[:, sl * d + hh * 512:sl * d + (hh + 1) * 512],
                                         ptt[:, hh * 512:(hh + 1) * 512])
                dd = [s_tp]
                if t >= 2:
                    dd.append(prev[(t - 2, "mm")])
                s_cp = sc.step("vector", f_cp, deps=dd, note=f"{tag}_cp{t}_{hh}")
                prev[(t, "cp%d" % hh)] = s_cp
                tp_cp.append(s_cp)

            def f_mm(e, sl=sl):
                for k in range(ncd):
                    ins = e.matmul(plg[:, sl * E:(sl + 1) * E],
                                   xt[:, sl * d + k * 128:sl * d + (k + 1) * 128],
                                   wr[:, k * E:(k + 1) * E],
                                   start=(k == 0), stop=(k == ncd - 1))
                return ins
            dd = tp_cp + pre
            if t >= 2:
                dd.append(prev[(t - 2, "lg")])
            s_mm = sc.step("tensor", f_mm, deps=dd, note=f"{tag}_mm{t}")
            prev[(t, "mm")] = s_mm

            def f_lg(e, sl=sl):
                if b_d is not None:
                    return e.tensor_tensor(lg[:, sl * E:(sl + 1) * E],
                                           plg[:, sl * E:(sl + 1) * E],
                                           ro[:, 0:E], ALU.add)
                return e.tensor_copy(lg[:, sl * E:(sl + 1) * E],
                                     plg[:, sl * E:(sl + 1) * E])
            dd = [s_mm] + ([prev[(t - 2, "done")]] if t >= 2 else [])
            s_lg = sc.step("vector", f_lg, deps=dd, note=f"{tag}_lg{t}")
            prev[(t, "lg")] = s_lg

            if mode == "store":
                def f_st(e, dma, t=t, sl=sl):
                    dma(out=out_d[t * 128:(t + 1) * 128, 0:E],
                        in_=lg[:, sl * E:(sl + 1) * E])
                s_done = sc.dma("gpsimd", f_st, 1, deps=[s_lg],
                                note=f"{tag}_st{t}")
            else:
                s_done = emit_top2(p, sc, tag, t, sl, E, lg, sg, io, stt, out_d,
                                   s_lg, prev, equ=equ, prevd=prevd)
            prev[(t, "done")] = s_done
    return sc.lasts()


def emit_top2(p, sc, tag, t, sl, E, lg, sg, io, stt, gates_d, s_lg, prev,
              equ=None, prevd=None):
    so = sl * 5 * E
    base = 8 if False else 8
    c0 = sl * 4 + 0 + 8

    def f_nm(e):
        return e.reduce_max(stt[:, c0:c0 + 1], lg[:, sl * E:(sl + 1) * E],
                            axis=AX.X, negate=True)
    dd = [s_lg] + ([prev[(t - 2, "done")]] if t >= 2 else [])
    s_nm = sc.step("vector", f_nm, deps=dd, note=f"{tag}_g1_{t}")

    def f_ex(e):
        return e.activation(sg[:, so:so + E], lg[:, sl * E:(sl + 1) * E],
                            AF.Exp, bias=stt[:, c0:c0 + 1],
                            accum_out=stt[:, c0 + 1:c0 + 2])
    s_ex = sc.step("scalar", f_ex, deps=[s_nm], note=f"{tag}_g2_{t}")

    def f_rz(e):
        return e.reciprocal(stt[:, c0 + 1:c0 + 2], stt[:, c0 + 1:c0 + 2])
    s_rz = sc.step("vector", f_rz, deps=[s_ex], note=f"{tag}_g3_{t}")

    def f_pr(e):
        return e.tensor_scalar(sg[:, so:so + E], sg[:, so:so + E],
                               stt[:, c0 + 1:c0 + 2], None, ALU.mult)
    s_pr = sc.step("vector", f_pr, deps=[s_rz], note=f"{tag}_g4_{t}")

    def f_mx(e):
        return e.tensor_scalar(stt[:, c0 + 2:c0 + 3], stt[:, c0:c0 + 1],
                               -1.0, None, ALU.mult)
    s_mx = sc.step("vector", f_mx, deps=[s_nm], note=f"{tag}_g5_{t}")

    def f_eq(e):
        return e.tensor_scalar(equ[:, sl * E:(sl + 1) * E],
                               lg[:, sl * E:(sl + 1) * E],
                               stt[:, c0 + 2:c0 + 3], None, ALU.is_ge)
    s_eq = sc.step("vector", f_eq, deps=[s_mx, prevd.get((t - 2, "eq"))],
                   note=f"{tag}_g6_{t}")
    prevd[(t, "eq")] = s_eq

    def f_cd(e):
        return e.select(sg[:, so + 2 * E:so + 3 * E], equ[:, sl * E:(sl + 1) * E],
                        io[:, 0:E], io[:, E:2 * E], add_drain=True)
    s_cd = sc.step("vector", f_cd, deps=[s_eq], note=f"{tag}_g7_{t}")

    def f_i1(e):
        return e.tensor_reduce(stt[:, c0 + 3:c0 + 4], sg[:, so + 2 * E:so + 3 * E],
                               axis=AX.X, op=ALU.min)
    s_i1 = sc.step("vector", f_i1, deps=[s_cd], note=f"{tag}_g8_{t}")

    def f_o1(e):
        return e.tensor_scalar(sg[:, so + E:so + 2 * E], io[:, 0:E],
                               stt[:, c0 + 3:c0 + 4], None, ALU.is_equal)
    s_o1 = sc.step("vector", f_o1, deps=[s_i1], note=f"{tag}_g9_{t}")

    sc2 = 8 + sl * 4
    def f_p1(e):
        return e.tensor_reduce(sg[:, so + 3 * E:so + 3 * E + 1],
                               sg[:, so:so + E], axis=AX.X, op=ALU.max)
    s_p1 = sc.step("vector", f_p1, deps=[s_pr, s_o1], note=f"{tag}_gA_{t}")

    def f_mp(e):
        return e.scalar_tensor_tensor(sg[:, so + 2 * E:so + 3 * E],
                                      sg[:, so + E:so + 2 * E], -1.0,
                                      io[:, E:2 * E] if False else sg[:, so:so + E],
                                      ALU.mult, ALU.bypass) \
            if False else \
            e.tensor_tensor(sg[:, so + 2 * E:so + 3 * E],
                            sg[:, so:so + E], sg[:, so + E:so + 2 * E],
                            ALU.mult)
    s_mp0 = sc.step("vector", f_mp, deps=[s_pr, s_o1], note=f"{tag}_gB_{t}")

    def f_mp2(e):
        return e.tensor_sub(sg[:, so + 2 * E:so + 3 * E], sg[:, so:so + E],
                            sg[:, so + 2 * E:so + 3 * E])
    s_mp = sc.step("vector", f_mp2, deps=[s_mp0], note=f"{tag}_gC_{t}")

    def f_p2(e):
        return e.tensor_reduce(sg[:, so + 3 * E + 1:so + 3 * E + 2],
                               sg[:, so + 2 * E:so + 3 * E], axis=AX.X,
                               op=ALU.max)
    s_p2 = sc.step("vector", f_p2, deps=[s_mp], note=f"{tag}_gD_{t}")

    def f_e2(e):
        return e.tensor_scalar(equ[:, sl * E:(sl + 1) * E],
                               sg[:, so + 2 * E:so + 3 * E],
                               sg[:, so + 3 * E + 1:so + 3 * E + 2], None,
                               ALU.is_ge)
    s_e2 = sc.step("vector", f_e2, deps=[s_p2, s_cd], note=f"{tag}_gE_{t}")

    def f_cd2(e):
        return e.select(sg[:, so + 2 * E:so + 3 * E], equ[:, sl * E:(sl + 1) * E],
                        io[:, 0:E], io[:, E:2 * E], add_drain=True)
    s_cd2 = sc.step("vector", f_cd2, deps=[s_e2], note=f"{tag}_gF_{t}")

    def f_i2(e):
        return e.tensor_reduce(sg[:, so + 3 * E + 2:so + 3 * E + 3],
                               sg[:, so + 2 * E:so + 3 * E], axis=AX.X,
                               op=ALU.min)
    s_i2 = sc.step("vector", f_i2, deps=[s_cd2], note=f"{tag}_gG_{t}")

    def f_o2(e):
        return e.tensor_scalar(sg[:, so + 4 * E:so + 5 * E], io[:, 0:E],
                               sg[:, so + 3 * E + 2:so + 3 * E + 3], None,
                               ALU.is_equal)
    s_o2 = sc.step("vector", f_o2, deps=[s_i2], note=f"{tag}_gH_{t}")

    def f_d12(e):
        return e.tensor_sub(sg[:, so + 3 * E:so + 3 * E + 1],
                            sg[:, so + 3 * E:so + 3 * E + 1],
                            sg[:, so + 3 * E + 1:so + 3 * E + 2])
    s_d = sc.step("vector", f_d12, deps=[s_p1, s_p2], note=f"{tag}_gI_{t}")

    def f_sg(e):
        return e.activation(sg[:, so + 3 * E:so + 3 * E + 1],
                            sg[:, so + 3 * E:so + 3 * E + 1], AF.Sigmoid)
    s_s = sc.step("scalar", f_sg, deps=[s_d], note=f"{tag}_gJ_{t}")

    def f_g2(e):
        return e.tensor_scalar(sg[:, so + 3 * E + 1:so + 3 * E + 2],
                               sg[:, so + 3 * E:so + 3 * E + 1],
                               -1.0, 1.0, ALU.mult, ALU.add)
    s_g2 = sc.step("vector", f_g2, deps=[s_s], note=f"{tag}_gK_{t}")

    def f_t1(e):
        return e.tensor_scalar(sg[:, so + E:so + 2 * E], sg[:, so + E:so + 2 * E],
                               sg[:, so + 3 * E:so + 3 * E + 1], None, ALU.mult)
    s_t1 = sc.step("vector", f_t1, deps=[s_g2], note=f"{tag}_gL_{t}")

    def f_t2(e):
        return e.scalar_tensor_tensor(sg[:, so:so + E],
                                      sg[:, so + 4 * E:so + 5 * E],
                                      sg[:, so + 3 * E + 1:so + 3 * E + 2],
                                      sg[:, so + E:so + 2 * E],
                                      ALU.mult, ALU.add)
    s_t2 = sc.step("vector", f_t2, deps=[s_t1, s_o2], note=f"{tag}_gM_{t}")

    def f_st(e, dma):
        dma(out=gates_d[t * 128:(t + 1) * 128, 0:E], in_=sg[:, so:so + E])
    return sc.dma("gpsimd", f_st, 1, deps=[s_t2], note=f"{tag}_gst{t}")



def ph_route_l1(p, logits_d, gates_d, Tt, E, cap, tag="rt"):
    cfg = p.cfg
    sc = p.sched
    nt = Tt // 128
    fence = sc.lasts()
    p.extra_inputs.setdefault("c_iota8", np.arange(8, dtype=np.float32))
    p.extra_inputs.setdefault("c_utri",
                              np.triu(np.ones((128, 128), dtype=np.float32)))
    p.extra_inputs.setdefault("c_ones", np.ones((128, 128), dtype=np.float32))
    iota_d = p.dram_const("c_iota8", [8], F32)
    utri_d = p.dram_const("c_utri", [128, 128], F32)
    ones_d = p.dram_const("c_ones", [128, 128], F32)
    with ExitStack() as es:
        ten = lambda n, sh, dt: es.enter_context(p.nc.sbuf_tensor(tag + n, sh, dt))
        lg = ten("_l", [128, 2 * E], F32)
        sg = ten("_s", [128, 8 * E], F32)
        equ = ten("_eq", [128, 2 * E], mybir.dt.uint8)
        io = ten("_io", [128, 2 * E], F32)
        ut = ten("_ut", [128, 128], F32)
        on = ten("_on", [128, 128], F32)
        car = ten("_cr", [128, 2 * E], F32)
        stt = ten("_stt", [128, 16], F32)
        pps = es.enter_context(p.nc.psum_tensor(tag + "_pp", [128, 2 * E], F32))
        pcr = es.enter_context(p.nc.psum_tensor(tag + "_pc", [128, E], F32))

        def f_c(e, dma):
            dma(out=io[:, 0:E], in_=bcast_row(iota_d, 128))
            dma(out=ut[:, :], in_=utri_d[:, :])
            dma(out=on[:, :], in_=ones_d[:, :])
        s_c = sc.dma("sync", f_c, 3, deps=fence, note=tag + "_c")

        def f_big(e):
            return e.memset(io[:, E:2 * E], 1.0e9)
        s_big = sc.step("vector", f_big, deps=fence, note=tag + "_bg")

        def f_c0(e):
            return e.memset(car[:, 0:2 * E], 0.0)
        s_car0 = sc.step("vector", f_c0, deps=fence, note=tag + "_c0")

        prev = {}
        s_colsum_prev = None
        s_carcp_prev = s_car0
        for t in range(nt):
            sl = t % 2

            def f_ld(e, dma, t=t, sl=sl):
                dma(out=lg[:, sl * E:(sl + 1) * E],
                    in_=logits_d[t * 128:(t + 1) * 128, 0:E])
            deps = list(fence) if t < 2 else [prev[(t - 2, "oh")],
                                              prev[(t - 2, "st")]]
            s_ld = sc.dma("sync", f_ld, 1, deps=deps, note=f"{tag}_ld{t}")

            def f_nm(e, sl=sl):
                return e.reduce_max(stt[:, sl * 8:sl * 8 + 1],
                                    lg[:, sl * E:(sl + 1) * E],
                                    axis=AX.X, negate=True)
            dd = [s_ld] + ([prev[(t - 2, "gate")]] if t >= 2 else [])
            s_nm = sc.step("vector", f_nm, deps=dd, note=f"{tag}_nm{t}")

            def f_ex(e, sl=sl):
                return e.activation(sg[:, sl * E:(sl + 1) * E],
                                    lg[:, sl * E:(sl + 1) * E], AF.Exp,
                                    bias=stt[:, sl * 8:sl * 8 + 1],
                                    accum_out=stt[:, sl * 8 + 1:sl * 8 + 2])
            dd = [s_nm] + ([prev[(t - 2, "oh")]] if t >= 2 else [])
            s_ex = sc.step("scalar", f_ex, deps=dd, note=f"{tag}_ex{t}")

            def f_rz(e, sl=sl):
                return e.reciprocal(stt[:, sl * 8 + 1:sl * 8 + 2],
                                    stt[:, sl * 8 + 1:sl * 8 + 2])
            s_rz = sc.step("vector", f_rz, deps=[s_ex], note=f"{tag}_rz{t}")

            def f_mx(e, sl=sl):
                return e.tensor_scalar(stt[:, sl * 8 + 2:sl * 8 + 3],
                                       stt[:, sl * 8:sl * 8 + 1], -1.0, None,
                                       ALU.mult)
            s_mx = sc.step("vector", f_mx, deps=[s_nm], note=f"{tag}_mx{t}")

            def f_eq(e, sl=sl):
                return e.tensor_scalar(equ[:, sl * E:(sl + 1) * E],
                                       lg[:, sl * E:(sl + 1) * E],
                                       stt[:, sl * 8 + 2:sl * 8 + 3], None,
                                       ALU.is_ge)
            dd = [s_mx] + ([prev[(t - 2, "eq")]] if t >= 2 else [])
            s_eq = sc.step("vector", f_eq, deps=dd, note=f"{tag}_eq{t}")
            prev[(t, "eq")] = s_eq

            def f_cd(e, sl=sl):
                return e.select(sg[:, 4 * E:5 * E], equ[:, sl * E:(sl + 1) * E],
                                io[:, 0:E], io[:, E:2 * E], add_drain=True)
            dd = [s_eq, s_big, s_c] + ([prev[(t - 1, "i1")]] if t >= 1 else [])
            s_cd = sc.step("vector", f_cd, deps=dd, note=f"{tag}_cd{t}")

            def f_i1(e, sl=sl):
                return e.tensor_reduce(stt[:, sl * 8 + 3:sl * 8 + 4],
                                       sg[:, 4 * E:5 * E], axis=AX.X, op=ALU.min)
            s_i1 = sc.step("vector", f_i1, deps=[s_cd], note=f"{tag}_i1{t}")
            prev[(t, "i1")] = s_i1

            def f_o1(e, sl=sl):
                return e.tensor_scalar(sg[:, (2 + sl) * E:(3 + sl) * E],
                                       io[:, 0:E],
                                       stt[:, sl * 8 + 3:sl * 8 + 4], None,
                                       ALU.is_equal)
            s_o1 = sc.step("vector", f_o1, deps=[s_i1], note=f"{tag}_o1{t}")
            prev[(t, "oh")] = s_o1

            def f_ps(e, sl=sl):
                e.matmul(pps[:, sl * E:(sl + 1) * E], ut[:, :],
                         sg[:, (2 + sl) * E:(3 + sl) * E], start=True, stop=False)
                return e.matmul(pps[:, sl * E:(sl + 1) * E], on[0:1, 0:128],
                                car[0:1, sl * E:(sl + 1) * E], start=False,
                                stop=True)
            dd = [s_o1, s_c, s_carcp_prev]
            if t >= 2:
                dd.append(prev[(t - 2, "pos")])
            s_ps = sc.step("tensor", f_ps, deps=dd, note=f"{tag}_ps{t}")

            def f_cs(e, sl=sl, t=t):
                return e.matmul(pcr[0:1, 0:E], on[:, 0:1],
                                sg[:, (2 + sl) * E:(3 + sl) * E],
                                start=True, stop=True)
            dd = [s_o1]
            if s_colsum_prev is not None:
                dd.append(prev[(t - 1, "carcp")])
            s_cs = sc.step("tensor", f_cs, deps=dd, note=f"{tag}_cs{t}")
            s_colsum_prev = s_cs

            def f_ccp(e, sl=sl):
                return e.tensor_tensor(car[0:1, (1 - sl) * E:(2 - sl) * E],
                                       car[0:1, sl * E:(sl + 1) * E],
                                       pcr[0:1, 0:E], ALU.add)
            dd = [s_cs, s_carcp_prev]
            s_ccp = sc.step("vector", f_ccp, deps=dd, note=f"{tag}_ccp{t}")
            prev[(t, "carcp")] = s_ccp
            s_carcp_prev = s_ccp

            def f_pm(e, sl=sl):
                return e.tensor_tensor(sg[:, 5 * E:6 * E],
                                       sg[:, (2 + sl) * E:(3 + sl) * E],
                                       pps[:, sl * E:(sl + 1) * E], ALU.mult)
            dd = [s_ps] + ([prev[(t - 1, "pc")]] if t >= 1 else [])
            s_pm = sc.step("vector", f_pm, deps=dd, note=f"{tag}_pm{t}")
            prev[(t, "pos")] = s_pm

            def f_pc(e, sl=sl):
                return e.tensor_reduce(stt[:, sl * 8 + 4:sl * 8 + 5],
                                       sg[:, 5 * E:6 * E], axis=AX.X, op=ALU.add)
            dd = [s_pm] + ([prev[(t - 1, "pc")]] if t >= 1 else [])
            s_pc = sc.step("vector", f_pc, deps=dd, note=f"{tag}_pc{t}")
            prev[(t, "pc")] = s_pc

            def f_kp(e, sl=sl):
                return e.tensor_scalar(stt[:, sl * 8 + 5:sl * 8 + 6],
                                       stt[:, sl * 8 + 4:sl * 8 + 5],
                                       float(cap), None, ALU.is_le)
            s_kp = sc.step("vector", f_kp, deps=[s_pc], note=f"{tag}_kp{t}")

            def f_gt(e, sl=sl):
                return e.tensor_tensor(stt[:, sl * 8 + 6:sl * 8 + 7],
                                       stt[:, sl * 8 + 5:sl * 8 + 6],
                                       stt[:, sl * 8 + 1:sl * 8 + 2], ALU.mult)
            s_gt = sc.step("vector", f_gt, deps=[s_kp, s_rz], note=f"{tag}_gt{t}")

            def f_gg(e, sl=sl):
                return e.tensor_scalar(sg[:, (6 + sl) * E:(7 + sl) * E],
                                       sg[:, (2 + sl) * E:(3 + sl) * E],
                                       stt[:, sl * 8 + 6:sl * 8 + 7], None,
                                       ALU.mult)
            dd = [s_gt] + ([prev[(t - 2, "st")]] if t >= 2 else [])
            s_gg = sc.step("vector", f_gg, deps=dd, note=f"{tag}_gg{t}")
            prev[(t, "gate")] = s_gg

            def f_st(e, dma, t=t, sl=sl):
                dma(out=gates_d[t * 128:(t + 1) * 128, 0:E],
                    in_=sg[:, (6 + sl) * E:(7 + sl) * E])
            prev[(t, "st")] = sc.dma("gpsimd", f_st, 1, deps=[s_gg],
                                     note=f"{tag}_gst{t}")
    return sc.lasts()



def ph_moe(p, xin_fm_d, we1_d, be1_d, we2_d, be2_d, gates_d, h_fm_d,
           acc_init_tm_d, acc_out_tm_d, act="gelu", tag="moe"):
    cfg = p.cfg
    sc = p.sched
    d, F, E, ncd, nt, nF = cfg.d, cfg.F, cfg.E, cfg.nc_d, cfg.nt, cfg.nF
    T = cfg.T
    fence = sc.lasts()
    es_moe = ExitStack()
    acc = es_moe.enter_context(p.nc.sbuf_tensor(tag + "_acc", [128, nt * d], BF16))
    gsb = es_moe.enter_context(p.nc.sbuf_tensor(tag + "_g", [128, nt * E], F32))

    with ExitStack() as es:
        tmp = es.enter_context(p.nc.sbuf_tensor(tag + "_i", [128, 4 * d], F32))
        def f_g(e, dma):
            for t in range(nt):
                dma(out=gsb[:, t * E:(t + 1) * E],
                    in_=gates_d[t * 128:(t + 1) * 128, 0:E])
        s_g = sc.dma("sync", f_g, nt, deps=fence, note=tag + "_gld")
        init_steps = []
        if acc_init_tm_d is not None:
            for gi in range(nt // 4):
                def f_l(e, dma, gi=gi):
                    sl = gi % 2
                    for j in range(4):
                        dma(out=tmp[:, ((sl * 2 + 0) * 2 + j) * d:
                                    ((sl * 2 + 0) * 2 + j + 1) * d]
                            if False else
                            tmp[:, j * d:(j + 1) * d] if sl == 0 else
                            tmp[:, (j) * d:(j + 1) * d],
                            in_=acc_init_tm_d[(gi * 4 + j) * 128:
                                              (gi * 4 + j + 1) * 128, 0:d])
                deps = list(fence) if gi == 0 else [init_steps[-1]]
                s_l = sc.dma("sync", f_l, 4, deps=deps, note=f"{tag}_il{gi}")

                def f_c(e, gi=gi):
                    for j in range(4):
                        ins = e.tensor_copy(
                            acc[:, (gi * 4 + j) * d:(gi * 4 + j + 1) * d],
                            tmp[:, j * d:(j + 1) * d])
                    return ins
                s_cc = sc.step("vector", f_c, deps=[s_l], note=f"{tag}_ic{gi}")
                init_steps.append(s_cc)
        else:
            def f_z(e):
                return e.memset(acc[:, :], 0.0)
            init_steps.append(sc.step("vector", f_z, deps=fence,
                                      note=tag + "_z"))

    for ei in range(E):
        ph_proj(p, xin_fm_d, we1_d[ei], d, F, b_d=be1_d[ei],
                out_fm_d=h_fm_d, act=act, tag=f"{tag}u{ei}")
        ph_moe_down(p, h_fm_d, we2_d[ei], be2_d[ei], gsb, ei, acc,
                    extra_deps=init_steps + [s_g], tag=f"{tag}d{ei}")

    fence2 = sc.lasts()
    for gi in range(nt // 4):
        def f_st(e, dma, gi=gi):
            for j in range(4):
                t = gi * 4 + j
                dma(out=acc_out_tm_d[t * 128:(t + 1) * 128, 0:d],
                    in_=acc[:, t * d:(t + 1) * d])
        sc.dma("gpsimd", f_st, 4, deps=fence2, note=f"{tag}_ast{gi}")
    es_moe.close()
    return sc.lasts()


def ph_moe_down(p, h_fm_d, w_d, b_d, gsb, ei, acc, extra_deps=(), tag="md"):
    cfg = p.cfg
    sc = p.sched
    d, F, E, nF, nt = cfg.d, cfg.F, cfg.E, cfg.nF, cfg.nt
    fence = sc.lasts()
    with ExitStack() as es:
        ten = lambda n, sh, dt: es.enter_context(p.nc.sbuf_tensor(tag + n, sh, dt))
        wbf = ten("_w", [128, nF * d], BF16)
        wst = ten("_ws", [128, 2 * d], F32)
        bro = ten("_br", [128, d], F32)
        xg = ten("_x", [128, 2 * nF * 128], BF16)
        s0 = ten("_s0", [128, 2 * d], F32)
        pj = es.enter_context(p.nc.psum_tensor(tag + "_pj", [128, 2 * d], F32))

        wld, wcast = [], []
        for k in range(nF):
            def f(e, dma, k=k):
                dma(out=wst[:, (k % 2) * d:(k % 2) * d + d],
                    in_=w_d[k * 128:(k + 1) * 128, 0:d])
            deps = list(fence)
            if k >= 2:
                deps.append(wcast[k - 2])
            wld.append(sc.dma("sync", f, 1, deps=deps, note=f"{tag}_wld{k}"))

            def fc(e, k=k):
                return e.tensor_copy(wbf[:, k * d:(k + 1) * d],
                                     wst[:, (k % 2) * d:(k % 2) * d + d])
            wcast.append(sc.step("vector", fc, deps=[wld[k]], note=f"{tag}_wc{k}"))
        w_ready = [wcast[-1]]

        def f_b(e, dma):
            dma(out=bro[:, 0:d], in_=bcast_row(b_d, 128))
        b_ready = [sc.dma("sync", f_b, 1, deps=fence, note=tag + "_b")]

        tst = {}
        for t in range(nt):
            sl = t % 2

            def f_ld(e, dma, t=t, sl=sl):
                q4 = max(nF // 4, 1)
                for pc in range(0, nF, q4):
                    dma(out=xg[:, (sl * nF + pc) * 128:(sl * nF + pc + q4) * 128]
                        .rearrange("p (c x) -> p c x", c=q4),
                        in_=h_fm_d[pc:pc + q4, 0:128, t * 128:(t + 1) * 128]
                        .rearrange("c p x -> p c x"))
            deps = list(fence)
            if t >= 2:
                deps.append(tst[t - 2][0])
            s_ld = sc.dma("sync", f_ld, 4 if nF >= 4 else nF, deps=deps,
                          note=f"{tag}_ld{t}")

            def f_mm(e, t=t, sl=sl):
                for k in range(nF):
                    for cg in range(d // 512):
                        ins = e.matmul(
                            pj[:, (sl * (d // 512) + cg) * 512:
                               (sl * (d // 512) + cg + 1) * 512],
                            xg[:, (sl * nF + k) * 128:(sl * nF + k + 1) * 128],
                            wbf[:, k * d + cg * 512:k * d + (cg + 1) * 512],
                            start=(k == 0), stop=(k == nF - 1))
                return ins
            deps = [s_ld] + w_ready
            if t >= 2:
                deps.append(tst[t - 2][1])
            s_mm = sc.step("tensor", f_mm, deps=deps, note=f"{tag}_mm{t}")

            def f_e1(e, t=t, sl=sl):
                return e.tensor_tensor(s0[:, sl * d:(sl + 1) * d],
                                       pj[:, sl * d:(sl + 1) * d],
                                       bro[:, 0:d], ALU.add)
            deps = [s_mm] + b_ready
            if t >= 2:
                deps.append(tst[t - 2][2])
            s_e1 = sc.step("vector", f_e1, deps=deps, note=f"{tag}_e1{t}")

            def f_e2(e, t=t, sl=sl):
                return e.scalar_tensor_tensor(acc[:, t * d:(t + 1) * d],
                                              s0[:, sl * d:(sl + 1) * d],
                                              gsb[:, t * E + ei:t * E + ei + 1],
                                              acc[:, t * d:(t + 1) * d],
                                              ALU.mult, ALU.add)
            deps = [s_e1] + list(extra_deps)
            s_e2 = sc.step("vector", f_e2, deps=deps, note=f"{tag}_e2{t}")
            tst[t] = (s_e2, s_e1, s_e2)
    return sc.lasts()



def ph_lattn(p, qfm_d, kfm_d, vtm_d, ctxfm_d, tag="la"):
    cfg = p.cfg
    sc = p.sched
    S, d, nh, dh, ncd = cfg.S, cfg.d, cfg.nh, cfg.dh, cfg.nc_d
    C = S // 128
    NQG = S // 512
    fence = sc.lasts()
    if "c_lmask" not in p.extra_inputs:
        m = np.zeros((4, 128, 512), dtype=np.float32)
        kr = np.arange(128)[:, None]
        qc = np.arange(512)[None, :]
        for off in range(4):
            m[off] = (kr <= qc - 128 * off)
        p.extra_inputs["c_lmask"] = m
    lm_d = p.dram_const("c_lmask", [4, 128, 512], F32)
    insts = [(s, h) for s in range(cfg.nseq) for h in range(nh)]
    with ExitStack() as es:
        ten = lambda n, sh, dt: es.enter_context(p.nc.sbuf_tensor(tag + n, sh, dt))
        qsb = ten("_q", [128, 2 * S], BF16)
        ksb = ten("_k", [128, 2 * S], BF16)
        vsb = ten("_v", [128, 2 * C * 64], BF16)
        wtb = ten("_w", [128, C * 512], BF16)
        cst = ten("_c", [128, 2 * 512], F32)
        lmf = ten("_mf", [128, 4 * 512], F32)
        psc = es.enter_context(p.nc.psum_tensor(tag + "_ps", [128, 2 * 512], F32))
        pcx = es.enter_context(p.nc.psum_tensor(tag + "_px", [128, 512], F32))

        def f_m(e, dma):
            dma(out=lmf[:, :].rearrange("p (o x) -> p o x", o=4),
                in_=lm_d[0:4, 0:128, 0:512].rearrange("o p x -> p o x"))
        s_m = sc.dma("sync", f_m, 1, deps=fence, note=tag + "_m")

        s_mb = s_m

        prev_rel = {}
        prev_wev = {}
        prev_ctx = None
        prev_cst = {}
        for ii, (s, h) in enumerate(insts):
            sl = ii % 2
            qo = sl * S
            vo = sl * C * 64

            def f_q(e, dma, s=s, h=h, qo=qo):
                dma(out=qsb[0:64, qo:qo + S],
                    in_=qfm_d[h // 2, (h % 2) * 64:(h % 2) * 64 + 64,
                              s * S:(s + 1) * S])
                dma(out=ksb[0:64, qo:qo + S],
                    in_=kfm_d[h // 2, (h % 2) * 64:(h % 2) * 64 + 64,
                              s * S:(s + 1) * S])

            def f_v(e, dma, s=s, h=h, vo=vo):
                q4 = max(C // 4, 1)
                for pc in range(0, C, q4):
                    dma(out=vsb[:, vo + pc * 64:vo + (pc + q4) * 64],
                        in_=vtm_d[s * S + pc * 128:s * S + (pc + q4) * 128,
                                  h * 64:(h + 1) * 64]
                        .rearrange("(c p) x -> p c x", p=128))
            deps = list(fence)
            if ii >= 2:
                deps.append(prev_rel[ii - 2])
            s_ldq = sc.dma("sync", f_q, 2, deps=deps, note=f"{tag}_lq{ii}")
            s_ldv = sc.dma("sync", f_v, 4 if C >= 4 else C, deps=deps,
                           note=f"{tag}_lv{ii}")

            last = None
            for qg in range(NQG):
                w_evs = []
                for kt in range(4 * qg + 4):
                    off = kt - 4 * qg
                    pb = kt % 2

                    def f_w(e, kt=kt, qg=qg, pb=pb, qo=qo):
                        return e.matmul(psc[:, pb * 512:(pb + 1) * 512],
                                        ksb[0:64, qo + kt * 128:qo + (kt + 1) * 128],
                                        qsb[0:64, qo + qg * 512:qo + (qg + 1) * 512],
                                        start=True, stop=True)
                    deps = [s_ldq]
                    pe_prev = prev_wev.get((ii, qg, kt - 2)) or \
                        (prev_wev.get((ii, qg - 1, 4 * qg - 1)) if kt < 2 and qg else None) or \
                        (prev_wev.get((ii - 1, NQG - 1, 4 * NQG - 1)) if kt < 2 and qg == 0 else None)
                    if pe_prev is not None:
                        deps.append(pe_prev)
                    s_w = sc.step("tensor", f_w, deps=deps, note=f"{tag}_w{ii}_{qg}_{kt}")

                    def f_we(e, kt=kt, off=off, pb=pb):
                        if off >= 0:
                            return e.tensor_tensor(wtb[:, kt * 512:(kt + 1) * 512],
                                                   psc[:, pb * 512:(pb + 1) * 512],
                                                   lmf[:, off * 512:(off + 1) * 512],
                                                   ALU.mult)
                        return e.tensor_copy(wtb[:, kt * 512:(kt + 1) * 512],
                                             psc[:, pb * 512:(pb + 1) * 512])
                    deps = [s_w, s_mb]
                    if prev_ctx is not None:
                        deps.append(prev_ctx)
                    s_we = sc.step("vector", f_we, deps=deps,
                                   note=f"{tag}_we{ii}_{qg}_{kt}")
                    prev_wev[(ii, qg, kt)] = s_we
                    w_evs.append(s_we)

                def f_x(e, qg=qg, vo=vo):
                    n = 4 * qg + 4
                    for kt in range(n):
                        ins = e.matmul(pcx[0:64, 0:512],
                                       vsb[:, vo + kt * 64:vo + (kt + 1) * 64],
                                       wtb[:, kt * 512:(kt + 1) * 512],
                                       start=(kt == 0), stop=(kt == n - 1))
                    return ins
                deps = w_evs + [s_ldv]
                if prev_cst.get(qg % 2):
                    deps.append(prev_cst[qg % 2])
                s_x = sc.step("tensor", f_x, deps=deps, note=f"{tag}_x{ii}_{qg}")
                prev_ctx = s_x
                last = s_x

                csl = qg % 2

                def f_cc(e, csl=csl):
                    return e.tensor_copy(cst[0:64, csl * 512:(csl + 1) * 512],
                                         pcx[0:64, 0:512])
                s_cc = sc.step("vector", f_cc, deps=[s_x], note=f"{tag}_cc{ii}_{qg}")

                def f_st(e, dma, s=s, h=h, qg=qg, csl=csl):
                    dma(out=ctxfm_d[h // 2, (h % 2) * 64:(h % 2) * 64 + 64,
                                    s * S + qg * 512:s * S + (qg + 1) * 512],
                        in_=cst[0:64, csl * 512:(csl + 1) * 512])
                s_stc = sc.dma("gpsimd", f_st, 1, deps=[s_cc],
                               note=f"{tag}_st{ii}_{qg}")
                prev_cst[csl] = s_stc
            prev_rel[ii] = last
    return sc.lasts()



def build_l0ffn(cfg):
    p = PB(cfg)
    nc = p.nc
    d, F, E, ncd, nF, T = cfg.d, cfg.F, cfg.E, cfg.nc_d, cfg.nF, cfg.T
    x_d = nc.dram_tensor("x", [T, d], F32, kind="ExternalInput")
    wq_d = nc.dram_tensor("wq", [d, d], F32, kind="ExternalInput")
    wk_d = nc.dram_tensor("wk", [d, d], F32, kind="ExternalInput")
    wv_d = nc.dram_tensor("wv", [d, d], F32, kind="ExternalInput")
    wo_d = nc.dram_tensor("wo", [d, d], F32, kind="ExternalInput")
    bo_d = nc.dram_tensor("bo", [d], F32, kind="ExternalInput")
    wr_d = nc.dram_tensor("wr", [d, E], F32, kind="ExternalInput")
    we1_d = nc.dram_tensor("we1", [E, d, F], F32, kind="ExternalInput")
    be1_d = nc.dram_tensor("be1", [E, F], F32, kind="ExternalInput")
    we2_d = nc.dram_tensor("we2", [E, F, d], F32, kind="ExternalInput")
    be2_d = nc.dram_tensor("be2", [E, d], F32, kind="ExternalInput")
    g1_d = nc.dram_tensor("ln1g", [d], F32, kind="ExternalInput")
    b1_d = nc.dram_tensor("ln1b", [d], F32, kind="ExternalInput")
    g2_d = nc.dram_tensor("ln2g", [d], F32, kind="ExternalInput")
    b2_d = nc.dram_tensor("ln2b", [d], F32, kind="ExternalInput")
    y_d = nc.dram_tensor("y", [T, d], F32, kind="ExternalOutput")
    xin_fm = nc.dram_tensor("s_xin", [ncd, 128, T], BF16)
    qfm = nc.dram_tensor("s_q", [ncd, 128, T], BF16)
    kfm = nc.dram_tensor("s_k", [ncd, 128, T], BF16)
    vtm = nc.dram_tensor("s_v", [T, d], BF16)
    ctxfm = nc.dram_tensor("s_ctx", [ncd, 128, T], BF16)
    ao_tm = nc.dram_tensor("s_ao", [T, d], BF16)
    out1_tm = nc.dram_tensor("s_o1", [T, d], F32)
    out1_fm = nc.dram_tensor("s_o1f", [ncd, 128, T], BF16)
    gates = nc.dram_tensor("s_gt", [T, E], F32)
    h_fm = nc.dram_tensor("s_h", [nF, 128, T], BF16)
    acc_tm = nc.dram_tensor("s_acc", [T, d], BF16)

    p.start()
    with nc.Block() as block:
        with ExitStack() as es0:
            sb = mk_ln_sb(p, "ln0", es0)
            ph_ln(p, x_d, fm_d=xin_fm, eps=1e-5, tag="ln0", sb=sb)
        p.sched.emit_pending(block)
        ph_proj(p, xin_fm, wq_d, d, d, out_fm_d=qfm, tag="pq")
        p.sched.emit_pending(block)
        ph_proj(p, xin_fm, wk_d, d, d, out_fm_d=kfm, tag="pk")
        p.sched.emit_pending(block)
        ph_proj(p, xin_fm, wv_d, d, d, out_tm_d=vtm, tag="pv")
        p.sched.emit_pending(block)
        ph_attn(p, qfm, kfm, vtm, ctxfm, "std", 1.0 / math.sqrt(cfg.dh), "at")
        p.sched.emit_pending(block)
        ph_proj(p, ctxfm, wo_d, d, d, b_d=bo_d, out_tm_d=ao_tm, tag="po")
        p.sched.emit_pending(block)
        ph_resid(p, "sum_ln", x_d, ao_tm, tm_d=out1_tm, fm_d=out1_fm,
                 eps=1e-5, eps2=1e-6, g_d=g1_d, b_d=b1_d, adt=BF16, tag="r1")
        p.sched.emit_pending(block)
        ph_logits(p, out1_tm, wr_d, E, mode="l0top2", out_d=gates, tag="lg")
        p.sched.emit_pending(block)
        ph_moe(p, out1_fm, we1_d, be1_d, we2_d, be2_d, gates, h_fm,
               out1_tm, acc_tm, act="gelu", tag="mo")
        p.sched.emit_pending(block)
        ph_resid(p, "x_cln", x_d, acc_tm, y_d=y_d, coef=0.5, eps=1e-6,
                 g_d=g2_d, b_d=b2_d, adt=BF16, tag="r2")
        p.sched.emit_pending(block)
        p.sched.emit_final(block)
    return p


def mk_ln_sb(p, tag, es):
    t = lambda n, sh, dt: es.enter_context(p.nc.sbuf_tensor(tag + n, sh, dt))
    return {
        "xg": t("sxg", [128, 2 * 4 * p.cfg.d], F32),
        "yg": t("syg", [128, 2 * 4 * p.cfg.d], F32),
        "yb": t("syb", [128, 2 * 4 * p.cfg.d], BF16),
        "stt": t("sst", [128, 2 * 8 * 4], F32),
        "gb": t("sgb", [128, 2 * p.cfg.d], F32),
        "fms": t("sfm", [128, 2 * p.cfg.d], BF16),
        "ptp": es.enter_context(p.nc.psum_tensor(tag + "spt", [128, 1024], BF16)),
    }


def build_l0attn(cfg):
    p = PB(cfg)
    nc = p.nc
    d, ncd, T = cfg.d, cfg.nc_d, cfg.T
    x_d = nc.dram_tensor("x", [T, d], F32, kind="ExternalInput")
    names = ["wq", "bq", "wk", "bk", "wv", "bv", "wo", "bo", "gamma", "beta"]
    shp = {"w": [d, d], "b": [d], "g": [d]}
    hd = {}
    for n in names:
        sh = [d, d] if n.startswith("w") else [d]
        hd[n] = nc.dram_tensor(n, sh, F32, kind="ExternalInput")
    y_d = nc.dram_tensor("y", [T, d], F32, kind="ExternalOutput")
    xin_fm = nc.dram_tensor("s_xin", [ncd, 128, T], BF16)
    qfm = nc.dram_tensor("s_q", [ncd, 128, T], BF16)
    kfm = nc.dram_tensor("s_k", [ncd, 128, T], BF16)
    vtm = nc.dram_tensor("s_v", [T, d], BF16)
    ctxfm = nc.dram_tensor("s_ctx", [ncd, 128, T], BF16)
    ao_tm = nc.dram_tensor("s_ao", [T, d], BF16)
    out1_tm = nc.dram_tensor("s_o1", [T, d], F32)

    p.start()
    with nc.Block() as block:
        with ExitStack() as es0:
            sb = mk_ln_sb(p, "ln0", es0)
            ph_ln(p, x_d, fm_d=xin_fm, eps=1e-5, tag="ln0", sb=sb)
        p.sched.emit_pending(block)
        ph_proj(p, xin_fm, hd["wq"], d, d, b_d=hd["bq"], out_fm_d=qfm, tag="pq")
        p.sched.emit_pending(block)
        ph_proj(p, xin_fm, hd["wk"], d, d, b_d=hd["bk"], out_fm_d=kfm, tag="pk")
        p.sched.emit_pending(block)
        ph_proj(p, xin_fm, hd["wv"], d, d, b_d=hd["bv"], out_tm_d=vtm, tag="pv")
        p.sched.emit_pending(block)
        ph_attn(p, qfm, kfm, vtm, ctxfm, "resh", 1.0 / 8.0, "at")
        p.sched.emit_pending(block)
        ph_proj(p, ctxfm, hd["wo"], d, d, b_d=hd["bo"], out_tm_d=ao_tm, tag="po")
        p.sched.emit_pending(block)
        ph_resid(p, "sum_ln", x_d, ao_tm, tm_d=out1_tm, eps=1e-5, eps2=1e-5,
                 g_d=hd["gamma"], b_d=hd["beta"], adt=BF16, tag="r1")
        p.sched.emit_pending(block)
        ph_resid(p, "x_ca", x_d, out1_tm, y_d=y_d, coef=1.0, adt=F32, tag="r2")
        p.sched.emit_pending(block)
        p.sched.emit_final(block)
    return p


def build_l1log(cfg):
    p = PB(cfg)
    nc = p.nc
    d, E, T = cfg.d, cfg.E, cfg.T
    x_d = nc.dram_tensor("x", [T, d], F32, kind="ExternalInput")
    gw_d = nc.dram_tensor("gw", [E, d], F32, kind="ExternalInput")
    gb_d = nc.dram_tensor("gb", [E], F32, kind="ExternalInput")
    lo_d = nc.dram_tensor("lo", [T, E], F32, kind="ExternalOutput")
    p.start()
    with nc.Block() as block:
        ph_logits(p, x_d, gw_d, E, b_d=gb_d, ln_first=True, mode="store",
                  out_d=lo_d, w_is_ExD=True, eps=1e-5, tag="lg")
        p.sched.emit_pending(block)
        p.sched.emit_final(block)
    return p


def build_l1route(cfg):
    p = PB(cfg)
    nc = p.nc
    E, Tt = cfg.E, cfg.Ttot
    lo_d = nc.dram_tensor("lo", [Tt, E], F32, kind="ExternalInput")
    ga_d = nc.dram_tensor("ga", [Tt, E], F32, kind="ExternalOutput")
    p.start()
    with nc.Block() as block:
        ph_route_l1(p, lo_d, ga_d, Tt, E, cap=cfg.cap, tag="rt")
        p.sched.emit_pending(block)
        p.sched.emit_final(block)
    return p


def build_l1ffn(cfg):
    p = PB(cfg)
    nc = p.nc
    d, F, E, ncd, nF, T = cfg.d, cfg.F, cfg.E, cfg.nc_d, cfg.nF, cfg.T
    x_d = nc.dram_tensor("x", [T, d], F32, kind="ExternalInput")
    ga_d = nc.dram_tensor("ga", [T, E], F32, kind="ExternalInput")
    w1_d = nc.dram_tensor("w1", [E, d, F], F32, kind="ExternalInput")
    b1_d = nc.dram_tensor("b1", [E, F], F32, kind="ExternalInput")
    w2_d = nc.dram_tensor("w2", [E, F, d], F32, kind="ExternalInput")
    b2_d = nc.dram_tensor("b2", [E, d], F32, kind="ExternalInput")
    y_d = nc.dram_tensor("y", [T, d], F32, kind="ExternalOutput")
    xin_fm = nc.dram_tensor("s_xin", [ncd, 128, T], BF16)
    h_fm = nc.dram_tensor("s_h", [nF, 128, T], BF16)
    acc_tm = nc.dram_tensor("s_acc", [T, d], BF16)
    p.start()
    with nc.Block() as block:
        with ExitStack() as es0:
            sb = mk_ln_sb(p, "ln0", es0)
            ph_ln(p, x_d, fm_d=xin_fm, eps=1e-5, tag="ln0", sb=sb)
        p.sched.emit_pending(block)
        ph_moe(p, xin_fm, w1_d, b1_d, w2_d, b2_d, ga_d, h_fm, None, acc_tm,
               act="relu", tag="mo")
        p.sched.emit_pending(block)
        ph_resid(p, "x_ca", x_d, acc_tm, y_d=y_d, coef=0.5, adt=BF16, tag="r2")
        p.sched.emit_pending(block)
        p.sched.emit_final(block)
    return p


def build_l1attn(cfg):
    p = PB(cfg)
    nc = p.nc
    d, ncd, T = cfg.d, cfg.nc_d, cfg.T
    x_d = nc.dram_tensor("x", [T, d], F32, kind="ExternalInput")
    hd = {}
    for n in ["wq", "bq", "wk", "bk", "wv", "bv", "wo", "bo", "lng", "lnb"]:
        sh = [d, d] if n.startswith("w") else [d]
        hd[n] = nc.dram_tensor(n, sh, F32, kind="ExternalInput")
    y_d = nc.dram_tensor("y", [T, d], F32, kind="ExternalOutput")
    xin_fm = nc.dram_tensor("s_xin", [ncd, 128, T], BF16)
    qfm = nc.dram_tensor("s_q", [ncd, 128, T], BF16)
    kfm = nc.dram_tensor("s_k", [ncd, 128, T], BF16)
    vtm = nc.dram_tensor("s_v", [T, d], BF16)
    ctxf = nc.dram_tensor("s_ctx", [ncd, 128, T], F32)
    o_tm = nc.dram_tensor("s_o", [T, d], F32)
    on_fm = nc.dram_tensor("s_on", [ncd, 128, T], BF16)
    pr_tm = nc.dram_tensor("s_pr", [T, d], BF16)
    p.start()
    with nc.Block() as block:
        with ExitStack() as es0:
            sb = mk_ln_sb(p, "ln0", es0)
            ph_ln(p, x_d, fm_d=xin_fm, eps=1e-5, tag="ln0", sb=sb)
        p.sched.emit_pending(block)
        ph_proj(p, xin_fm, hd["wq"], d, d, b_d=hd["bq"], out_fm_d=qfm,
                act="elu1", w_transposed=True, tag="pq")
        p.sched.emit_pending(block)
        ph_proj(p, xin_fm, hd["wk"], d, d, b_d=hd["bk"], out_fm_d=kfm,
                act="elu1", w_transposed=True, tag="pk")
        p.sched.emit_pending(block)
        ph_proj(p, xin_fm, hd["wv"], d, d, b_d=hd["bv"], out_tm_d=vtm,
                w_transposed=True, tag="pv")
        p.sched.emit_pending(block)
        ph_lattn(p, qfm, kfm, vtm, ctxf, tag="la")
        p.sched.emit_pending(block)
        ph_fm2tm(p, ctxf, o_tm, dt=F32, tag="f2")
        p.sched.emit_pending(block)
        with ExitStack() as es1:
            sb2 = mk_ln_sb(p, "ln1", es1)
            ph_ln(p, o_tm, fm_d=on_fm, eps=1e-5, g_d=hd["lng"], b_d=hd["lnb"],
                  tag="ln1", sb=sb2)
        p.sched.emit_pending(block)
        ph_proj(p, on_fm, hd["wo"], d, d, b_d=hd["bo"], out_tm_d=pr_tm,
                w_transposed=True, tag="po")
        p.sched.emit_pending(block)
        ph_resid(p, "x_ca", x_d, pr_tm, y_d=y_d, coef=1.0, adt=BF16, tag="r2")
        p.sched.emit_pending(block)
        p.sched.emit_final(block)
    return p



_PROGS = {}


def _get_prog(name, cfg):
    if name not in _PROGS:
        _PROGS[name] = {
            "l0ffn": build_l0ffn, "l0attn": build_l0attn,
            "l1log": build_l1log, "l1route": build_l1route,
            "l1ffn": build_l1ffn, "l1attn": build_l1attn,
            "fused": build_fused,
        }[name](cfg)
    return _PROGS[name]


def _run(name, cfg, per_core_maps):
    p = _get_prog(name, cfg)
    for m in per_core_maps:
        for cn, cv in p.extra_inputs.items():
            m[cn] = cv
    res = run_bass_kernel_spmd(p.nc, per_core_maps, core_ids=list(range(NCORES)))
    return res.results


def kernel(**inputs):
    cfg = Cfg()
    T = cfg.T
    x0 = np.ascontiguousarray(np.asarray(inputs["L0_attn_x"], dtype=np.float32))
    B, S, d = x0.shape
    xs = [np.ascontiguousarray(x0[2 * c:2 * c + 2].reshape(T, d))
          for c in range(NCORES)]

    def f32(n):
        return np.ascontiguousarray(np.asarray(inputs[n], dtype=np.float32))

    def l0ffn_maps(xc):
        base = {"wq": f32("L0_ffn_Wq"), "wk": f32("L0_ffn_Wk"),
                "wv": f32("L0_ffn_Wv"), "wo": f32("L0_ffn_Wo"),
                "bo": f32("L0_ffn_bo"), "wr": f32("L0_ffn_Wr"),
                "we1": f32("L0_ffn_We1"), "be1": f32("L0_ffn_be1"),
                "we2": f32("L0_ffn_We2"), "be2": f32("L0_ffn_be2"),
                "ln1g": f32("L0_ffn_ln1_g"), "ln1b": f32("L0_ffn_ln1_b"),
                "ln2g": f32("L0_ffn_ln2_g"), "ln2b": f32("L0_ffn_ln2_b")}
        return [dict(base, x=xc[c]) for c in range(NCORES)]

    def l0attn_maps(xc):
        base = {"wq": f32("L0_attn_wq"), "bq": f32("L0_attn_bq"),
                "wk": f32("L0_attn_wk"), "bk": f32("L0_attn_bk"),
                "wv": f32("L0_attn_wv"), "bv": f32("L0_attn_bv"),
                "wo": f32("L0_attn_wo"), "bo": f32("L0_attn_bo"),
                "gamma": f32("L0_attn_gamma"), "beta": f32("L0_attn_beta")}
        return [dict(base, x=xc[c]) for c in range(NCORES)]

    def l1log_maps(xc):
        base = {"gw": f32("L1_ffn_gate_w"), "gb": f32("L1_ffn_gate_b")}
        return [dict(base, x=xc[c]) for c in range(NCORES)]

    def l1ffn_maps(xc, gc):
        base = {"w1": f32("L1_ffn_w1"), "b1": f32("L1_ffn_b1"),
                "w2": f32("L1_ffn_w2"), "b2": f32("L1_ffn_b2")}
        return [dict(base, x=xc[c], ga=gc[c]) for c in range(NCORES)]

    def l1attn_maps(xc):
        base = {"wq": f32("L1_attn_Wq"), "bq": f32("L1_attn_bq"),
                "wk": f32("L1_attn_Wk"), "bk": f32("L1_attn_bk"),
                "wv": f32("L1_attn_Wv"), "bv": f32("L1_attn_bv"),
                "wo": f32("L1_attn_Wo"), "bo": f32("L1_attn_bo"),
                "lng": f32("L1_attn_ln_g"), "lnb": f32("L1_attn_ln_b")}
        return [dict(base, x=xc[c]) for c in range(NCORES)]

    def l1ffn_step(xc):
        lg = _run("l1log", cfg, l1log_maps(xc))
        lg_full = np.ascontiguousarray(
            np.concatenate([lg[c]["lo"] for c in range(NCORES)], axis=0))
        ga = _run("l1route", cfg, [{"lo": lg_full} for _ in range(NCORES)])
        ga_full = ga[0]["ga"]
        gc = [np.ascontiguousarray(ga_full[c * T:(c + 1) * T])
              for c in range(NCORES)]
        r = _run("l1ffn", cfg, l1ffn_maps(xc, gc))
        return [r[c]["y"] for c in range(NCORES)]

    r = _run("l0ffn", cfg, l0ffn_maps(xs))
    xs = [r[c]["y"] for c in range(NCORES)]
    r = _run("l0attn", cfg, l0attn_maps(xs))
    xs = [r[c]["y"] for c in range(NCORES)]
    r = _run("l0ffn", cfg, l0ffn_maps(xs))
    xs = [r[c]["y"] for c in range(NCORES)]
    xs = l1ffn_step(xs)
    r = _run("l1attn", cfg, l1attn_maps(xs))
    xs = [r[c]["y"] for c in range(NCORES)]
    xs = l1ffn_step(xs)

    out = np.concatenate([x.reshape(2, S, d) for x in xs], axis=0)
    return out.astype(np.float32)



class LP:

    GD = 6
    LV = 3
    LA = 2
    LT = 2

    def __init__(self, p):
        self.p = p
        self.nc = p.nc
        self.s = {}
        for i in range(self.GD):
            self.s[f"gd{i}"] = [p.es.enter_context(p.nc.semaphore(f"gd{i}")), 0]
        for i in range(self.LV):
            self.s[f"lv{i}"] = [p.es.enter_context(p.nc.semaphore(f"lv{i}")), 0]
        for i in range(self.LA):
            self.s[f"la{i}"] = [p.es.enter_context(p.nc.semaphore(f"la{i}")), 0]
        for i in range(self.LT):
            self.s[f"lt{i}"] = [p.es.enter_context(p.nc.semaphore(f"lt{i}")), 0]
        for i in range(2):
            self.s[f"ls{i}"] = [p.es.enter_context(p.nc.semaphore(f"ls{i}")), 0]
            self.s[f"gp{i}"] = [p.es.enter_context(p.nc.semaphore(f"gp{i}")), 0]
        self.s["ph"] = [p.es.enter_context(p.nc.semaphore("ph")), 0]
        self.nph = 0

    def h(self, n):
        return self.s[n][0]

    def base(self, n):
        return self.s[n][1]

    def bump(self, n, total):
        self.s[n][1] += total
        return self.s[n][1]

    def phase_end(self, closer="vector"):
        eng = getattr(self.nc, closer)
        for n, (hd, v) in self.s.items():
            if n != "ph" and v > 0:
                eng.wait_ge(hd, v)
        sc = self.p.sched
        for n in sorted(sc.sems):
            hd, v = sc.sems[n]
            if v > 0:
                eng.wait_ge(hd, v)
        self.nph += 1
        eng.nop().then_inc(self.s["ph"][0], 1)
        self.s["ph"][1] = self.nph

    def phase_begin(self, engines=("sync", "gpsimd", "tensor", "vector", "scalar")):
        for en in engines:
            getattr(self.nc, en).wait_ge(self.s["ph"][0], self.nph)

    def sched_sync(self):
        sc = self.p.sched
        for n, (hd, v) in self.s.items():
            if v > 0:
                if n not in sc.sems:
                    sc.sems[n] = [hd, 0]
                if sc.sems[n][1] != v:
                    sc.sems[n][1] = v
                    st = Step()
                    st.eng, st.sem, st.val = "gpsimd", n, v
                    st.idx = len(sc.steps)
                    st.fn, st.deps, st.n_dma, st.note = None, {}, None, "lp"
                    sc.steps.append(st)
                    sc.last_by_sem[n] = st



def prebump(nc, lp, pairs, eng="vector"):
    e = getattr(nc, eng)
    out = {}
    for name, delta in pairs:
        rem = delta
        while rem > 0:
            step = min(rem, 7)
            e.sem_inc(lp.h(name), step)
            rem -= step
        if delta > 0:
            out[name] = lp.bump(name, delta)
        else:
            out[name] = lp.base(name)
    return out


def lp_ln(p, lp, rc, x_d, fm_d=None, tm_d=None, eps=1e-5, g_d=None, b_d=None,
          tag="ln"):
    nc = p.nc
    cfg = p.cfg
    d, ncd = cfg.d, cfg.nc_d
    nt = cfg.nt
    grp = 4
    assert nt % (2 * grp) == 0
    niter = nt // (2 * grp)
    affine = g_d is not None
    fm = fm_d is not None
    tm = tm_d is not None
    es = ExitStack()
    t_ = lambda n, sh, dt: es.enter_context(nc.sbuf_tensor(tag + n, sh, dt))
    xg = t_("_x", [128, 2 * grp * d], F32)
    yg = t_("_y", [128, 2 * grp * d], F32)
    stt = t_("_st", [128, 2 * 8 * grp], F32)
    yb = t_("_yb", [128, 2 * grp * d], BF16) if fm else None
    gb = t_("_gb", [128, 2 * d], F32) if affine else None
    fms = t_("_fm", [128, 2 * grp * d], BF16) if fm else None
    ptp = es.enter_context(nc.psum_tensor(tag + "_pt", [128, 1024], BF16)) \
        if fm else None

    GD, GS, LV, LA, LT, GP = "gd0", "gd1", "lv0", "la0", "lt0", "gp0"
    vh = 6 + (2 if affine else 0) + ((1 + grp) if fm else 0)
    ah = 2
    th = grp if fm else 0
    gh = (grp if tm else 0) * 16 + (grp if fm else 0) * 16
    ph_ = 1 if gh else 0
    D_LV, D_LA, D_LT, D_GD, D_GS, D_GP = 2 * vh, 2 * ah, 2 * th, 2 * grp * 16, \
        2 * gh, 2 * ph_
    if affine:
        nc.sync.dma_start(out=gb[:, 0:d], in_=bcast_row(g_d, 128)) \
            .then_inc(lp.h("ls0"), 16)
        nc.sync.dma_start(out=gb[:, d:2 * d], in_=bcast_row(b_d, 128)) \
            .then_inc(lp.h("ls0"), 16)
        bLS = lp.bump("ls0", 32)

    bb = prebump(nc, lp, [(LV, D_LV), (LA, D_LA), (LT, D_LT), (GP, D_GP)])
    bLV, bLA, bLT, bGP = bb[LV], bb[LA], bb[LT], bb[GP]
    bGD, bGS = lp.base(GD), lp.base(GS)
    if affine:
        nc.vector.wait_ge(lp.h("ls0"), bLS)

    lv_off = lambda hf, k: hf * vh + k
    la_off = lambda hf, k: hf * ah + k
    lt_off = lambda hf, k: hf * th + k
    gd_off = lambda hf: (hf + 1) * grp * 16
    gs_off = lambda hf, k: hf * gh + 16 * k
    o_y = 6
    o_g = 7
    o_b = 8
    o_cast = 6 + (2 if affine else 0) + 1
    o_cp0 = o_cast

    with fori2(p, niter) as i:
        for hf in range(2):
            o = hf * grp
            so = hf * 8 * grp
            rc.wait("gpsimd", lp.h(LV), i, D_LV, bLV - D_LV + lv_off(hf, o_y))
            if gh:
                rc.wait("gpsimd", lp.h(GP), i, D_GP, bGP - D_GP + hf * ph_ + ph_)
            for j in range(grp):
                row = rc.off("gpsimd", i, 2 * grp * 128 * d,
                             (hf * grp + j) * 128 * d, (p.cfg.T - 128) * d)
                dyn_dma(nc, "gpsimd", xg[:, (o + j) * d:(o + j + 1) * d],
                        dyn_ap(x_d, row, [[d, 128], [1, d]]), lp.h(GD))
            rc.wait("scalar", lp.h(GD), i, D_GD, bGD + gd_off(hf))
            rc.wait("scalar", lp.h(LV), i, D_LV, bLV - D_LV + lv_off(hf, o_cast if fm else o_y))
            if gh:
                rc.wait("scalar", lp.h(GP), i, D_GP, bGP - D_GP + hf * ph_ + ph_)
            for j in range(grp):
                ins = nc.scalar.activation(
                    yg[:, (o + j) * d:(o + j + 1) * d],
                    xg[:, (o + j) * d:(o + j + 1) * d], AF.Square,
                    accum_out=stt[:, so + grp + j:so + grp + j + 1])
            ins.then_inc(lp.h(LA), 1)
            rc.wait("vector", lp.h(GD), i, D_GD, bGD + gd_off(hf))
            for j in range(grp):
                ins = nc.vector.tensor_reduce(
                    stt[:, so + j:so + j + 1],
                    xg[:, (o + j) * d:(o + j + 1) * d], axis=AX.X, op=ALU.add)
            ins.then_inc(lp.h(LV), 1)
            rc.wait("vector", lp.h(LV), i, D_LV, bLV + lv_off(hf, 1))
            nc.vector.tensor_scalar(stt[:, so:so + grp], stt[:, so:so + grp],
                                    1.0 / d, None, ALU.mult).then_inc(lp.h(LV), 1)
            rc.wait("vector", lp.h(LA), i, D_LA, bLA + la_off(hf, 1))
            rc.wait("vector", lp.h(LV), i, D_LV, bLV + lv_off(hf, 2))
            ins = nc.vector.tensor_scalar(
                stt[:, so + 3 * grp:so + 4 * grp], stt[:, so + grp:so + 2 * grp],
                1.0 / d, eps, ALU.mult, ALU.add)
            nc.vector.tensor_tensor(
                stt[:, so + 2 * grp:so + 3 * grp], stt[:, so:so + grp],
                stt[:, so:so + grp], ALU.mult).then_inc(lp.h(LV), 1)
            rc.wait("vector", lp.h(LV), i, D_LV, bLV + lv_off(hf, 3))
            nc.vector.tensor_sub(
                stt[:, so + 3 * grp:so + 4 * grp],
                stt[:, so + 3 * grp:so + 4 * grp],
                stt[:, so + 2 * grp:so + 3 * grp]).then_inc(lp.h(LV), 1)
            rc.wait("scalar", lp.h(LV), i, D_LV, bLV + lv_off(hf, 4))
            nc.scalar.sqrt(stt[:, so + 4 * grp:so + 5 * grp],
                           stt[:, so + 3 * grp:so + 4 * grp]).then_inc(lp.h(LA), 1)
            rc.wait("vector", lp.h(LA), i, D_LA, bLA + la_off(hf, 2))
            nc.vector.reciprocal(stt[:, so + 4 * grp:so + 5 * grp],
                                 stt[:, so + 4 * grp:so + 5 * grp]).then_inc(lp.h(LV), 1)
            rc.wait("vector", lp.h(LV), i, D_LV, bLV + lv_off(hf, 5))
            for j in range(grp):
                ins = nc.vector.tensor_scalar(
                    yg[:, (o + j) * d:(o + j + 1) * d],
                    xg[:, (o + j) * d:(o + j + 1) * d],
                    stt[:, so + j:so + j + 1],
                    stt[:, so + 4 * grp + j:so + 4 * grp + j + 1],
                    ALU.subtract, ALU.mult)
            ins.then_inc(lp.h(LV), 1)
            prev_lv = o_y
            if affine:
                rc.wait("vector", lp.h(LV), i, D_LV, bLV + lv_off(hf, o_y))
                for j in range(grp):
                    ins = nc.vector.tensor_tensor(
                        yg[:, (o + j) * d:(o + j + 1) * d],
                        yg[:, (o + j) * d:(o + j + 1) * d], gb[:, 0:d], ALU.mult)
                ins.then_inc(lp.h(LV), 1)
                rc.wait("vector", lp.h(LV), i, D_LV, bLV + lv_off(hf, o_g))
                for j in range(grp):
                    ins = nc.vector.tensor_tensor(
                        yg[:, (o + j) * d:(o + j + 1) * d],
                        yg[:, (o + j) * d:(o + j + 1) * d], gb[:, d:2 * d], ALU.add)
                ins.then_inc(lp.h(LV), 1)
                prev_lv = o_b
            if fm:
                rc.wait("vector", lp.h(LV), i, D_LV, bLV + lv_off(hf, prev_lv))
                rc.wait("vector", lp.h(LT), i, D_LT, bLT - D_LT + lt_off(hf, th))
                for j in range(grp):
                    ins = nc.vector.tensor_copy(
                        yb[:, (o + j) * d:(o + j + 1) * d],
                        yg[:, (o + j) * d:(o + j + 1) * d])
                ins.then_inc(lp.h(LV), 1)
                for j in range(grp):
                    rc.wait("tensor", lp.h(LV), i, D_LV, bLV + lv_off(hf, o_cast))
                    if j > 0:
                        rc.wait("tensor", lp.h(LV), i, D_LV, bLV + lv_off(hf, o_cp0 + j))
                    elif hf == 1:
                        rc.wait("tensor", lp.h(LV), i, D_LV, bLV + lv_off(0, o_cp0 + grp))
                    else:
                        rc.wait("tensor", lp.h(LV), i, D_LV, bLV - D_LV + lv_off(1, o_cp0 + grp))
                    for c in range(ncd):
                        ins = nc.tensor.transpose(
                            ptp[:, c * 128:(c + 1) * 128],
                            yb[:, (o + j) * d + c * 128:(o + j) * d + (c + 1) * 128],
                            p.identb[:, :])
                    ins.then_inc(lp.h(LT), 1)
                    rc.wait("vector", lp.h(LT), i, D_LT, bLT + lt_off(hf, j + 1))
                    if gh:
                        rc.wait("vector", lp.h(GP), i, D_GP, bGP - D_GP + hf * ph_ + ph_)
                    nc.vector.tensor_copy(fms[:, (o + j) * d:(o + j + 1) * d],
                                          ptp[:, 0:d]).then_inc(lp.h(LV), 1)
            nst = 0
            if tm:
                rc.wait("gpsimd", lp.h(LV), i, D_LV, bLV + lv_off(hf, prev_lv))
                for j in range(grp):
                    row = rc.off("gpsimd", i, 2 * grp * 128 * d,
                                 (hf * grp + j) * 128 * d, (p.cfg.T - 128) * d)
                    dyn_dma(nc, "gpsimd", dyn_ap(tm_d, row, [[d, 128], [1, d]]),
                            yg[:, (o + j) * d:(o + j + 1) * d], lp.h(GS))
                    nst += 1
            if fm:
                for j in range(grp):
                    rc.wait("gpsimd", lp.h(LV), i, D_LV, bLV + lv_off(hf, o_cp0 + j + 1))
                    dyn_dma(nc, "gpsimd",
                            dyn_ap(fm_d,
                                   rc.off("gpsimd", i, 2 * grp * 128,
                                          hf * grp * 128 + j * 128,
                                          p.cfg.T - 128),
                                   [[p.cfg.T, 128], [128 * p.cfg.T, ncd],
                                    [1, 128]]),
                            fms[:, (o + j) * d:(o + j + 1) * d]
                            .rearrange("p (c x) -> p c x", c=ncd), lp.h(GS))
                    nst += 1
            if nst:
                rc.wait("gpsimd", lp.h(GS), i, D_GS, bGS + gs_off(hf, nst))
                nc.gpsimd.sem_inc(lp.h(GP), 1)
    lp.bump(GD, niter * D_GD)
    lp.bump(GS, niter * D_GS)
    lp.bump(LV, niter * D_LV)
    lp.bump(LA, niter * D_LA)
    lp.bump(LT, niter * D_LT)
    lp.bump(GP, niter * D_GP)
    es.close()


def finalize_lp(p, lp):
    nc = p.nc
    for n, (h, v) in sorted(lp.s.items()):
        if v > 0:
            nc.gpsimd.wait_ge(h, v)
    for n in sorted(p.sched.sems):
        h, v = p.sched.sems[n]
        if v > 0 and n not in lp.s:
            nc.gpsimd.wait_ge(h, v)



def lp_w_setup(p, lp, nc, w_d, din, dout, wbf, wst, wst_nb, wsl):
    nci = din // 128
    LS, LV = "ls1", "lv1"
    bLS = lp.base(LS)
    nwv = 0
    for k in range(nci):
        if k >= 1:
            nc.sync.wait_ge(lp.h(LV), lp.base(LV) + nwv)
        nc.sync.dma_start(out=wst[:, (k % wst_nb) * wsl:(k % wst_nb) * wsl + dout],
                          in_=w_d[k * 128:(k + 1) * 128, 0:dout]) \
            .then_inc(lp.h(LS), 16)
        nc.vector.wait_ge(lp.h(LS), bLS + 16 * (k + 1))
        nc.vector.tensor_copy(wbf[:, k * dout:(k + 1) * dout],
                              wst[:, (k % wst_nb) * wsl:(k % wst_nb) * wsl + dout]) \
            .then_inc(lp.h(LV), 1)
        nwv += 1
    lp.bump(LS, 16 * nci)
    return lp.bump(LV, nwv)


def lp_proj(p, lp, rc, xfm_d, w_d, din, dout, b_d=None, out_fm_d=None,
            out_tm_d=None, act=None, tag="pj"):
    nc = p.nc
    cfg = p.cfg
    nci, nco = din // 128, dout // 128
    T = cfg.T
    es = ExitStack()
    t_ = lambda n, sh, dt: es.enter_context(nc.sbuf_tensor(tag + n, sh, dt))
    wbf = t_("_w", [128, nci * dout], BF16)
    wsl = max(din, dout)
    wst_nb = 2
    wst = t_("_ws", [128, wst_nb * wsl], F32)
    pj = es.enter_context(nc.psum_tensor(tag + "_pj", [128, 8 * 512], F32))
    LS, GD, GS, LV, LA, PE, GP = "ls1", "gd2", "gd3", "lv1", "la1", "lt1", "gp1"

    bW = lp_w_setup(p, lp, nc, w_d, din, dout, wbf, wst, wst_nb, wsl)
    nbs = 0
    bcol = t_("_bc", [128, max(nco, 8)], F32)
    bro = t_("_br", [128, dout], F32) if out_tm_d is not None else None
    if b_d is not None:
        if out_fm_d is not None:
            for m in range(nco):
                nc.sync.dma_start(out=bcol[:, m:m + 1],
                                  in_=b_d[m * 128:(m + 1) * 128].unsqueeze(1)) \
                    .then_inc(lp.h(LS), 16)
                nbs += 16
        if out_tm_d is not None:
            nc.sync.dma_start(out=bro[:, 0:dout], in_=bcast_row(b_d, 128)) \
                .then_inc(lp.h(LS), 16)
            nbs += 16
    bB = lp.bump(LS, nbs)
    AFm = {None: AF.Identity, "gelu": AF.Gelu, "relu": AF.Relu}[act]

    if out_fm_d is not None:
        GPH = T // 512
        HPB = 2
        niter = GPH // HPB
        NSG = nco // 4
        SGB = HPB * NSG
        og = t_("_o", [128, 2 * nco * 512], BF16) if nco <= 8 else \
            t_("_o", [128, 2 * 32 // 2 * 512], BF16)
        nog = 2 * nco if nco <= 8 else 32
        og = t_("_og", [128, nog * 512], BF16)
        xg = t_("_x", [128, HPB * nci * 512], BF16)
        D_GD, D_PE, D_GS, D_GP = HPB * nci * 16, SGB, HPB * nco * 16, HPB
        D_LA = SGB
        bb = prebump(nc, lp, [(PE, D_PE), (LA, D_LA), (GP, D_GP)])
        bPE, bLA, bGP = bb[PE], bb[LA], bb[GP]
        bGD, bGS = lp.base(GD), lp.base(GS)
        nc.scalar.wait_ge(lp.h(LS), max(bB, 1))
        nc.scalar.wait_ge(lp.h(LV), bW)
        nc.tensor.wait_ge(lp.h(LV), bW)
        with fori2(p, niter) as i:
            for hf in range(HPB):
                rc.wait("gpsimd", lp.h(PE), i, D_PE, bPE - D_PE + hf * NSG + NSG)
                for k in range(nci):
                    dyn_dma(nc, "gpsimd",
                            xg[:, (hf * nci + k) * 512:(hf * nci + k + 1) * 512],
                            dyn_ap(xfm_d,
                                   rc.off("gpsimd", i, HPB * 512,
                                          hf * 512 + k * 128 * T,
                                          (nci - 1) * 128 * T + T - 512),
                                   [[T, 128], [1, 512]]),
                            lp.h(GD))
            for hf in range(HPB):
                for sg in range(NSG):
                    s_in_body = hf * NSG + sg
                    pbank = (s_in_body % 2) * 4
                    (rc.wait("tensor", lp.h(LA), i, D_LA, bLA - D_LA + s_in_body - 2 + D_LA + 1) if s_in_body - 2 < 0 else rc.wait("tensor", lp.h(LA), i, D_LA, bLA + s_in_body - 2 + 1))
                    rc.wait("tensor", lp.h(GD), i, D_GD, bGD + (hf + 1) * nci * 16)
                    for mj in range(4):
                        m = sg * 4 + mj
                        for k in range(nci):
                            ins = nc.tensor.matmul(
                                pj[:, (pbank + mj) * 512:(pbank + mj + 1) * 512],
                                wbf[:, k * dout + m * 128:k * dout + (m + 1) * 128],
                                xg[:, (hf * nci + k) * 512:(hf * nci + k + 1) * 512],
                                start=(k == 0), stop=(k == nci - 1))
                    ins.then_inc(lp.h(PE), 1)
                    rc.wait("scalar", lp.h(PE), i, D_PE, bPE + s_in_body + 1)
                    rc.wait("scalar", lp.h(GP), i, D_GP, bGP - D_GP + hf + 1)
                    for mj in range(4):
                        m = sg * 4 + mj
                        osl = (hf * nco + m) if nco <= 8 else m
                        kw = {}
                        if b_d is not None:
                            kw["bias"] = bcol[:, m:m + 1]
                        ins = nc.scalar.activation(
                            og[:, osl * 512:(osl + 1) * 512],
                            pj[:, (pbank + mj) * 512:(pbank + mj + 1) * 512],
                            AFm, **kw)
                    ins.then_inc(lp.h(LA), 1)
                    rc.wait("gpsimd", lp.h(LA), i, D_LA, bLA + s_in_body + 1)
                    for mj in range(4):
                        m = sg * 4 + mj
                        osl = (hf * nco + m) if nco <= 8 else m
                        dyn_dma(nc, "gpsimd",
                                dyn_ap(out_fm_d,
                                       rc.off("gpsimd", i, HPB * 512,
                                              hf * 512 + m * 128 * T,
                                              (nco - 1) * 128 * T + T - 512),
                                       [[T, 128], [1, 512]]),
                                og[:, osl * 512:(osl + 1) * 512], lp.h(GS))
                rc.wait("gpsimd", lp.h(GS), i, D_GS, bGS + (hf + 1) * nco * 16)
                nc.gpsimd.sem_inc(lp.h(GP), 1)
        lp.bump(GD, niter * D_GD)
        lp.bump(PE, niter * D_PE)
        lp.bump(LA, niter * D_LA)
        lp.bump(GS, niter * D_GS)
        lp.bump(GP, niter * D_GP)

    if out_tm_d is not None:
        og2 = t_("_o2", [128, 2 * dout], BF16)
        xg2 = t_("_x2", [128, 2 * nci * 128], BF16)
        ncg = dout // 512
        niter = (T // 128) // 2
        nxd = max(nci // 8, 1)
        D_GD, D_PE, D_GS, D_GP, D_LV = 2 * nxd * 16, 2, 2 * 16, 2, 2
        bb = prebump(nc, lp, [(PE, D_PE), (LV, D_LV), (GP, D_GP)])
        bPE, bLV, bGP = bb[PE], bb[LV], bb[GP]
        bGD2, bGS2 = lp.base(GD), lp.base(GS)
        nc.vector.wait_ge(lp.h(LS), max(bB, 1))
        nc.vector.wait_ge(lp.h(LV), bW)
        nc.tensor.wait_ge(lp.h(LV), bW)
        with fori2(p, niter) as i:
            for hf in range(2):
                rc.wait("gpsimd", lp.h(PE), i, D_PE, bPE - D_PE + hf + 1)
                q8 = min(nci, 8)
                for dk in range(nxd):
                    dyn_dma(nc, "gpsimd",
                            xg2[:, (hf * nci + dk * q8) * 128:
                                (hf * nci + dk * q8 + q8) * 128]
                            .rearrange("p (c x) -> p c x", c=q8),
                            dyn_ap(xfm_d,
                                   rc.off("gpsimd", i, 256,
                                          hf * 128 + dk * q8 * 128 * T,
                                          (nci - 1) * 128 * T + T - 128),
                                   [[T, 128], [128 * T, q8], [1, 128]]),
                            lp.h(GD))
            for hf in range(2):
                rc.wait("tensor", lp.h(GD), i, D_GD, bGD2 + (hf + 1) * nxd * 16)
                rc.wait("tensor", lp.h(LV), i, D_LV, bLV - D_LV + hf + 1)
                for cg in range(ncg):
                    for k in range(nci):
                        ins = nc.tensor.matmul(
                            pj[:, (hf * ncg + cg) * 512:(hf * ncg + cg + 1) * 512],
                            xg2[:, (hf * nci + k) * 128:(hf * nci + k + 1) * 128],
                            wbf[:, k * dout + cg * 512:k * dout + (cg + 1) * 512],
                            start=(k == 0), stop=(k == nci - 1))
                ins.then_inc(lp.h(PE), 1)
                rc.wait("vector", lp.h(PE), i, D_PE, bPE + hf + 1)
                rc.wait("vector", lp.h(GP), i, D_GP, bGP - D_GP + hf + 1)
                if b_d is not None:
                    nc.vector.tensor_tensor(og2[:, hf * dout:(hf + 1) * dout],
                                            pj[:, hf * dout:(hf + 1) * dout],
                                            bro[:, 0:dout], ALU.add) \
                        .then_inc(lp.h(LV), 1)
                else:
                    nc.vector.tensor_copy(og2[:, hf * dout:(hf + 1) * dout],
                                          pj[:, hf * dout:(hf + 1) * dout]) \
                        .then_inc(lp.h(LV), 1)
                rc.wait("gpsimd", lp.h(LV), i, D_LV, bLV + hf + 1)
                dyn_dma(nc, "gpsimd",
                        dyn_ap(out_tm_d,
                               rc.off("gpsimd", i, 256 * dout, hf * 128 * dout,
                                      (T - 128) * dout),
                               [[dout, 128], [1, dout]]),
                        og2[:, hf * dout:(hf + 1) * dout], lp.h(GS))
                rc.wait("gpsimd", lp.h(GS), i, D_GS, bGS2 + (hf + 1) * 16)
                nc.gpsimd.sem_inc(lp.h(GP), 1)
        lp.bump(GD, niter * D_GD)
        lp.bump(PE, niter * D_PE)
        lp.bump(LV, niter * D_LV)
        lp.bump(GS, niter * D_GS)
        lp.bump(GP, niter * D_GP)
    es.close()



from contextlib import contextmanager


@contextmanager
def fori2(p, n, level=0):
    nc = p.nc
    if not hasattr(p, "_fori_regs"):
        p._fori_regs = {}
        p._fori_ctr = 0
    if n == 1:
        p._fori_ctr += 1
        yield 0
        return
    if level not in p._fori_regs:
        p._fori_regs[level] = nc.alloc_registers(f"fori2_l{level}")
    registers = p._fori_regs[level]
    name = f"f2_{p._fori_ctr}"
    p._fori_ctr += 1
    loop_start, loop_end = name + "_loop", name + "_end"
    nc.regs_mov(registers, 0)
    nc.br(loop_start)
    with nc.body(loop_start):
        sv = nc.snap(registers, min_val=0, max_val=n - 1)
        yield sv
        nc.regs_alu(registers, registers, 1, op=ALU.add)
        nc.br_lt(registers, n, on_true=loop_start, on_false=loop_end)
    nc.switch_bb(loop_end)
    _ENG_BY_ET = {"Pool": "gpsimd", "DVE": "vector", "Activation": "scalar",
                  "PE": "tensor", "SP": "sync"}
    if not hasattr(sv, "val") or type(sv.val).__name__ != "RegisterHandles":
        return
    for h in sv.val.handles:
        hh = h
        if type(hh).__name__ == "Register64Pair":
            continue
        en = _ENG_BY_ET.get(str(hh.engine).split(".")[-1])
        if en:
            try:
                getattr(nc, en).free_register(hh)
            except Exception:
                pass


class RC:
    KEEP = 4

    def __init__(self, p):
        self.p = p
        self.r = {}
        self.pend = {}

    def _get(self, eng):
        if eng not in self.r:
            e = getattr(self.p.nc, eng)
            self.r[eng] = ([e.alloc_register(f"rc_{eng}_{k}") for k in range(3)],
                           [None, None])
            self.pend[eng] = []
        return self.r[eng]

    def _mul(self, eng, i, D):
        e = getattr(self.p.nc, eng)
        regs, cache = self._get(eng)
        key = (id(i), D, getattr(self.p, "_fori_ctr", 0))
        if cache[0] != key:
            e.reg_alu(regs[0], i, D, ALU.mult)
            cache[0] = key
        return regs[0]

    def wait(self, eng, sem_h, i, D, c):
        e = getattr(self.p.nc, eng)
        regs, _ = self._get(eng)
        r0 = self._mul(eng, i, D)
        e.reg_alu(regs[1], r0, c, ALU.add)
        e.wait_ge(sem_h, regs[1])

    def _snap(self, eng, ro, maxv):
        e = getattr(self.p.nc, eng)
        s = e.snap(ro, min_val=0, max_val=maxv)
        pend = self.pend[eng]
        pend.append(s.val)
        if len(pend) > self.KEEP:
            e.free_register(pend.pop(0))
        return s

    def off(self, eng, i, D, c, maxv):
        e = getattr(self.p.nc, eng)
        regs, _ = self._get(eng)
        r0 = self._mul(eng, i, D)
        e.reg_alu(regs[2], r0, c, ALU.add)
        return self._snap(eng, regs[2], maxv)

    def off2(self, eng, i, Di, j, Dj, c, maxv):
        e = getattr(self.p.nc, eng)
        regs, _ = self._get(eng)
        r0 = self._mul(eng, i, Di)
        e.reg_alu(regs[1], j, Dj, ALU.mult)
        e.reg_alu(regs[1], regs[1], c, ALU.add)
        e.reg_alu(regs[2], r0, regs[1], ALU.add)
        return self._snap(eng, regs[2], maxv)


from concourse.bass_primitives_rust import RegisterHandle as _RegH

_ENG_ET = {"gpsimd": mybir.EngineType.Pool, "sync": mybir.EngineType.SP,
           "vector": mybir.EngineType.DVE, "scalar": mybir.EngineType.Activation,
           "tensor": mybir.EngineType.PE}


def dyn_ap(handle, off, pattern):
    h = handle.tensor if isinstance(handle, bass.AP) else handle
    return bass.AP(h, off, pattern)


def dyn_dma(nc, eng, out, in_, sem=None, inc=16, **kw):
    e = getattr(nc, eng)
    ins = e.dma_start(out=out, in_=in_, **kw)
    if sem is not None:
        ins.then_inc(sem, inc)
    et = _ENG_ET[eng]
    for a in list(ins.ins.ins) + list(ins.ins.outs):
        if type(a).__name__ == "RegisterAccessPattern" and "_tmp_" in a.regref:
            e.free_register(_RegH(a.regref, et))
    return ins



def lp_moe(p, lp, rc, xin_fm_d, we1_d, be1_d, we2_d, be2_d, gates_d, h_fm_d,
           ye_ds, acc_init_tm_d, acc_out_tm_d, act="gelu", tag="moe"):
    cfg = p.cfg
    d, F, E = cfg.d, cfg.F, cfg.E
    for ei in range(E):
        lp_proj(p, lp, rc, xin_fm_d, we1_d[ei], d, F, b_d=be1_d[ei],
                out_fm_d=h_fm_d, act=act, tag=f"{tag}u{ei}")
        lp.phase_end()
        lp.phase_begin()
        lp_proj(p, lp, rc, h_fm_d, we2_d[ei], F, d, b_d=be2_d[ei],
                out_tm_d=ye_ds[ei], tag=f"{tag}d{ei}")
        lp.phase_end()
        lp.phase_begin()


def ph_moe_combine(p, gates_d, ye_ds, acc_init_tm_d, acc_out_tm_d, tag="mc"):
    cfg = p.cfg
    sc = p.sched
    d, E, nt = cfg.d, cfg.E, cfg.nt
    fence = sc.lasts()
    with ExitStack() as es:
        t_ = lambda n, sh, dt: es.enter_context(p.nc.sbuf_tensor(tag + n, sh, dt))
        ag = t_("_a", [128, 2 * d], F32)
        yg = t_("_y", [128, 2 * E * d], BF16)
        gg = t_("_g", [128, 2 * E], F32)
        prev = {}
        for t in range(nt):
            sl = t % 2

            def f_ld(e, dma, t=t, sl=sl):
                if acc_init_tm_d is not None:
                    dma(out=ag[:, sl * d:(sl + 1) * d],
                        in_=acc_init_tm_d[t * 128:(t + 1) * 128, 0:d])
                dma(out=gg[:, sl * E:(sl + 1) * E],
                    in_=gates_d[t * 128:(t + 1) * 128, 0:E])
                for ei in range(E):
                    dma(out=yg[:, (sl * E + ei) * d:(sl * E + ei + 1) * d],
                        in_=ye_ds[ei][t * 128:(t + 1) * 128, 0:d])
            deps = list(fence)
            if t >= 2:
                deps.append(prev[(t - 2, "st")])
            nd = (1 if acc_init_tm_d is not None else 0) + 1 + E
            s_ld = sc.dma("sync", f_ld, nd, deps=deps, note=f"{tag}_ld{t}")

            if acc_init_tm_d is None:
                def f_z(e, sl=sl):
                    return e.memset(ag[:, sl * d:(sl + 1) * d], 0.0)
                dd = [s_ld] + ([prev[(t - 2, "st")]] if t >= 2 else [])
                s_prev = sc.step("vector", f_z, deps=dd, note=f"{tag}_z{t}")
            else:
                s_prev = s_ld
            for ei in range(E):
                def f_acc(e, t=t, sl=sl, ei=ei):
                    return e.scalar_tensor_tensor(
                        ag[:, sl * d:(sl + 1) * d],
                        yg[:, (sl * E + ei) * d:(sl * E + ei + 1) * d],
                        gg[:, sl * E + ei:sl * E + ei + 1],
                        ag[:, sl * d:(sl + 1) * d], ALU.mult, ALU.add)
                s_prev = sc.step("vector", f_acc, deps=[s_prev, s_ld],
                                 note=f"{tag}_a{t}_{ei}")

            def f_st(e, dma, t=t, sl=sl):
                dma(out=acc_out_tm_d[t * 128:(t + 1) * 128, 0:d],
                    in_=ag[:, sl * d:(sl + 1) * d])
            prev[(t, "st")] = sc.dma("gpsimd", f_st, 1, deps=[s_prev],
                                     note=f"{tag}_st{t}")
    return sc.lasts()



def lp_attn(p, lp, rc, qfm_d, kfm_d, vtm_d, ctx_tm_d, kind, scale, tag):
    nc = p.nc
    cfg = p.cfg
    S, d, nh, dh, ncd = cfg.S, cfg.d, cfg.nh, cfg.dh, cfg.nc_d
    T = cfg.T
    if kind == "std":
        KEYS = S
        insts = [(s, h) for s in range(cfg.nseq) for h in range(nh)]
    else:
        KEYS = (d // dh) * 128
        insts = [(s, w) for s in range(cfg.nseq) for w in range(S // 128)]
    RB = KEYS // 128
    C = KEYS // 128
    es = ExitStack()
    t_ = lambda n, sh, dt: es.enter_context(nc.sbuf_tensor(tag + n, sh, dt))
    ksb = t_("_k", [128, 2 * KEYS], BF16)
    vsb = t_("_v", [128, 2 * C * 64], BF16)
    qt = t_("_qt", [128, 128], BF16)
    atn = t_("_at", [128, KEYS], BF16)
    att = t_("_tt", [128, C * 128], BF16)
    cst = t_("_cs", [128, 64], BF16)
    stt = t_("_st", [128, 8], F32)
    psc = es.enter_context(nc.psum_tensor(tag + "_ps", [128, KEYS], F32))
    ptp = es.enter_context(nc.psum_tensor(tag + "_pt", [128, KEYS], BF16))
    pcx = es.enter_context(nc.psum_tensor(tag + "_px", [128, 64], F32))

    LS, QG, CS, PEA, LVA, LAA = "ls0", "gd4", "gd5", "lt0", "lv2", "la0"
    NTG = C // 4
    dQG, dCS, dLA = 16, 16, 1
    dPE = 3
    dLV = 5
    bb = prebump(nc, lp, [(PEA, RB * dPE), (LVA, RB * dLV), (LAA, RB * dLA)],
                 eng="vector")
    for ii, (s, hw) in enumerate(insts):
        sl = ii % 2
        qo, vo = sl * KEYS, sl * C * 64
        bQG, bCS = lp.base(QG), lp.base(CS)
        bPE, bLV, bLA = lp.base(PEA), lp.base(LVA), lp.base(LAA)
        if ii >= 2:
            nc.sync.wait_ge(lp.h(PEA), bPE - RB * dPE)
        if kind == "std":
            h = hw
            nc.sync.dma_start(out=ksb[0:64, qo:qo + S],
                              in_=kfm_d[h // 2, (h % 2) * 64:(h % 2) * 64 + 64,
                                        s * S:(s + 1) * S]).then_inc(lp.h(LS), 16)
            q4 = max(C // 4, 1)
            for pc in range(0, C, q4):
                nc.sync.dma_start(
                    out=vsb[:, vo + pc * 64:vo + (pc + q4) * 64],
                    in_=vtm_d[s * S + pc * 128:s * S + (pc + q4) * 128,
                              h * 64:(h + 1) * 64]
                    .rearrange("(c p) x -> p c x", p=128)).then_inc(lp.h(LS), 16)
            nld = 1 + (C // q4)
        else:
            w = hw
            nc.sync.dma_start(out=ksb[0:64, qo:qo + KEYS]
                              .rearrange("p (c h x) -> p c h x", c=ncd, h=2),
                              in_=kfm_d[0:ncd, 0:128,
                                        s * S + w * 128:s * S + (w + 1) * 128]
                              .rearrange("c (h p) x -> p c h x", h=2)) \
                .then_inc(lp.h(LS), 16)
            nc.sync.dma_start(out=vsb[:, vo:vo + C * 64],
                              in_=vtm_d[s * S + w * 128:s * S + (w + 1) * 128, 0:d]) \
                .then_inc(lp.h(LS), 16)
            nld = 2
        bLS = lp.bump(LS, nld * 16)

        with fori2(p, RB, level=1) as rb:
            rc.wait("gpsimd", lp.h(PEA), rb, dPE, bPE - dPE + 1)
            if kind == "std":
                dyn_dma(nc, "gpsimd", qt[0:64, 0:128],
                        dyn_ap(qfm_d,
                               rc.off("gpsimd", rb, 128, h * 64 * T + s * S,
                                      d * T - 128), [[T, 64], [1, 128]]),
                        lp.h(QG))
            else:
                dyn_dma(nc, "gpsimd", qt[0:64, 0:128],
                        dyn_ap(qfm_d,
                               rc.off("gpsimd", rb, 64 * T,
                                      s * S + w * 128, d * T - 128),
                               [[T, 64], [1, 128]]),
                        lp.h(QG))
            nc.tensor.wait_ge(lp.h(LS), bLS)
            rc.wait("tensor", lp.h(QG), rb, dQG, bQG + 16)
            rc.wait("tensor", lp.h(LAA), rb, dLA, bLA - dLA + 1)
            for cg in range(KEYS // 512):
                ins = nc.tensor.matmul(psc[:, cg * 512:(cg + 1) * 512],
                                       qt[0:64, 0:128],
                                       ksb[0:64, qo + cg * 512:qo + (cg + 1) * 512],
                                       start=True, stop=True)
            ins.then_inc(lp.h(PEA), 1)
            rc.wait("vector", lp.h(PEA), rb, dPE, bPE + 1)
            nc.vector.reduce_max(stt[:, 0:1], psc[:, 0:KEYS], axis=AX.X,
                                 negate=True).then_inc(lp.h(LVA), 1)
            rc.wait("vector", lp.h(LVA), rb, dLV, bLV + 1)
            nc.vector.tensor_scalar(stt[:, 1:2], stt[:, 0:1], scale, None,
                                    ALU.mult).then_inc(lp.h(LVA), 1)
            rc.wait("scalar", lp.h(LVA), rb, dLV, bLV + 2)
            rc.wait("scalar", lp.h(LVA), rb, dLV, bLV - dLV + 4)
            nc.scalar.activation(atn[:, 0:KEYS], psc[:, 0:KEYS], AF.Exp,
                                 bias=stt[:, 1:2], scale=scale,
                                 accum_out=stt[:, 2:3]).then_inc(lp.h(LAA), 1)
            rc.wait("vector", lp.h(LAA), rb, dLA, bLA + 1)
            nc.vector.reciprocal(stt[:, 3:4], stt[:, 2:3]).then_inc(lp.h(LVA), 1)
            rc.wait("tensor", lp.h(LAA), rb, dLA, bLA + 1)
            rc.wait("tensor", lp.h(LVA), rb, dLV, bLV - dLV + 4)
            for kc in range(C):
                ins = nc.tensor.transpose(ptp[:, kc * 128:(kc + 1) * 128],
                                          atn[:, kc * 128:(kc + 1) * 128],
                                          p.identb[:, :])
            ins.then_inc(lp.h(PEA), 1)
            rc.wait("vector", lp.h(PEA), rb, dPE, bPE + 2)
            nc.vector.tensor_copy(att[:, 0:C * 128],
                                  ptp[:, 0:C * 128]).then_inc(lp.h(LVA), 1)
            rc.wait("tensor", lp.h(LVA), rb, dLV, bLV + 4)
            rc.wait("tensor", lp.h(LVA), rb, dLV, bLV - dLV + dLV)
            for kc in range(C):
                ins = nc.tensor.matmul(pcx[0:128, 0:64] if False else pcx[:, 0:64],
                                       att[:, kc * 128:(kc + 1) * 128],
                                       vsb[:, vo + kc * 64:vo + (kc + 1) * 64],
                                       start=(kc == 0), stop=(kc == C - 1))
            ins.then_inc(lp.h(PEA), 1)
            rc.wait("vector", lp.h(PEA), rb, dPE, bPE + dPE)
            rc.wait("vector", lp.h(CS), rb, dCS, bCS - dCS + 16)
            nc.vector.tensor_scalar(cst[:, 0:64], pcx[:, 0:64], stt[:, 3:4],
                                    None, ALU.mult).then_inc(lp.h(LVA), 1)
            rc.wait("gpsimd", lp.h(LVA), rb, dLV, bLV + dLV)
            if kind == "std":
                dyn_dma(nc, "gpsimd",
                        dyn_ap(ctx_tm_d,
                               rc.off("gpsimd", rb, 128 * d,
                                      s * S * d + h * 64, (T - 1) * d),
                               [[d, 128], [1, 64]]),
                        cst[:, 0:64], lp.h(CS))
            else:
                dyn_dma(nc, "gpsimd",
                        dyn_ap(ctx_tm_d,
                               rc.off("gpsimd", rb, 64,
                                      (s * S + w * 128) * d, (T - 1) * d),
                               [[d, 128], [1, 64]]),
                        cst[:, 0:64], lp.h(CS))
        lp.bump(QG, RB * dQG)
        lp.bump(CS, RB * dCS)
        lp.bump(PEA, RB * dPE)
        lp.bump(LVA, RB * dLV)
        lp.bump(LAA, RB * dLA)
    es.close()



def lp_tm2fm(p, lp, rc, tm_d, fm_d, tag="t2f"):
    nc = p.nc
    cfg = p.cfg
    d, ncd, nt, T = cfg.d, cfg.nc_d, cfg.nt, cfg.T
    es = ExitStack()
    t_ = lambda n, sh, dt: es.enter_context(nc.sbuf_tensor(tag + n, sh, dt))
    xg = t_("_x", [128, 2 * d], BF16)
    fg = t_("_f", [128, 2 * d], BF16)
    ptp = es.enter_context(nc.psum_tensor(tag + "_pt", [128, 1024], BF16))
    GD, GS, LV, PE = "gd4", "gd5", "lv2", "lt0"
    D_GD, D_GS, D_LV, D_PE = 2 * 16, 2 * 16, 2 * 2, 2 * 2
    bb = prebump(nc, lp, [(PE, D_PE), (LV, D_LV)])
    bPE, bLV = bb[PE], bb[LV]
    bGD, bGS = lp.base(GD), lp.base(GS)
    with fori2(p, nt // 2) as i:
        for hf in range(2):
            rc.wait("gpsimd", lp.h(PE), i, D_PE, bPE - D_PE + hf * 2 + 2)
            dyn_dma(nc, "gpsimd", xg[:, hf * d:(hf + 1) * d],
                    dyn_ap(tm_d, rc.off("gpsimd", i, 256 * d, hf * 128 * d,
                                        (T - 128) * d),
                           [[d, 128], [1, d]]), lp.h(GD))
            for g2 in range(2):
                rc.wait("tensor", lp.h(GD), i, D_GD, bGD + (hf + 1) * 16)
                if hf == 0 and g2 == 0:
                    rc.wait("tensor", lp.h(LV), i, D_LV, bLV - D_LV + 4)
                else:
                    rc.wait("tensor", lp.h(LV), i, D_LV, bLV + hf * 2 + g2)
                for c in range(4):
                    ins = nc.tensor.transpose(
                        ptp[:, g2 * 512 + c * 128:g2 * 512 + (c + 1) * 128],
                        xg[:, hf * d + (g2 * 4 + c) * 128:hf * d + (g2 * 4 + c + 1) * 128],
                        p.identb[:, :])
                ins.then_inc(lp.h(PE), 1)
                rc.wait("vector", lp.h(PE), i, D_PE, bPE + hf * 2 + g2 + 1)
                rc.wait("vector", lp.h(GS), i, D_GS, bGS - D_GS + (hf + 1) * 16)
                nc.vector.tensor_copy(fg[:, hf * d + g2 * 512:hf * d + (g2 + 1) * 512],
                                      ptp[:, g2 * 512:(g2 + 1) * 512]) \
                    .then_inc(lp.h(LV), 1)
            rc.wait("gpsimd", lp.h(LV), i, D_LV, bLV + (hf + 1) * 2)
            dyn_dma(nc, "gpsimd",
                    dyn_ap(fm_d,
                           rc.off("gpsimd", i, 256, hf * 128, T - 128),
                           [[T, 128], [128 * T, ncd], [1, 128]]),
                    fg[:, hf * d:(hf + 1) * d].rearrange("p (c x) -> p c x", c=ncd),
                    lp.h(GS))
    lp.bump(GD, (nt // 2) * D_GD)
    lp.bump(GS, (nt // 2) * D_GS)
    lp.bump(LV, (nt // 2) * D_LV)
    lp.bump(PE, (nt // 2) * D_PE)
    es.close()



def ph_gsel(p, gfull_d, sel_d, gown_d, tag="gs"):
    cfg = p.cfg
    sc = p.sched
    E, nt, T = cfg.E, cfg.nt, cfg.T
    fence = sc.lasts()
    with ExitStack() as es:
        t_ = lambda n, sh, dt: es.enter_context(p.nc.sbuf_tensor(tag + n, sh, dt))
        sel = t_("_s", [128, 8], F32)
        cand = t_("_c", [128, 2 * 8 * E], F32)
        acc = t_("_a", [128, 2 * E], F32)

        def f_s(e, dma):
            dma(out=sel[:, :], in_=bcast_row(sel_d, 128))
        s_sel = sc.dma("sync", f_s, 1, deps=fence, note=tag + "_sel")
        prev = {}
        for t in range(nt):
            sl = t % 2

            def f_ld(e, dma, t=t, sl=sl):
                for c in range(NCORES):
                    dma(out=cand[:, (sl * 8 + c) * E:(sl * 8 + c + 1) * E],
                        in_=gfull_d[c * T + t * 128:c * T + (t + 1) * 128, 0:E])
            deps = list(fence)
            if t >= 2:
                deps.append(prev[(t - 2, "st")])
            s_ld = sc.dma("sync", f_ld, NCORES, deps=deps, note=f"{tag}_ld{t}")

            def f_z(e, t=t, sl=sl):
                return e.memset(acc[:, sl * E:(sl + 1) * E], 0.0)
            dd = [s_ld] + ([prev[(t - 2, "st")]] if t >= 2 else [])
            s_p = sc.step("vector", f_z, deps=dd, note=f"{tag}_z{t}")
            for c in range(NCORES):
                def f_a(e, t=t, sl=sl, c=c):
                    return e.scalar_tensor_tensor(
                        acc[:, sl * E:(sl + 1) * E],
                        cand[:, (sl * 8 + c) * E:(sl * 8 + c + 1) * E],
                        sel[:, c:c + 1], acc[:, sl * E:(sl + 1) * E],
                        ALU.mult, ALU.add)
                s_p = sc.step("vector", f_a, deps=[s_p, s_sel, s_ld],
                              note=f"{tag}_a{t}_{c}")

            def f_st(e, dma, t=t, sl=sl):
                dma(out=gown_d[t * 128:(t + 1) * 128, 0:E],
                    in_=acc[:, sl * E:(sl + 1) * E])
            prev[(t, "st")] = sc.dma("gpsimd", f_st, 1, deps=[s_p],
                                     note=f"{tag}_st{t}")
    return sc.lasts()


def build_fused(cfg):
    p = PB(cfg, race=False)
    nc = p.nc
    d, F, E, ncd, nF, T, Tt = cfg.d, cfg.F, cfg.E, cfg.nc_d, cfg.nF, cfg.T, \
        cfg.Ttot
    IN = {}
    specs = {
        "L0_ffn_Wq": [d, d], "L0_ffn_Wk": [d, d], "L0_ffn_Wv": [d, d],
        "L0_ffn_Wo": [d, d], "L0_ffn_bo": [d], "L0_ffn_Wr": [d, E],
        "L0_ffn_We1": [E, d, F], "L0_ffn_be1": [E, F],
        "L0_ffn_We2": [E, F, d], "L0_ffn_be2": [E, d],
        "L0_ffn_ln1_g": [d], "L0_ffn_ln1_b": [d],
        "L0_ffn_ln2_g": [d], "L0_ffn_ln2_b": [d],
        "L0_attn_wq": [d, d], "L0_attn_bq": [d], "L0_attn_wk": [d, d],
        "L0_attn_bk": [d], "L0_attn_wv": [d, d], "L0_attn_bv": [d],
        "L0_attn_wo": [d, d], "L0_attn_bo": [d],
        "L0_attn_gamma": [d], "L0_attn_beta": [d],
        "L1_ffn_gate_w": [E, d], "L1_ffn_gate_b": [E],
        "L1_ffn_w1": [E, d, F], "L1_ffn_b1": [E, F],
        "L1_ffn_w2": [E, F, d], "L1_ffn_b2": [E, d],
        "L1_attn_Wq": [d, d], "L1_attn_bq": [d], "L1_attn_Wk": [d, d],
        "L1_attn_bk": [d], "L1_attn_Wv": [d, d], "L1_attn_bv": [d],
        "L1_attn_Wo": [d, d], "L1_attn_bo": [d],
        "L1_attn_ln_g": [d], "L1_attn_ln_b": [d],
    }
    for n, sh in specs.items():
        IN[n] = nc.dram_tensor(n, sh, F32, kind="ExternalInput")
    x_d = nc.dram_tensor("x", [T, d], F32, kind="ExternalInput")
    sel_d = nc.dram_tensor("c_gsel", [NCORES], F32, kind="ExternalInput")
    y_d = nc.dram_tensor("y", [T, d], F32, kind="ExternalOutput")
    S_ = lambda n, sh, dt: nc.dram_tensor(
        n, sh, dt, kind="ExternalOutput" if n in DBG_OUTS else "Internal")
    xin_fm = S_("s_xin", [ncd, 128, T], BF16)
    qfm = S_("s_q", [ncd, 128, T], BF16)
    kfm = S_("s_k", [ncd, 128, T], BF16)
    vtm = S_("s_v", [T, d], BF16)
    ctm = S_("s_ctm", [T, d], BF16)
    cfm = S_("s_cfm", [ncd, 128, T], BF16)
    ao_tm = S_("s_ao", [T, d], BF16)
    o1_tm = S_("s_o1", [T, d], F32)
    o1_fm = S_("s_o1f", [ncd, 128, T], BF16)
    gates = S_("s_gt", [T, E], F32)
    h_fm = S_("s_h", [nF, 128, T], BF16)
    ye_ds = [S_(f"s_ye{e}", [T, d], BF16) for e in range(E)]
    acc_tm = S_("s_acc", [T, d], F32)
    xc = [S_(f"s_x{i}", [T, d], F32) for i in range(6)]
    lg_loc = S_("s_lgl", [T, E], F32)
    lg_full = S_("s_lgf", [Tt, E], F32)
    g_full = S_("s_gf", [Tt, E], F32)
    ctxf = S_("s_ctxf", [ncd, 128, T], F32)
    o_tm2 = S_("s_otm", [T, d], F32)
    on_fm = S_("s_on", [ncd, 128, T], BF16)
    pr_tm = S_("s_pr", [T, d], BF16)

    p.start()
    lp = LP(p)
    rc = RC(p)

    def static_block():
        with nc.Block() as b:
            p.sched.emit_pending(b)
        lp.sched_sync()

    static_block()
    lp.phase_end()
    lp.phase_begin()

    _prbctr = [0]

    def _nfree(eng):
        hs = []
        try:
            while True:
                _prbctr[0] += 1
                hs.append(eng.alloc_register(name=f"_prb{_prbctr[0]}"))
        except Exception:
            pass
        for h in hs:
            eng.free_register(h)
        return len(hs)

    def _dbg(tag):
        if DBGREG:
            print((tag, _nfree(nc.gpsimd), _nfree(nc.vector),
                   _nfree(nc.scalar), _nfree(nc.tensor), _nfree(nc.sync)),
                  flush=True)

    def LPH(fn, *a, **kw):
        fn(p, lp, rc, *a, **kw)
        lp.phase_end()
        lp.phase_begin()
        lp.sched_sync()
        _dbg(("lp", kw.get("tag", getattr(fn, "__name__", "?"))))

    def SPH(fn, *a, **kw):
        fn(p, *a, **kw)
        static_block()
        lp.phase_end()
        lp.phase_begin()
        _dbg(("sp", kw.get("tag", getattr(fn, "__name__", "?"))))

    cci = [0]

    def emit_l0ffn(xi, yo, tagp):
        LPH(lp_ln, xi, fm_d=xin_fm, eps=1e-5, tag=tagp + "ln")
        LPH(lp_proj, xin_fm, IN["L0_ffn_Wq"], d, d, out_fm_d=qfm, tag=tagp + "q")
        LPH(lp_proj, xin_fm, IN["L0_ffn_Wk"], d, d, out_fm_d=kfm, tag=tagp + "k")
        LPH(lp_proj, xin_fm, IN["L0_ffn_Wv"], d, d, out_tm_d=vtm, tag=tagp + "v")
        LPH(lp_attn, qfm, kfm, vtm, ctm, "std", 1.0 / 8.0, tagp + "a")
        LPH(lp_tm2fm, ctm, cfm, tag=tagp + "t")
        LPH(lp_proj, cfm, IN["L0_ffn_Wo"], d, d, b_d=IN["L0_ffn_bo"],
            out_tm_d=ao_tm, tag=tagp + "o")
        SPH(ph_resid, "sum_ln", xi, ao_tm, tm_d=o1_tm, fm_d=o1_fm, eps=1e-5,
            eps2=1e-6, g_d=IN["L0_ffn_ln1_g"], b_d=IN["L0_ffn_ln1_b"],
            adt=BF16, tag=tagp + "r1")
        SPH(ph_logits, o1_tm, IN["L0_ffn_Wr"], E, mode="l0top2", out_d=gates,
            tag=tagp + "lg")
        LPH(lp_moe, xin_fm if False else o1_fm, IN["L0_ffn_We1"],
            IN["L0_ffn_be1"], IN["L0_ffn_We2"], IN["L0_ffn_be2"], gates,
            h_fm, ye_ds, o1_tm, acc_tm, act="gelu", tag=tagp + "m")
        SPH(ph_moe_combine, gates, ye_ds, o1_tm, acc_tm, tag=tagp + "mc")
        SPH(ph_resid, "x_cln", xi, acc_tm, y_d=yo, coef=0.5, eps=1e-6,
            g_d=IN["L0_ffn_ln2_g"], b_d=IN["L0_ffn_ln2_b"], adt=F32,
            tag=tagp + "r2")

    def emit_l0attn(xi, yo, tagp):
        LPH(lp_ln, xi, fm_d=xin_fm, eps=1e-5, tag=tagp + "ln")
        LPH(lp_proj, xin_fm, IN["L0_attn_wq"], d, d, b_d=IN["L0_attn_bq"],
            out_fm_d=qfm, tag=tagp + "q")
        LPH(lp_proj, xin_fm, IN["L0_attn_wk"], d, d, b_d=IN["L0_attn_bk"],
            out_fm_d=kfm, tag=tagp + "k")
        LPH(lp_proj, xin_fm, IN["L0_attn_wv"], d, d, b_d=IN["L0_attn_bv"],
            out_tm_d=vtm, tag=tagp + "v")
        LPH(lp_attn, qfm, kfm, vtm, ctm, "resh", 1.0 / 8.0, tagp + "a")
        LPH(lp_tm2fm, ctm, cfm, tag=tagp + "t")
        LPH(lp_proj, cfm, IN["L0_attn_wo"], d, d, b_d=IN["L0_attn_bo"],
            out_tm_d=ao_tm, tag=tagp + "o")
        SPH(ph_resid, "sum_ln", xi, ao_tm, tm_d=o1_tm, eps=1e-5, eps2=1e-5,
            g_d=IN["L0_attn_gamma"], b_d=IN["L0_attn_beta"], adt=BF16,
            tag=tagp + "r1")
        SPH(ph_resid, "x_ca", xi, o1_tm, y_d=yo, coef=1.0, adt=F32,
            tag=tagp + "r2")

    def emit_l1ffn(xi, yo, tagp):
        LPH(lp_ln, xi, fm_d=xin_fm, eps=1e-5, tag=tagp + "ln")
        SPH(ph_logits, xi, IN["L1_ffn_gate_w"], E, b_d=IN["L1_ffn_gate_b"],
            ln_first=True, mode="store", out_d=lg_loc, w_is_ExD=True,
            eps=1e-5, tag=tagp + "lg")
        ccs = p.es.enter_context(nc.semaphore(f"cc{cci[0]}"))
        cci[0] += 1
        nc.gpsimd.collective_compute(
            "AllGather", ALU.bypass, replica_groups=[list(range(NCORES))],
            ins=[lg_loc.ap().opt()], outs=[lg_full.ap().opt()]).then_inc(ccs, 1)
        nc.gpsimd.wait_ge(ccs, 1)
        nc.gpsimd.sem_inc(lp.h("gp0"), 1)
        lp.bump("gp0", 1)
        lp.phase_end("gpsimd")
        lp.phase_begin()
        lp.sched_sync()
        SPH(ph_route_l1, lg_full, g_full, Tt, E, cap=cfg.cap, tag=tagp + "rt")
        SPH(ph_gsel, g_full, sel_d, gates, tag=tagp + "gs")
        LPH(lp_moe, xin_fm, IN["L1_ffn_w1"], IN["L1_ffn_b1"], IN["L1_ffn_w2"],
            IN["L1_ffn_b2"], gates, h_fm, ye_ds, None, acc_tm, act="relu",
            tag=tagp + "m")
        SPH(ph_moe_combine, gates, ye_ds, None, acc_tm, tag=tagp + "mc")
        SPH(ph_resid, "x_ca", xi, acc_tm, y_d=yo, coef=0.5, adt=F32,
            tag=tagp + "r2")

    def emit_l1attn(xi, yo, tagp):
        LPH(lp_ln, xi, fm_d=xin_fm, eps=1e-5, tag=tagp + "ln")
        SPH(ph_proj, xin_fm, IN["L1_attn_Wq"], d, d, b_d=IN["L1_attn_bq"],
            out_fm_d=qfm, act="elu1", w_transposed=True, tag=tagp + "q")
        SPH(ph_proj, xin_fm, IN["L1_attn_Wk"], d, d, b_d=IN["L1_attn_bk"],
            out_fm_d=kfm, act="elu1", w_transposed=True, tag=tagp + "k")
        SPH(ph_proj, xin_fm, IN["L1_attn_Wv"], d, d, b_d=IN["L1_attn_bv"],
            out_tm_d=vtm, w_transposed=True, tag=tagp + "v")
        SPH(ph_lattn, qfm, kfm, vtm, ctxf, tag=tagp + "la")
        SPH(ph_fm2tm, ctxf, o_tm2, dt=F32, tag=tagp + "f2")
        LPH(lp_ln, o_tm2, fm_d=on_fm, eps=1e-5, g_d=IN["L1_attn_ln_g"],
            b_d=IN["L1_attn_ln_b"], tag=tagp + "l2")
        SPH(ph_proj, on_fm, IN["L1_attn_Wo"], d, d, b_d=IN["L1_attn_bo"],
            out_tm_d=pr_tm, w_transposed=True, tag=tagp + "o")
        SPH(ph_resid, "x_ca", xi, pr_tm, y_d=yo, coef=1.0, adt=BF16,
            tag=tagp + "r2")

    emit_l0ffn(x_d, xc[0], "A")
    emit_l0attn(xc[0], xc[1], "B")
    emit_l0ffn(xc[1], xc[2], "C")
    emit_l1ffn(xc[2], xc[3], "D")
    emit_l1attn(xc[3], xc[4], "E")
    emit_l1ffn(xc[4], y_d, "G")
    finalize_lp(p, lp)
    return p


def kernel(**inputs):
    cfg = Cfg()
    T = cfg.T
    x0 = np.ascontiguousarray(np.asarray(inputs["L0_attn_x"], dtype=np.float32))
    B, S, d = x0.shape
    xs = [np.ascontiguousarray(x0[2 * c:2 * c + 2].reshape(T, d))
          for c in range(NCORES)]

    def f32(n):
        return np.ascontiguousarray(np.asarray(inputs[n], dtype=np.float32))

    p = _get_prog("fused", cfg)
    base = {n: f32(n) for n in inputs if n != "L0_attn_x"}
    maps = []
    for c in range(NCORES):
        m = dict(base)
        m["x"] = xs[c]
        sel = np.zeros(NCORES, dtype=np.float32)
        sel[c] = 1.0
        m["c_gsel"] = sel
        for cn, cv in p.extra_inputs.items():
            m[cn] = cv
        maps.append(m)
    res = run_bass_kernel_spmd(p.nc, maps, core_ids=list(range(NCORES)))
    out = np.concatenate([res.results[c]["y"].reshape(2, S, d)
                          for c in range(NCORES)], axis=0)
    return out.astype(np.float32)
